# version 13
# baseline (speedup 1.0000x reference)
import numpy as np
import ml_dtypes
import contextlib
import concourse.bass as bass
import concourse.mybir as mybir
from concourse.bass_utils import run_bass_kernel_spmd

F32 = mybir.dt.float32
BF16 = mybir.dt.bfloat16
AF = mybir.ActivationFunctionType
ALU = mybir.AluOpType

T = 2048
C = 1024
NT = 16
NCT = 8
HD = 64
LORA_SCALE = 2.0
EXP_SHIFT0 = 6.0
EXP_SHIFT1 = 6.0
GCOL = 8
_bf = ml_dtypes.bfloat16
DBG_MIN = False



def _host_constants():
    cst = {}
    cst["ident_f32"] = np.eye(128, dtype=np.float32)
    cst["ones_1x128"] = np.ones((1, 128), dtype=_bf)
    cst["ones_8x1"] = np.ones((8, 16), dtype=_bf)
    cst["ones_c128"] = np.ones((128, 8), dtype=_bf)
    cst["zeros_guard"] = np.zeros((128, NCT, GCOL), dtype=_bf)
    cst["vones"] = np.ones((128, 16, 16), dtype=_bf)
    ps = np.zeros((16, 8 * 128), dtype=_bf)
    for j in range(8):
        ps[2 * j, j * 128:j * 128 + 64] = 1.0
        ps[2 * j + 1, j * 128 + 64:j * 128 + 128] = 1.0
    cst["pairsel"] = ps
    es = np.zeros((8, 1024), dtype=_bf)
    for e in range(8):
        es[e, e * 128:(e + 1) * 128] = 1.0
    cst["esel"] = es
    k = np.arange(128)[:, None]
    c0 = np.arange(1024)[None, :]
    cst["m0"] = (c0 >= k + 384).astype(_bf)[:, :896]
    i = np.arange(C // 2, dtype=np.float64)
    phi = np.power(10000.0, -2.0 * (i - 1.0) / C)
    ang = np.arange(T, dtype=np.float64)[None, :] * phi[:, None]
    cst["cosT"] = np.ascontiguousarray(
        np.cos(ang).astype(_bf).reshape(4, 128, T).transpose(1, 0, 2))
    cst["sinT"] = np.ascontiguousarray(
        np.sin(ang).astype(_bf).reshape(4, 128, T).transpose(1, 0, 2))
    sc = np.zeros((128, 8), dtype=np.float32)
    sc[:, 0] = -EXP_SHIFT0
    sc[:, 1] = -EXP_SHIFT1
    cst["fconsts"] = sc
    return cst


def _prep_inputs(I):
    H = {}
    A = np.ascontiguousarray
    dd, hh = np.meshgrid(np.arange(64), np.arange(16), indexing="ij")
    vperm = (64 * hh + dd).reshape(-1)
    W = I["L0_attn_W_qkv"]
    H["Wqkv"] = A(np.concatenate([W[:, :2048], W[:, 2048:][:, vperm]], axis=1))
    B = I["L0_attn_B_qkv"]
    H["B_qkv"] = A(np.concatenate([B[:, :2048], B[:, 2048:][:, vperm]], axis=1))
    H["b_qkv"] = A(I["L0_attn_b_qkv"][:2048].reshape(16, 128).T)
    H["b_v"] = A(I["L0_attn_b_qkv"][2048:].reshape(8, 128).T)
    H["A_qkv"] = A(I["L0_attn_A_qkv"].reshape(8, 128, 8).transpose(1, 0, 2))
    H["W_proj"] = A(I["L0_attn_W_proj"])
    H["b_proj"] = A(I["L0_attn_b_proj"].reshape(8, 128).T)
    H["A_proj"] = A(I["L0_attn_A_proj"].reshape(8, 128, 8).transpose(1, 0, 2))
    H["B_proj"] = A(I["L0_attn_B_proj"])
    H["WrT"] = A(I["L0_ffn_W_r"].T)
    H["WkT"] = A(I["L0_ffn_W_k"].T)
    H["WvT"] = A(I["L0_ffn_W_v"].T)
    H["mu_r"] = A(I["L0_ffn_mu_r"].reshape(8, 128).T)
    H["mu_k"] = A(I["L0_ffn_mu_k"].reshape(8, 128).T)
    perm = np.concatenate([np.arange(0, C, 2), np.arange(1, C, 2)])
    H["wqT"] = A(I["L1_attn_wq"][perm].T)
    H["wkT"] = A(I["L1_attn_wk"][perm].T)
    H["wvT"] = A(I["L1_attn_wv"].T)
    H["gate_wT"] = A(I["L1_ffn_gate_w"].T.reshape(8, 128, 8).transpose(1, 0, 2))
    H["up_w"] = A(I["L1_ffn_up_w"])
    H["up_b"] = A(I["L1_ffn_up_b"].reshape(8, 8, 128).transpose(2, 1, 0))
    H["down_w"] = A(I["L1_ffn_down_w"])
    H["down_b"] = A(I["L1_ffn_down_b"])
    H["sh_up_w"] = A(I["L1_ffn_sh_up_w"])
    H["sh_up_b"] = A(I["L1_ffn_sh_up_b"].reshape(8, 128).T)
    H["sh_down_w"] = A(I["L1_ffn_sh_down_w"])
    H["sh_down_b"] = A(I["L1_ffn_sh_down_b"].reshape(8, 128).T)
    for k2 in H:
        if H[k2].dtype != np.float32:
            H[k2] = H[k2].astype(np.float32)
    return H



class Prog:
    def __init__(self, debug_stage=None):
        self.debug_stage = debug_stage
        self.dbg = {}
        self.stopped = False

    def sem_init(self, nc):
        self.nc = nc
        self.cnt = {}
        self.sems = {}
        for s in ["sIN", "sW", "sMV", "sOUT", "cT", "cV", "cS"]:
            cm = nc.semaphore(s)
            self.sems[s] = self._stack.enter_context(cm)
            self.cnt[s] = 0

    def inc(self, instr, name, by=1):
        instr.then_inc(self.sems[name], by)
        self.cnt[name] += by
        return self.cnt[name]

    def wait(self, eng, name, target=None):
        eng.wait_ge(self.sems[name], self.cnt[name] if target is None else target)


def build_program(debug_stage=None):
    nc = bass.Bass("TRN2", target_bir_lowering=False,
                   detect_race_conditions=False)
    P = Prog(debug_stage)
    P._stack = contextlib.ExitStack()
    P.sem_init(nc)
    stack = P._stack

    din = {}
    def dram_in(name, shape, dtype=F32):
        din[name] = nc.dram_tensor(name, list(shape), dtype, kind="ExternalInput").ap()
        return din[name]

    x_h = dram_in("x", (T, C))
    Wqkv_h = dram_in("Wqkv", (C, 3 * C))
    bqkv_h = dram_in("b_qkv", (128, 16))
    bv_h = dram_in("b_v", (128, 8))
    Aqkv_h = dram_in("A_qkv", (128, 8, 8))
    Bqkv_h = dram_in("B_qkv", (8, 3 * C))
    Wproj_h = dram_in("W_proj", (C, C))
    bproj_h = dram_in("b_proj", (128, 8))
    Aproj_h = dram_in("A_proj", (128, 8, 8))
    Bproj_h = dram_in("B_proj", (8, C))
    WrT_h = dram_in("WrT", (C, C))
    WkT_h = dram_in("WkT", (C, C))
    WvT_h = dram_in("WvT", (C, C))
    mur_h = dram_in("mu_r", (128, 8))
    muk_h = dram_in("mu_k", (128, 8))
    wqT_h = dram_in("wqT", (C, C))
    wkT_h = dram_in("wkT", (C, C))
    wvT_h = dram_in("wvT", (C, C))
    gateT_h = dram_in("gate_wT", (128, 8, 8))
    upw_h = dram_in("up_w", (8, C, C))
    upb_h = dram_in("up_b", (128, 8, 8))
    dww_h = dram_in("down_w", (8, C, C))
    dwb_h = dram_in("down_b", (8, C))
    shup_h = dram_in("sh_up_w", (C, C))
    shupb_h = dram_in("sh_up_b", (128, 8))
    shdw_h = dram_in("sh_down_w", (C, C))
    shdwb_h = dram_in("sh_down_b", (128, 8))
    ident_h = dram_in("ident_f32", (128, 128))
    ones1_h = dram_in("ones_1x128", (1, 128), BF16)
    ones8_h = dram_in("ones_8x1", (8, 16), BF16)
    onesc_h = dram_in("ones_c128", (128, 8), BF16)
    zguard_h = dram_in("zeros_guard", (128, NCT, GCOL), BF16)
    vones_h = dram_in("vones", (128, 16, 16), BF16)
    pairsel_h = dram_in("pairsel", (16, 1024), BF16)
    esel_h = dram_in("esel", (8, 1024), BF16)
    m0_h = dram_in("m0", (128, 896), BF16)
    cosT_h = dram_in("cosT", (128, 4, T), BF16)
    sinT_h = dram_in("sinT", (128, 4, T), BF16)
    fconsts_h = dram_in("fconsts", (128, 8))
    out_h = nc.dram_tensor("out", [T, C], F32, kind="ExternalOutput").ap()
    dbg_h = {}
    if debug_stage is not None:
        for nm, shp, dt in [("dbg_xt", [128, NCT, T], F32),
                            ("dbg_xn", [128, NCT, GCOL + T], BF16),
                            ("dbg_bq", [128, NCT, T], BF16),
                            ("dbg_bk", [128, NCT, T], BF16),
                            ("dbg_bv", [128, 16, 1040], BF16),
                            ("dbg_sall", [16, T], BF16),
                            ("dbg_comb", [8, T], BF16),
                            ("dbg_normc", [8, T], BF16),
                            ("dbg_la", [128, 16, 8], F32),
                            ("dbg_ea", [128, 16, 8], F32)]:
            dbg_h[nm] = nc.dram_tensor(nm, shp, dt, kind="ExternalOutput").ap()

    S = lambda *a: stack.enter_context(nc.sbuf_tensor(*a))

    XT = S("XT", [128, NCT, T], F32)
    BQ = S("BQ", [128, NCT, T], BF16)
    BK = S("BK", [128, NCT, T], BF16)
    BV = S("BV", [128, 16, 1040], BF16)
    ST = [S("ST0", [128, NCT, 256], BF16), S("ST1", [128, NCT, 256], BF16)]
    BqS = [S("BqS0", [8, 256], BF16), S("BqS1", [8, 256], BF16)]
    XNv = BV.rearrange("p a b -> p (a b)")[:, 0:NCT * (GCOL + T)] \
            .rearrange("p (a b) -> p a b", b=GCOL + T)
    xn_state = {"XN": XNv, "cm": None}

    def xn_guard_zero():
        P.inc(sy.dma_start(out=XNv[:, :, 0:GCOL], in_=zguard_h), "sIN", 16)
    BVsp = BV.rearrange("p a b -> p (a b)")
    onesA = BVsp[:, 16448:16576]
    ident = S("identb", [128, 128], F32)
    ones1 = S("ones1b", [1, 128], BF16)
    ones8 = S("ones8b", [8, 16], BF16)
    onesc = S("onescb", [128, 8], BF16)
    fconsts = S("fconstsb", [128, 8], F32)
    bqkv = S("bqkvb", [128, 16], F32)
    bv = S("bvb", [128, 8], F32)
    bproj = S("bprojb", [128, 8], F32)
    mur = S("murb", [128, 8], F32)
    muk = S("mukb", [128, 8], F32)
    upb = S("upbb", [128, 8, 8], F32)
    shupb = S("shupbb", [128, 8], F32)
    shdwb = S("shdwbb", [128, 8], F32)
    Aqkv = S("Aqkvb", [128, 8, 8], BF16)
    Aproj = S("Aprojb", [128, 8, 8], BF16)
    gateT = S("gateTb", [128, 8, 8], BF16)

    RS = {}

    def rs_alloc(name, shape, dtype):
        cm = nc.sbuf_tensor(name, shape, dtype, side="right")
        RS[name] = (cm, cm.__enter__())
        return RS[name][1]

    def rs_free(name):
        cm, _ = RS.pop(name)
        cm.__exit__(None, None, None)

    BQf32 = BQ.bitcast(F32)
    BQrow = BQf32.rearrange("p a b -> p (a b)")
    rowsA = BQrow[0:1, 0:T]
    rowsB = BQrow[0:1, T:2 * T]
    rowsE = BQrow[0:1, 2 * T:3 * T]
    BKflat = BK.rearrange("p a b -> p (a b)")
    rowsC = BKflat[0:1, 0:T]
    rowsD = BKflat[0:1, T:2 * T]
    BKf32 = BK.bitcast(F32).rearrange("p a b -> p (a b)")
    invden = BKf32[0:1, 0:T]
    a_invdenC = BKflat[0:1, 2 * T:3 * T]
    BQflat32 = BQf32.rearrange("p a b -> p (a b)")
    outStage = [BQflat32[:, 0:1024], BQflat32[:, 2048:3072]]
    bcMoE = BKflat[:, 0:T]
    esel = BKflat[0:8, 3 * T:3 * T + 1024]
    GB = BQflat32[:, 1024:1216].rearrange("p (a b) -> p a b", b=12)
    MB = GB
    LA = BQflat32[:, 1216:1344].rearrange("p (a b) -> p a b", b=8)
    EA = BQflat32[:, 1344:1472].rearrange("p (a b) -> p a b", b=8)
    MA = BQflat32[:, 1472:1600].rearrange("p (a b) -> p a b", b=8)
    GL = BQflat32[:, 1600:1792].rearrange("p (a b) -> p a b", b=12)
    ML = BQflat32[:, 1792:1984].rearrange("p (a b) -> p a b", b=12)
    psA = stack.enter_context(nc.psum_tensor("psA", [128, T], F32))
    psB = stack.enter_context(nc.psum_tensor("psB", [128, T], F32))

    sStage = rs_alloc("sStage", [128, T], BF16)

    nc.gpsimd.dma_reset(range(155, 256))
    nc.gpsimd.sem_clear(range(155, 256))
    nc.all_engine_barrier()

    sy, gp, te, ve, sc = nc.sync, nc.gpsimd, nc.tensor, nc.vector, nc.scalar

    P.inc(sy.dma_start(out=onesA,
                       in_=vones_h.rearrange("p a b -> p (a b)")[:, 0:128]), "sIN", 16)
    for dst, src in [(ident, ident_h), (ones1, ones1_h), (ones8, ones8_h),
                     (onesc, onesc_h), (fconsts, fconsts_h), (bqkv, bqkv_h),
                     (bv, bv_h), (bproj, bproj_h), (mur, mur_h), (muk, muk_h),
                     (upb, upb_h), (shupb, shupb_h), (shdwb, shdwb_h)]:
        P.inc(sy.dma_start(out=dst[:], in_=src), "sIN", 16)
    for dst, src in [(Aqkv, Aqkv_h), (Aproj, Aproj_h), (gateT, gateT_h)]:
        P.inc(gp.dma_start(out=dst[:], in_=src), "sW", 16)

    wstate = {"i": 0, "free": [0, 0]}

    def load_w(src_ap, b_src=None, width=256):
        i = wstate["i"] % 2
        wstate["i"] += 1
        gp.wait_ge(P.sems["cT"], wstate["free"][i])
        P.inc(gp.dma_start(out=ST[i][:, :, 0:width], in_=src_ap), "sW", 16)
        if b_src is not None:
            P.inc(gp.dma_start(out=BqS[i][:, 0:width], in_=b_src), "sW", 16)
        te.wait_ge(P.sems["sW"], P.cnt["sW"])
        return (ST[i], BqS[i])

    def slot_done(instr):
        i = (wstate["i"] - 1) % 2
        wstate["free"][i] = P.cnt["cT"]

    def wcols(h, f0, w):
        return h.rearrange("(j p) f -> p j f", p=128)[:, :, f0:f0 + w]

    def xn(cc, t0=0, t1=T):
        return xn_state["XN"][:, cc, GCOL + t0:GCOL + t1]

    def finish():
        P.wait(te, "cV")
        P.wait(te, "cS")
        P.wait(te, "sMV")
        if debug_stage is None:
            dma_done = {}
            pendf = {}
            fj = 0
            for tt in range(NT):
                ob = outStage[tt % 2]
                if tt >= 2:
                    ve.wait_ge(P.sems["sOUT"], dma_done[tt - 2])
                for cc in range(NCT):
                    pslot = psB[:, (fj % 4) * 512:(fj % 4) * 512 + 128]
                    if fj >= 4:
                        te.wait_ge(P.sems["cV"], pendf[fj - 4])
                    P.inc(te.transpose(pslot, XT[:, cc, tt * 128:(tt + 1) * 128],
                                       ident[:]), "cT")
                    P.wait(ve, "cT")
                    pendf[fj] = P.inc(ve.tensor_copy(out=ob[:, cc * 128:(cc + 1) * 128],
                                                     in_=pslot), "cV")
                    fj += 1
                P.wait(sy, "cV")
                dma_done[tt] = P.inc(sy.dma_start(out=out_h[tt * 128:(tt + 1) * 128, :],
                                                  in_=ob), "sOUT", 16)
            P.wait(sy, "sOUT")
        else:
            P.wait(sy, "cV")
            P.wait(sy, "cS")
            P.wait(sy, "cT")
            pairs = [("dbg_xt", XT), ("dbg_bq", BQ),
                     ("dbg_bk", BK),
                     ("dbg_la", LA), ("dbg_ea", EA)]
            if DBG_MIN:
                pairs = [("dbg_xt", XT)]
            if "combT" in RS and not DBG_MIN:
                pairs.append(("dbg_comb", RS["combT"][1]))
            if "normC" in RS and not DBG_MIN:
                pairs.append(("dbg_normc", RS["normC"][1]))
            if "VR2" in RS and not DBG_MIN:
                pairs.append(("dbg_bv", RS["VR2"][1]))
            elif "VR" in RS:
                pairs.append(("dbg_bv", RS["VR"][1]))
            if not DBG_MIN:
                pairs.append(("dbg_xn", xn_state["XN"]))
            for nm, sb in pairs:
                P.inc(sy.dma_start(out=dbg_h[nm], in_=sb[:]), "sOUT", 16)
            P.wait(sy, "sOUT")
        stack.close()
        return nc

    xs0 = BQ.bitcast(F32)
    xs1 = BK.bitcast(F32)
    P.inc(sy.dma_start(out=xs0[:], in_=x_h[0:1024].rearrange("(a p) c -> p a c", p=128)), "sIN", 16)
    P.inc(sy.dma_start(out=xs1[:], in_=x_h[1024:2048].rearrange("(a p) c -> p a c", p=128)), "sIN", 16)
    P.wait(te, "sIN")
    pendx = {}
    ti = 0
    for tt in range(NT):
        src = xs0 if tt < 8 else xs1
        a = tt % 8
        for cc in range(NCT):
            pslot = psA[:, (ti % 4) * 512:(ti % 4) * 512 + 128]
            if ti >= 4:
                te.wait_ge(P.sems["cV"], pendx[ti - 4])
            P.inc(te.transpose(pslot, src[:, a, cc * 128:(cc + 1) * 128], ident[:]), "cT")
            P.wait(ve, "cT")
            pendx[ti] = P.inc(ve.tensor_copy(out=XT[:, cc, tt * 128:(tt + 1) * 128],
                                             in_=pslot), "cV")
            ti += 1
    te.wait_ge(P.sems["cV"], pendx[ti - 1])

    def layernorm():
        P.wait(sc, "cV")
        for cc in range(NCT):
            P.inc(sc.activation(out=BK[:, cc, :], in_=XT[:, cc, :], func=AF.Copy), "cS")
            P.inc(sc.activation(out=BQ[:, cc, :], in_=XT[:, cc, :], func=AF.Square), "cS")
        P.wait(te, "cS")
        for ch in range(4):
            for cc in range(NCT):
                P.inc(te.matmul(psA[0:1, ch * 512:(ch + 1) * 512], onesc[:, 0:1],
                                BK[:, cc, ch * 512:(ch + 1) * 512],
                                start=(cc == 0), stop=(cc == NCT - 1)), "cT")
            for cc in range(NCT):
                P.inc(te.matmul(psB[0:1, ch * 512:(ch + 1) * 512], onesc[:, 0:1],
                                BQ[:, cc, ch * 512:(ch + 1) * 512],
                                start=(cc == 0), stop=(cc == NCT - 1)), "cT")
        P.wait(ve, "cT")
        P.inc(ve.tensor_scalar(out=rowsA[:], in0=psA[0:1, 0:T], scalar1=1.0 / C,
                               scalar2=None, op0=ALU.mult), "cV")
        P.inc(ve.tensor_scalar(out=rowsB[:], in0=psB[0:1, 0:T], scalar1=1.0 / C,
                               scalar2=None, op0=ALU.mult), "cV")
        P.inc(ve.tensor_tensor(out=rowsE[:], in0=rowsA[:], in1=rowsA[:], op=ALU.mult), "cV")
        P.inc(ve.tensor_tensor(out=rowsB[:], in0=rowsB[:], in1=rowsE[:], op=ALU.subtract), "cV")
        P.inc(ve.tensor_scalar(out=rowsB[:], in0=rowsB[:], scalar1=1e-5, scalar2=None,
                               op0=ALU.add), "cV")
        P.inc(ve.reciprocal(out=rowsB[:], in_=rowsB[:]), "cV")
        P.wait(sc, "cV")
        P.inc(sc.activation(out=rowsD[:], in_=rowsB[:], func=AF.Sqrt), "cS")
        P.inc(sc.activation(out=rowsC[:], in_=rowsA[:], func=AF.Copy, scale=-1.0), "cS")
        P.wait(te, "cS")
        for ch in range(4):
            P.inc(te.matmul(psA[:, ch * 512:(ch + 1) * 512], ones1[:],
                            rowsC[:, ch * 512:(ch + 1) * 512], start=True, stop=True), "cT")
            P.inc(te.matmul(psB[:, ch * 512:(ch + 1) * 512], ones1[:],
                            rowsD[:, ch * 512:(ch + 1) * 512], start=True, stop=True), "cT")
        P.wait(ve, "cT")
        for cc in range(NCT):
            P.inc(ve.tensor_tensor(out=BQ[:, cc, :], in0=XT[:, cc, :], in1=psA[:, 0:T],
                                   op=ALU.add), "cV")
        for cc in range(NCT):
            P.inc(ve.tensor_tensor(out=xn(cc), in0=BQ[:, cc, :], in1=psB[:, 0:T],
                                   op=ALU.mult), "cV")
        P.wait(te, "cV")
        P.wait(sc, "cV")

    xn_guard_zero()
    layernorm()
    if debug_stage == "ln0":
        return finish()

    lora1T = sStage[0:8, 0:T]

    def compute_lora1T(A_ap):
        P.wait(te, "sW")
        P.wait(te, "cV")
        P.wait(te, "cS")
        for ch in range(4):
            for cc in range(NCT):
                P.inc(te.matmul(psB[0:8, ch * 512:(ch + 1) * 512], A_ap[:, cc, :],
                                xn(cc, ch * 512, (ch + 1) * 512),
                                start=(cc == 0), stop=(cc == NCT - 1)), "cT")
        P.wait(ve, "cT")
        P.inc(ve.tensor_scalar(out=lora1T, in0=psB[0:8, 0:T], scalar1=LORA_SCALE,
                               scalar2=None, op0=ALU.mult), "cV")
        P.wait(te, "cV")

    ffn_state = {}

    def build_shift(mu_ap):
        P.wait(ve, "cS")
        for cc in range(NCT):
            P.inc(ve.tensor_tensor(out=sStage[:, 0:T], in0=xn(cc),
                                   in1=xn_state["XN"][:, cc, GCOL - 1:GCOL - 1 + T],
                                   op=ALU.subtract), "cV")
            P.inc(ve.tensor_scalar(out=sStage[:, 0:T], in0=sStage[:, 0:T],
                                   scalar1=mu_ap[:, cc:cc + 1], scalar2=None,
                                   op0=ALU.mult), "cV")
            P.inc(ve.tensor_tensor(out=BQ[:, cc, :], in0=sStage[:, 0:T],
                                   in1=xn_state["XN"][:, cc, GCOL - 1:GCOL - 1 + T],
                                   op=ALU.add), "cV")

    def mm_1024(dst_cb, w_h, rhs_cb):
        pend = {}
        for fo in range(0, NCT, 2):
            slot, _b = load_w(wcols(w_h, fo * 128, 256))
            last = None
            for sub in range(2):
                ft = fo + sub
                psum = (psA, psB)[ft % 2]
                if ft >= 2:
                    sem, cnt = pend[ft - 2]
                    te.wait_ge(P.sems[sem], cnt)
                for ch in range(4):
                    ps = psum[:, ch * 512:(ch + 1) * 512]
                    for cc in range(NCT):
                        last = te.matmul(ps, slot[:, cc, sub * 128:(sub + 1) * 128],
                                         rhs_cb(cc, ch * 512, (ch + 1) * 512),
                                         start=(cc == 0), stop=(cc == NCT - 1))
                        P.inc(last, "cT")
                pend[ft] = dst_cb(ft, psum)
            slot_done(last)
        for ft in (NCT - 2, NCT - 1):
            sem, cnt = pend[ft]
            te.wait_ge(P.sems[sem], cnt)

    def ffn0():
        kact = rs_alloc("kactR", [128, NCT, T], BF16)
        ffn_state["kact"] = kact
        build_shift(mur)
        P.wait(te, "cV")

        def evict_r(ft, psum):
            P.wait(sc, "cT")
            c = P.inc(sc.activation(out=BK[:, ft, :], in_=psum[:, 0:T], func=AF.Sigmoid), "cS")
            return ("cS", c)
        mm_1024(evict_r, WrT_h, lambda cc, a, b: BQ[:, cc, a:b])

        build_shift(muk)
        P.wait(te, "cV")

        def evict_k(ft, psum):
            P.wait(sc, "cT")
            kact = ffn_state["kact"]
            P.inc(sc.activation(out=kact[:, ft, 0:T], in_=psum[:, 0:T], func=AF.Relu), "cS")
            P.wait(ve, "cS")
            c = P.inc(ve.tensor_tensor(out=kact[:, ft, 0:T], in0=kact[:, ft, 0:T],
                                       in1=kact[:, ft, 0:T], op=ALU.mult), "cV")
            return ("cV", c)
        mm_1024(evict_k, WkT_h, lambda cc, a, b: BQ[:, cc, a:b])

        def evict_kv(ft, psum):
            P.wait(ve, "cT")
            tmp = sStage[:, 0:T]
            P.inc(ve.tensor_tensor(out=tmp, in0=psum[:, 0:T], in1=BK[:, ft, :],
                                   op=ALU.mult), "cV")
            c = P.inc(ve.tensor_tensor(out=XT[:, ft, :], in0=XT[:, ft, :], in1=tmp,
                                       op=ALU.add), "cV")
            return ("cV", c)
        mm_1024(evict_kv, WvT_h, lambda cc, a, b: ffn_state["kact"][:, cc, a:b])
        P.wait(gp, "cT")
        P.wait(gp, "cV")
        rs_free("kactR")

    ffn0()
    if debug_stage == "ffn0":
        return finish()

    vst = {}

    def qkv_half(half):
        pend = {}
        fi = 0
        for sec, tb in [(0, 0), (1, 4)]:
            fbase = sec * C + half * 512
            for fo in range(0, 4, 2):
                slot, bsl = load_w(wcols(Wqkv_h, fbase + fo * 128, 256),
                                   Bqkv_h[:, fbase + fo * 128:fbase + (fo + 2) * 128])
                last = None
                for sub in range(2):
                    ft = fo + sub
                    psum = (psA, psB)[fi % 2]
                    if fi >= 2:
                        te.wait_ge(P.sems["cS"], pend[fi - 2])
                    for ch in range(4):
                        ps = psum[:, ch * 512:(ch + 1) * 512]
                        for cc in range(NCT):
                            last = te.matmul(ps, slot[:, cc, sub * 128:(sub + 1) * 128],
                                             xn(cc, ch * 512, (ch + 1) * 512),
                                             start=(cc == 0), stop=False)
                            P.inc(last, "cT")
                        last = te.matmul(ps, bsl[:, sub * 128:(sub + 1) * 128],
                                         lora1T[:, ch * 512:(ch + 1) * 512],
                                         start=False, stop=True)
                        P.inc(last, "cT")
                    P.wait(sc, "cT")
                    bcol = half * 4 + ft + sec * 8
                    pend[fi] = P.inc(sc.activation(out=BQ[:, tb + ft, :],
                                                   in_=psum[:, 0:T], func=AF.Identity,
                                                   bias=bqkv[:, bcol:bcol + 1]), "cS")
                    fi += 1
                slot_done(last)
        sc.wait_ge(P.sems["cS"], pend[fi - 1])
        te.wait_ge(P.sems["cS"], pend[fi - 1])

    def build_v():
        V1 = rs_alloc("VR", [128, 16, 1040], BF16)
        vst["V1"] = V1
        P.wait(ve, "cT")
        P.inc(ve.memset(V1[:, :, 1024:1040], 1.0), "cV")
        P.wait(te, "cV")
        for ci in range(4):
            slot, bsl = load_w(wcols(Wqkv_h, 2048 + ci * 256, 256),
                               Bqkv_h[:, 2048 + ci * 256:2048 + (ci + 1) * 256])
            last = None
            pend = {}
            for tt in range(NT):
                ps = psA[:, (tt % 4) * 512:(tt % 4) * 512 + 256]
                if tt >= 4:
                    te.wait_ge(P.sems["cV"], pend[tt - 4])
                for cc in range(NCT):
                    last = te.matmul(ps, xn(cc, tt * 128, (tt + 1) * 128),
                                     slot[:, cc, 0:256],
                                     start=(cc == 0), stop=False)
                    P.inc(last, "cT")
                last = te.matmul(ps, lora1T[:, tt * 128:(tt + 1) * 128],
                                 bsl[:, 0:256], start=False, stop=True)
                P.inc(last, "cT")
                P.wait(ve, "cT")
                pend[tt] = P.inc(ve.tensor_copy(out=V1[:, tt, ci * 256:(ci + 1) * 256],
                                                in_=ps), "cV")
            te.wait_ge(P.sems["cV"], pend[NT - 1])
            slot_done(last)
        P.wait(te, "sMV")

    def attn_half(half):
        V1 = vst["V1"]
        ST1f32 = ST[1].rearrange("p a b -> p (a b)").bitcast(F32)
        ebuf = [[sStage[:, 0:512], sStage[:, 512:1024]],
                [sStage[:, 1024:1536], sStage[:, 1536:2048]]]
        gate = {"pvprev": (P.cnt["cT"], P.cnt["cT"]),
                "expprev": (P.cnt["cS"], P.cnt["cS"]),
                "tt": [P.cnt["cV"], P.cnt["cV"]]}
        pp = 0
        for p in range(4):
            tile = p
            gpair = half * 4 + p
            he = half * 8 + 2 * p
            for qq in range(4):
                q0 = qq * 512
                SB = (pp % 2) * 1024
                exp_done = {}
                pv_done = {}
                sv = {}

                def emit_S(kt):
                    par = kt % 2
                    if kt >= 2:
                        te.wait_ge(P.sems["cS"], exp_done[(kt - 2, 1)])
                    else:
                        te.wait_ge(P.sems["cS"], gate["expprev"][kt])
                    for eo in range(2):
                        P.inc(te.matmul(
                            psA[:, eo * 1024 + par * 512:eo * 1024 + (par + 1) * 512],
                            BQ[64 * eo:64 * eo + 64, 4 + tile, kt * 128:(kt + 1) * 128],
                            BQ[64 * eo:64 * eo + 64, tile, q0:q0 + 512],
                            start=True, stop=True,
                            tile_position=(64 * eo, 0)), "cT")
                    sv[kt] = P.cnt["cT"]

                emit_S(0)
                emit_S(1)
                for kt in range(NT):
                    par = kt % 2
                    for eo in range(2):
                        if eo == 0:
                            sc.wait_ge(P.sems["cT"], sv[kt])
                        if kt >= 2:
                            sc.wait_ge(P.sems["cT"], pv_done[(kt - 2, eo)])
                        elif eo == 0:
                            sc.wait_ge(P.sems["cT"], gate["pvprev"][kt])
                        exp_done[(kt, eo)] = P.inc(
                            sc.activation(out=ebuf[par][eo],
                                          in_=psA[:, eo * 1024 + par * 512:eo * 1024 + (par + 1) * 512],
                                          func=AF.Exp, bias=fconsts[:, 0:1],
                                          scale=0.125), "cS")
                    vt = V1[:, kt, 0:1040].rearrange("p (d h) -> p d h", h=16)
                    for eo in range(2):
                        te.wait_ge(P.sems["cS"], exp_done[(kt, eo)])
                        if kt == 0 and eo == 0:
                            te.wait_ge(P.sems["cV"], gate["tt"][pp % 2])
                        r0 = 64 * eo
                        P.inc(te.matmul(psB[r0:r0 + 64, SB:SB + 512],
                                        vt[:, 0:64, he + eo],
                                        ebuf[par][eo],
                                        start=(kt == 0), stop=(kt == NT - 1),
                                        tile_position=(0, r0),
                                        skip_group_check=True), "cT")
                        P.inc(te.matmul(psB[r0:r0 + 64, SB + 512:SB + 1024],
                                        onesA[:, 0:64], ebuf[par][eo],
                                        start=(kt == 0), stop=(kt == NT - 1),
                                        tile_position=(0, r0),
                                        skip_group_check=True), "cT")
                        pv_done[(kt, eo)] = P.cnt["cT"]
                        if eo == 0 and kt + 2 < NT:
                            emit_S(kt + 2)
                gate["pvprev"] = (pv_done[(NT - 2, 1)], pv_done[(NT - 1, 1)])
                gate["expprev"] = (exp_done[(NT - 2, 1)], exp_done[(NT - 1, 1)])
                P.wait(ve, "cT")
                P.inc(ve.reciprocal(out=ST1f32[:, 0:512],
                                    in_=psB[:, SB + 512:SB + 1024]), "cV")
                P.inc(ve.tensor_tensor(out=BK[:, gpair, q0:q0 + 512],
                                       in0=psB[:, SB:SB + 512],
                                       in1=ST1f32[:, 0:512], op=ALU.mult), "cV")
                gate["tt"][pp % 2] = P.cnt["cV"]
                P.inc(ve.tensor_scalar(out=BK[:, gpair, q0:q0 + 512],
                                       in0=BK[:, gpair, q0:q0 + 512],
                                       scalar1=bv[:, gpair:gpair + 1],
                                       scalar2=None, op0=ALU.add), "cV")
                pp += 1
        P.wait(gp, "cV")
        wstate["free"] = [P.cnt["cT"], P.cnt["cT"]]

    compute_lora1T(Aqkv)
    qkv_half(0)
    if debug_stage == "qkv0":
        return finish()
    build_v()
    if debug_stage == "v0":
        return finish()
    attn_half(0)
    compute_lora1T(Aqkv)
    qkv_half(1)
    attn_half(1)
    if debug_stage == "attn0":
        return finish()

    P.wait(te, "cV")
    P.wait(te, "cS")
    for ch in range(4):
        for j in range(8):
            P.inc(te.matmul(psA[0:8, ch * 512:(ch + 1) * 512], Aproj[:, j, :],
                            BK[:, j, ch * 512:(ch + 1) * 512],
                            start=(j == 0), stop=(j == 7)), "cT")
    P.wait(ve, "cT")
    P.inc(ve.tensor_scalar(out=lora1T, in0=psA[0:8, 0:T], scalar1=LORA_SCALE,
                           scalar2=None, op0=ALU.mult), "cV")
    P.wait(te, "cV")
    tmpP = BV.rearrange("p a b -> p (a b)")[:, 0:T]

    def evict_proj(ft, psum):
        P.wait(sc, "cT")
        P.inc(sc.activation(out=tmpP, in_=psum[:, 0:T], func=AF.Identity,
                            bias=bproj[:, ft:ft + 1]), "cS")
        P.wait(ve, "cS")
        return P.inc(ve.tensor_tensor(out=XT[:, ft, :], in0=XT[:, ft, :], in1=tmpP,
                                      op=ALU.add), "cV")

    pendp = {}
    for fo in range(0, NCT, 2):
        slot, bsl = load_w(wcols(Wproj_h, fo * 128, 256), Bproj_h[:, fo * 128:(fo + 2) * 128])
        last = None
        for sub in range(2):
            ft = fo + sub
            psum = (psA, psB)[ft % 2]
            if ft >= 2:
                te.wait_ge(P.sems["cV"], pendp[ft - 2])
            for ch in range(4):
                ps = psum[:, ch * 512:(ch + 1) * 512]
                for j in range(8):
                    last = te.matmul(ps, slot[:, j, sub * 128:(sub + 1) * 128],
                                     BK[:, j, ch * 512:(ch + 1) * 512],
                                     start=(j == 0), stop=False)
                    P.inc(last, "cT")
                last = te.matmul(ps, bsl[:, sub * 128:(sub + 1) * 128],
                                 lora1T[:, ch * 512:(ch + 1) * 512],
                                 start=False, stop=True)
                P.inc(last, "cT")
            pendp[ft] = evict_proj(ft, psum)
        slot_done(last)
    te.wait_ge(P.sems["cV"], pendp[NCT - 1])
    ve.wait_ge(P.sems["cV"], pendp[NCT - 1])
    if debug_stage == "proj0":
        return finish()

    P.wait(gp, "cT")
    P.wait(gp, "cV")
    P.wait(gp, "cS")
    P.wait(sy, "cT")
    rs_free("VR")
    normC = rs_alloc("normC", [8, T], BF16)
    combT = rs_alloc("combT", [8, T], BF16)
    xn_guard_zero()
    layernorm()
    if debug_stage == "ln1":
        return finish()

    P.wait(te, "cV")
    for tt in range(NT):
        ps = psA[:, (tt % 4) * 512:(tt % 4) * 512 + 8]
        for cc in range(NCT):
            P.inc(te.matmul(ps, xn(cc, tt * 128, (tt + 1) * 128), gateT[:, cc, :],
                            start=(cc == 0), stop=(cc == NCT - 1)), "cT")
        P.wait(ve, "cT")
        P.inc(ve.tensor_copy(out=LA[:, tt, :], in_=ps), "cV")
        P.wait(te, "cV")
    P.inc(ve.tensor_tensor(out=GB[:, :, 0:4], in0=LA[:, :, 0:4], in1=LA[:, :, 4:8],
                           op=ALU.max), "cV")
    P.inc(ve.tensor_tensor(out=GB[:, :, 4:8], in0=GB[:, :, 0:4], in1=GB[:, :, 2:6],
                           op=ALU.max), "cV")
    P.inc(ve.tensor_tensor(out=GB[:, :, 8:12], in0=GB[:, :, 4:8], in1=GB[:, :, 5:9],
                           op=ALU.max), "cV")
    P.inc(ve.tensor_scalar(out=GB[:, :, 4:8], in0=GB[:, :, 8:12], scalar1=-1.0,
                           scalar2=None, op0=ALU.mult), "cV")
    P.wait(sy, "cV")
    P.inc(sy.dma_start(out=GL, in_=GB), "sMV", 16)
    P.wait(ve, "sMV")
    for tt in range(NT):
        P.inc(ve.tensor_scalar(out=MA[:, tt, :], in0=LA[:, tt, :],
                               scalar1=GL[:, tt, 8:9], scalar2=None, op0=ALU.is_ge), "cV")
    P.inc(ve.tensor_scalar(out=EA[:], in0=MA[:], scalar1=1e5, scalar2=None,
                           op0=ALU.mult), "cV")
    P.inc(ve.tensor_tensor(out=EA[:], in0=LA[:], in1=EA[:], op=ALU.subtract), "cV")
    P.inc(ve.tensor_tensor(out=MB[:, :, 0:4], in0=EA[:, :, 0:4], in1=EA[:, :, 4:8],
                           op=ALU.max), "cV")
    P.inc(ve.tensor_tensor(out=MB[:, :, 4:8], in0=MB[:, :, 0:4], in1=MB[:, :, 2:6],
                           op=ALU.max), "cV")
    P.inc(ve.tensor_tensor(out=MB[:, :, 8:12], in0=MB[:, :, 4:8], in1=MB[:, :, 5:9],
                           op=ALU.max), "cV")
    P.wait(sy, "cV")
    P.inc(sy.dma_start(out=ML[:], in_=MB[:]), "sMV", 16)
    P.wait(ve, "sMV")
    P.wait(sc, "sMV")
    for tt in range(NT):
        P.inc(ve.tensor_scalar(out=MA[:, tt, :], in0=LA[:, tt, :],
                               scalar1=ML[:, tt, 8:9], scalar2=None, op0=ALU.is_ge), "cV")
        P.inc(sc.activation(out=EA[:, tt, :], in_=LA[:, tt, :], func=AF.Exp,
                            bias=GL[:, tt, 4:5]), "cS")
    P.wait(ve, "cS")
    P.inc(ve.tensor_tensor(out=EA[:], in0=EA[:], in1=MA[:], op=ALU.mult), "cV")
    P.wait(te, "cV")
    for tt in range(NT):
        ps = psB[0:8, (tt % 4) * 512:(tt % 4) * 512 + 128]
        P.inc(te.transpose(ps, EA[:, tt, :], ident[:]), "cT")
        P.wait(ve, "cT")
        P.inc(ve.tensor_copy(out=combT[:, tt * 128:(tt + 1) * 128], in_=ps), "cV")
        P.wait(te, "cV")
    for ch in range(4):
        P.inc(te.matmul(psA[0:64, ch * 512:(ch + 1) * 512], onesA[0:8, 0:64],
                        combT[:, ch * 512:(ch + 1) * 512], start=True, stop=True), "cT")
    P.wait(ve, "cT")
    P.inc(ve.reciprocal(out=psB[0:64, 0:T], in_=psA[0:64, 0:T]), "cV")
    P.inc(ve.tensor_tensor(out=normC[:], in0=combT[:], in1=psB[0:8, 0:T],
                           op=ALU.mult), "cV")
    P.wait(te, "cV")
    if debug_stage == "gate":
        return finish()
    rs_free("combT")

    def evict_shared_h(ft, psum):
        P.wait(sc, "cT")
        c = P.inc(sc.activation(out=BK[:, ft, :], in_=psum[:, 0:T], func=AF.Silu,
                                bias=shupb[:, ft:ft + 1]), "cS")
        return ("cS", c)
    mm_1024(evict_shared_h, shup_h, lambda cc, a, b: xn(cc, a, b))

    def evict_shared_d(ft, psum):
        P.wait(sc, "cT")
        P.inc(sc.activation(out=sStage[:, 0:T], in_=psum[:, 0:T], func=AF.Identity,
                            bias=shdwb[:, ft:ft + 1]), "cS")
        P.wait(ve, "cS")
        c = P.inc(ve.tensor_tensor(out=XT[:, ft, :], in0=XT[:, ft, :], in1=sStage[:, 0:T],
                                   op=ALU.add), "cV")
        return ("cV", c)
    mm_1024(evict_shared_d, shdw_h, lambda cc, a, b: BK[:, cc, a:b])
    P.wait(sy, "cT")
    P.inc(sy.dma_start(out=esel, in_=esel_h), "sIN", 16)
    P.wait(te, "sIN")
    if debug_stage == "shared":
        return finish()

    for e in range(8):
        P.wait(te, "cV")
        for ch in range(4):
            P.inc(te.matmul(psB[:, ch * 512:(ch + 1) * 512], esel[:, e * 128:(e + 1) * 128],
                            normC[:, ch * 512:(ch + 1) * 512],
                            start=True, stop=True), "cT")
        P.wait(sc, "cT")
        P.inc(sc.copy(out=bcMoE, in_=psB[:, 0:T]), "cS")
        P.wait(ve, "cS")
        P.wait(te, "cS")

        def evict_up(ft, psum, e=e):
            P.wait(sc, "cT")
            P.inc(sc.activation(out=sStage[:, 0:T], in_=psum[:, 0:T], func=AF.Silu,
                                bias=upb[:, ft, e:e + 1]), "cS")
            P.wait(ve, "cS")
            c = P.inc(ve.tensor_tensor(out=BQ[:, ft, :], in0=sStage[:, 0:T], in1=bcMoE,
                                       op=ALU.mult), "cV")
            return ("cV", c)
        mm_1024(evict_up, upw_h[e], lambda cc, a, b: xn(cc, a, b))

        def evict_down(ft, psum, e=e):
            P.wait(ve, "cT")
            return P.inc(ve.tensor_tensor(out=XT[:, ft, :], in0=XT[:, ft, :],
                                          in1=psum[:, 0:T], op=ALU.add), "cV")

        pendd = {}
        for fo in range(0, NCT, 2):
            slot, bsl = load_w(wcols(dww_h[e], fo * 128, 256),
                               dwb_h[:, fo * 128:(fo + 2) * 128] if e == 0 else None)
            last = None
            for sub in range(2):
                ft = fo + sub
                psum = (psA, psB)[ft % 2]
                if ft >= 2:
                    te.wait_ge(P.sems["cV"], pendd[ft - 2])
                for ch in range(4):
                    ps = psum[:, ch * 512:(ch + 1) * 512]
                    for cc in range(NCT):
                        last = te.matmul(ps, slot[:, cc, sub * 128:(sub + 1) * 128],
                                         BQ[:, cc, ch * 512:(ch + 1) * 512],
                                         start=(cc == 0), stop=(e != 0 and cc == NCT - 1))
                        P.inc(last, "cT")
                    if e == 0:
                        last = te.matmul(ps, bsl[:, sub * 128:(sub + 1) * 128],
                                         normC[:, ch * 512:(ch + 1) * 512],
                                         start=False, stop=True)
                        P.inc(last, "cT")
                pendd[ft] = evict_down(ft, psum)
            slot_done(last)
        te.wait_ge(P.sems["cV"], pendd[NCT - 1])
    if debug_stage == "moe":
        return finish()
    rs_free("normC")

    def evict_qT(dst):
        def f(ft, psum):
            P.wait(ve, "cT")
            c = P.inc(ve.tensor_copy(out=dst[:, ft, :], in_=psum[:, 0:T]), "cV")
            return ("cV", c)
        return f
    mm_1024(evict_qT(BQ), wqT_h, lambda cc, a, b: xn(cc, a, b))
    mm_1024(evict_qT(BK), wkT_h, lambda cc, a, b: xn(cc, a, b))

    VR2 = rs_alloc("VR2", [128, 16, 1040], BF16)
    vst["V1"] = VR2
    vrf = VR2.rearrange("p a b -> p (a b)")
    cosv = vrf[:, 0:4 * T].rearrange("p (a t) -> p a t", t=T)
    sinv = vrf[:, 4 * T:8 * T].rearrange("p (a t) -> p a t", t=T)
    P.wait(gp, "cV")
    P.wait(gp, "cT")
    P.inc(gp.dma_start(out=cosv[:], in_=cosT_h), "sW", 16)
    P.inc(gp.dma_start(out=sinv[:], in_=sinT_h), "sW", 16)
    P.wait(ve, "sW")
    for buf in (BQ, BK):
        for i in range(4):
            qe, qo = buf[:, i, :], buf[:, i + 4, :]
            cs, sn = cosv[:, i, :], sinv[:, i, :]
            P.inc(ve.tensor_tensor(out=sStage[:, 0:T], in0=qo, in1=sn, op=ALU.mult), "cV")
            P.inc(ve.tensor_tensor(out=psA[:, 0:T], in0=qe, in1=cs, op=ALU.mult), "cV")
            P.inc(ve.tensor_tensor(out=psB[:, 0:T], in0=psA[:, 0:T], in1=sStage[:, 0:T],
                                   op=ALU.add), "cV")
            P.inc(ve.tensor_tensor(out=sStage[:, 0:T], in0=qe, in1=sn, op=ALU.mult), "cV")
            P.inc(ve.tensor_tensor(out=psA[:, 0:T], in0=qo, in1=cs, op=ALU.mult), "cV")
            P.inc(ve.tensor_tensor(out=qo, in0=psA[:, 0:T], in1=sStage[:, 0:T],
                                   op=ALU.subtract), "cV")
            P.inc(ve.tensor_copy(out=qe, in_=psB[:, 0:T]), "cV")
    P.wait(te, "cV")
    P.wait(gp, "cV")

    V1 = VR2
    for ci in range(4):
        slot, _b = load_w(wcols(wvT_h, ci * 256, 256))
        last = None
        pend = {}
        for tt in range(NT):
            ps = psA[:, (tt % 4) * 512:(tt % 4) * 512 + 256]
            if tt >= 4:
                te.wait_ge(P.sems["cV"], pend[tt - 4])
            for cc in range(NCT):
                last = te.matmul(ps, xn(cc, tt * 128, (tt + 1) * 128),
                                 slot[:, cc, 0:256], start=(cc == 0), stop=(cc == NCT - 1))
                P.inc(last, "cT")
            P.wait(ve, "cT")
            pend[tt] = P.inc(ve.tensor_copy(out=V1[:, tt, ci * 256:(ci + 1) * 256],
                                            in_=ps), "cV")
        te.wait_ge(P.sems["cV"], pend[NT - 1])
        slot_done(last)
    if debug_stage == "qkv1":
        return finish()

    P.wait(gp, "cT")
    P.wait(gp, "cV")
    P.wait(sy, "cT")
    BVflat = BV.rearrange("p a b -> p (a b)")
    m0 = BVflat[:, 7680:8576]
    P.inc(sy.dma_start(out=m0[:], in_=m0_h), "sIN", 16)
    P.wait(ve, "sIN")

    psO = psB.rearrange("p (a b) -> p a b", b=512)

    BVf32 = BVflat.bitcast(F32)
    a1inv32 = BVf32[:, 3328:3840]
    a1tmp = BVflat[:, 8576:9088]

    def expslot(kt):
        if kt < 4:
            return sStage[:, kt * 512:(kt + 1) * 512]
        return BVflat[:, (kt - 4) * 512:(kt - 3) * 512]

    for qc in range(4):
        q0 = qc * 512
        nk = 4 * qc + 4
        exp_gd = {}
        for kg in range(0, nk, 2):
            g = kg // 2
            gb = (g % 2) * 1024
            kn = min(2, nk - kg)
            if g >= 2:
                te.wait_ge(P.sems["cS"], exp_gd[g - 2])
            for ki in range(kn):
                kt = kg + ki
                for cc in range(NCT):
                    P.inc(te.matmul(psA[:, gb + ki * 512:gb + (ki + 1) * 512],
                                    BK[:, cc, kt * 128:(kt + 1) * 128],
                                    BQ[:, cc, q0:q0 + 512],
                                    start=(cc == 0), stop=(cc == NCT - 1)), "cT")
            P.wait(sc, "cT")
            for ki in range(kn):
                exp_gd[g] = P.inc(
                    sc.activation(out=expslot(kg + ki), in_=psA[:, gb + ki * 512:gb + (ki + 1) * 512],
                                  func=AF.Exp, bias=fconsts[:, 1:2], scale=0.03125), "cS")
            for ki in range(kn):
                kt = kg + ki
                r = kt - 4 * qc
                if 0 <= r <= 3:
                    ve.wait_ge(P.sems["cS"], exp_gd[g])
                    P.inc(ve.tensor_tensor(out=expslot(kt), in0=expslot(kt),
                                           in1=m0[:, 384 - 128 * r:384 - 128 * r + 512],
                                           op=ALU.mult), "cV")
        P.wait(te, "cV")
        P.wait(te, "cS")
        for kt in range(nk):
            for dt in range(4):
                P.inc(te.matmul(psO[:, dt, :], V1[:, kt, dt * 128:(dt + 1) * 128],
                                expslot(kt), start=(kt == 0), stop=(kt == nk - 1)), "cT")
            P.inc(te.matmul(psA[:, 1024:1536], onesA, expslot(kt),
                            start=(kt == 0), stop=(kt == nk - 1)), "cT")
        P.wait(ve, "cT")
        P.inc(ve.reciprocal(out=a1inv32, in_=psA[:, 1024:1536]), "cV")
        for dt in range(4):
            P.inc(ve.tensor_tensor(out=a1tmp, in0=psO[:, dt, :],
                                   in1=a1inv32, op=ALU.mult), "cV")
            P.inc(ve.tensor_tensor(out=XT[:, dt, q0:q0 + 512], in0=XT[:, dt, q0:q0 + 512],
                                   in1=a1tmp, op=ALU.add), "cV")
        P.wait(te, "cV")
        if debug_stage == "attn1q0" and qc == 0:
            return finish()
        for kt in range(nk):
            for dt in range(4):
                P.inc(te.matmul(psO[:, dt, :], V1[:, kt, (dt + 4) * 128:(dt + 5) * 128],
                                expslot(kt), start=(kt == 0), stop=(kt == nk - 1)), "cT")
        P.wait(ve, "cT")
        for dt in range(4):
            P.inc(ve.tensor_tensor(out=a1tmp, in0=psO[:, dt, :],
                                   in1=a1inv32, op=ALU.mult), "cV")
            P.inc(ve.tensor_tensor(out=XT[:, dt + 4, q0:q0 + 512],
                                   in0=XT[:, dt + 4, q0:q0 + 512],
                                   in1=a1tmp, op=ALU.add), "cV")
        P.wait(te, "cV")
        P.wait(sc, "cV")

    return finish()




_BUILD = {}


def _get_program(debug_stage=None):
    key = debug_stage
    if key not in _BUILD:
        _BUILD[key] = build_program(debug_stage)
    return _BUILD[key]


def kernel(**inputs):
    cst = _host_constants()
    H = _prep_inputs(inputs)
    nc = _get_program(None)
    x = np.asarray(inputs["L0_attn_x"], dtype=np.float32)
    base = {}
    base.update(H)
    base.update(cst)
    in_maps = []
    for c in range(8):
        m = dict(base)
        m["x"] = np.ascontiguousarray(x[c])
        in_maps.append(m)
    res = run_bass_kernel_spmd(nc, in_maps, core_ids=list(range(8)))
    out = np.stack([np.asarray(res.results[c]["out"]) for c in range(8)], axis=0)
    return out.astype(np.float32)


# version 16
# speedup vs baseline: 1.2005x; 1.2005x over previous
import numpy as np
import ml_dtypes
import contextlib
import concourse.bass as bass
import concourse.mybir as mybir
from concourse.bass_utils import run_bass_kernel_spmd

F32 = mybir.dt.float32
BF16 = mybir.dt.bfloat16
AF = mybir.ActivationFunctionType
ALU = mybir.AluOpType

T = 2048
C = 1024
NT = 16
NCT = 8
HD = 64
LORA_SCALE = 2.0
EXP_SHIFT0 = 6.0
EXP_SHIFT1 = 6.0
GCOL = 8
_bf = ml_dtypes.bfloat16
DBG_MIN = False



def _host_constants():
    cst = {}
    cst["ident_f32"] = np.eye(128, dtype=np.float32)
    cst["ones_1x128"] = np.ones((1, 128), dtype=_bf)
    cst["ones_8x1"] = np.ones((8, 16), dtype=_bf)
    cst["ones_c128"] = np.ones((128, 8), dtype=_bf)
    cst["zeros_guard"] = np.zeros((128, NCT, GCOL), dtype=_bf)
    cst["vones"] = np.ones((128, 16, 16), dtype=_bf)
    ps = np.zeros((16, 8 * 128), dtype=_bf)
    for j in range(8):
        ps[2 * j, j * 128:j * 128 + 64] = 1.0
        ps[2 * j + 1, j * 128 + 64:j * 128 + 128] = 1.0
    cst["pairsel"] = ps
    es = np.zeros((8, 1024), dtype=_bf)
    for e in range(8):
        es[e, e * 128:(e + 1) * 128] = 1.0
    cst["esel"] = es
    k = np.arange(128)[:, None]
    c0 = np.arange(1024)[None, :]
    cst["m0"] = (c0 >= k + 384).astype(_bf)[:, :896]
    i = np.arange(C // 2, dtype=np.float64)
    phi = np.power(10000.0, -2.0 * (i - 1.0) / C)
    ang = np.arange(T, dtype=np.float64)[None, :] * phi[:, None]
    cst["cosT"] = np.ascontiguousarray(
        np.cos(ang).astype(_bf).reshape(4, 128, T).transpose(1, 0, 2))
    cst["sinT"] = np.ascontiguousarray(
        np.sin(ang).astype(_bf).reshape(4, 128, T).transpose(1, 0, 2))
    sc = np.zeros((128, 8), dtype=np.float32)
    sc[:, 0] = -EXP_SHIFT0
    sc[:, 1] = -EXP_SHIFT1
    cst["fconsts"] = sc
    return cst


def _prep_inputs(I):
    H = {}
    A = np.ascontiguousarray
    dd, hh = np.meshgrid(np.arange(64), np.arange(16), indexing="ij")
    vperm = (64 * hh + dd).reshape(-1)
    W = I["L0_attn_W_qkv"]
    H["Wqkv"] = A(np.concatenate([W[:, :2048], W[:, 2048:][:, vperm]], axis=1))
    B = I["L0_attn_B_qkv"]
    H["B_qkv"] = A(np.concatenate([B[:, :2048], B[:, 2048:][:, vperm]], axis=1))
    H["b_qkv"] = A(I["L0_attn_b_qkv"][:2048].reshape(16, 128).T)
    H["b_v"] = A(I["L0_attn_b_qkv"][2048:].reshape(8, 128).T)
    H["A_qkv"] = A(I["L0_attn_A_qkv"].reshape(8, 128, 8).transpose(1, 0, 2))
    H["W_proj"] = A(I["L0_attn_W_proj"])
    H["b_proj"] = A(I["L0_attn_b_proj"].reshape(8, 128).T)
    H["A_proj"] = A(I["L0_attn_A_proj"].reshape(8, 128, 8).transpose(1, 0, 2))
    H["B_proj"] = A(I["L0_attn_B_proj"])
    H["WrT"] = A(I["L0_ffn_W_r"].T)
    H["WkT"] = A(I["L0_ffn_W_k"].T)
    H["WvT"] = A(I["L0_ffn_W_v"].T)
    H["mu_r"] = A(I["L0_ffn_mu_r"].reshape(8, 128).T)
    H["mu_k"] = A(I["L0_ffn_mu_k"].reshape(8, 128).T)
    perm = np.concatenate([np.arange(0, C, 2), np.arange(1, C, 2)])
    H["wqT"] = A(I["L1_attn_wq"][perm].T)
    H["wkT"] = A(I["L1_attn_wk"][perm].T)
    H["wvT"] = A(I["L1_attn_wv"].T)
    H["gate_wT"] = A(I["L1_ffn_gate_w"].T.reshape(8, 128, 8).transpose(1, 0, 2))
    H["up_w"] = A(I["L1_ffn_up_w"])
    H["up_b"] = A(I["L1_ffn_up_b"].reshape(8, 8, 128).transpose(2, 1, 0))
    H["down_w"] = A(I["L1_ffn_down_w"])
    H["down_b"] = A(I["L1_ffn_down_b"])
    H["sh_up_w"] = A(I["L1_ffn_sh_up_w"])
    H["sh_up_b"] = A(I["L1_ffn_sh_up_b"].reshape(8, 128).T)
    H["sh_down_w"] = A(I["L1_ffn_sh_down_w"])
    H["sh_down_b"] = A(I["L1_ffn_sh_down_b"].reshape(8, 128).T)
    for k2 in H:
        if H[k2].dtype != np.float32:
            H[k2] = H[k2].astype(np.float32)
    return H



class Prog:
    def __init__(self, debug_stage=None):
        self.debug_stage = debug_stage
        self.dbg = {}
        self.stopped = False

    def sem_init(self, nc):
        self.nc = nc
        self.cnt = {}
        self.sems = {}
        for s in ["sIN", "sW", "sMV", "sOUT", "cT", "cV", "cS"]:
            cm = nc.semaphore(s)
            self.sems[s] = self._stack.enter_context(cm)
            self.cnt[s] = 0

    def inc(self, instr, name, by=1):
        instr.then_inc(self.sems[name], by)
        self.cnt[name] += by
        return self.cnt[name]

    def wait(self, eng, name, target=None):
        eng.wait_ge(self.sems[name], self.cnt[name] if target is None else target)


def build_program(debug_stage=None):
    nc = bass.Bass("TRN2", target_bir_lowering=False,
                   detect_race_conditions=False)
    P = Prog(debug_stage)
    P._stack = contextlib.ExitStack()
    P.sem_init(nc)
    stack = P._stack

    din = {}
    def dram_in(name, shape, dtype=F32):
        din[name] = nc.dram_tensor(name, list(shape), dtype, kind="ExternalInput").ap()
        return din[name]

    x_h = dram_in("x", (T, C))
    Wqkv_h = dram_in("Wqkv", (C, 3 * C))
    bqkv_h = dram_in("b_qkv", (128, 16))
    bv_h = dram_in("b_v", (128, 8))
    Aqkv_h = dram_in("A_qkv", (128, 8, 8))
    Bqkv_h = dram_in("B_qkv", (8, 3 * C))
    Wproj_h = dram_in("W_proj", (C, C))
    bproj_h = dram_in("b_proj", (128, 8))
    Aproj_h = dram_in("A_proj", (128, 8, 8))
    Bproj_h = dram_in("B_proj", (8, C))
    WrT_h = dram_in("WrT", (C, C))
    WkT_h = dram_in("WkT", (C, C))
    WvT_h = dram_in("WvT", (C, C))
    mur_h = dram_in("mu_r", (128, 8))
    muk_h = dram_in("mu_k", (128, 8))
    wqT_h = dram_in("wqT", (C, C))
    wkT_h = dram_in("wkT", (C, C))
    wvT_h = dram_in("wvT", (C, C))
    gateT_h = dram_in("gate_wT", (128, 8, 8))
    upw_h = dram_in("up_w", (8, C, C))
    upb_h = dram_in("up_b", (128, 8, 8))
    dww_h = dram_in("down_w", (8, C, C))
    dwb_h = dram_in("down_b", (8, C))
    shup_h = dram_in("sh_up_w", (C, C))
    shupb_h = dram_in("sh_up_b", (128, 8))
    shdw_h = dram_in("sh_down_w", (C, C))
    shdwb_h = dram_in("sh_down_b", (128, 8))
    ident_h = dram_in("ident_f32", (128, 128))
    ones1_h = dram_in("ones_1x128", (1, 128), BF16)
    ones8_h = dram_in("ones_8x1", (8, 16), BF16)
    onesc_h = dram_in("ones_c128", (128, 8), BF16)
    zguard_h = dram_in("zeros_guard", (128, NCT, GCOL), BF16)
    vones_h = dram_in("vones", (128, 16, 16), BF16)
    pairsel_h = dram_in("pairsel", (16, 1024), BF16)
    esel_h = dram_in("esel", (8, 1024), BF16)
    m0_h = dram_in("m0", (128, 896), BF16)
    cosT_h = dram_in("cosT", (128, 4, T), BF16)
    sinT_h = dram_in("sinT", (128, 4, T), BF16)
    fconsts_h = dram_in("fconsts", (128, 8))
    out_h = nc.dram_tensor("out", [T, C], F32, kind="ExternalOutput").ap()
    dbg_h = {}
    if debug_stage is not None:
        for nm, shp, dt in [("dbg_xt", [128, NCT, T], F32),
                            ("dbg_xn", [128, NCT, GCOL + T], BF16),
                            ("dbg_bq", [128, NCT, T], BF16),
                            ("dbg_bk", [128, NCT, T], BF16),
                            ("dbg_bv", [128, 16, 1040], BF16),
                            ("dbg_sall", [16, T], BF16),
                            ("dbg_comb", [8, T], BF16),
                            ("dbg_normc", [8, T], BF16),
                            ("dbg_la", [128, 16, 8], F32),
                            ("dbg_ea", [128, 16, 8], F32)]:
            dbg_h[nm] = nc.dram_tensor(nm, shp, dt, kind="ExternalOutput").ap()

    S = lambda *a: stack.enter_context(nc.sbuf_tensor(*a))

    XT = S("XT", [128, NCT, T], F32)
    BQ = S("BQ", [128, NCT, T], BF16)
    BK = S("BK", [128, NCT, T], BF16)
    BV = S("BV", [128, 16, 1040], BF16)
    ST = [S("ST0", [128, NCT, 256], BF16), S("ST1", [128, NCT, 256], BF16)]
    BqS = [S("BqS0", [8, 256], BF16), S("BqS1", [8, 256], BF16)]
    XNv = BV.rearrange("p a b -> p (a b)")[:, 0:NCT * (GCOL + T)] \
            .rearrange("p (a b) -> p a b", b=GCOL + T)
    xn_state = {"XN": XNv, "cm": None}

    def xn_guard_zero():
        P.inc(sy.dma_start(out=XNv[:, :, 0:GCOL], in_=zguard_h), "sIN", 16)
    BVsp = BV.rearrange("p a b -> p (a b)")
    onesA = BVsp[:, 16448:16576]
    ident = S("identb", [128, 128], F32)
    ones1 = S("ones1b", [1, 128], BF16)
    ones8 = S("ones8b", [8, 16], BF16)
    onesc = S("onescb", [128, 8], BF16)
    fconsts = S("fconstsb", [128, 8], F32)
    bqkv = S("bqkvb", [128, 16], F32)
    bv = S("bvb", [128, 8], F32)
    bproj = S("bprojb", [128, 8], F32)
    mur = S("murb", [128, 8], F32)
    muk = S("mukb", [128, 8], F32)
    upb = S("upbb", [128, 8, 8], F32)
    shupb = S("shupbb", [128, 8], F32)
    shdwb = S("shdwbb", [128, 8], F32)
    Aqkv = S("Aqkvb", [128, 8, 8], BF16)
    Aproj = S("Aprojb", [128, 8, 8], BF16)
    gateT = S("gateTb", [128, 8, 8], BF16)

    RS = {}

    def rs_alloc(name, shape, dtype):
        cm = nc.sbuf_tensor(name, shape, dtype, side="right")
        RS[name] = (cm, cm.__enter__())
        return RS[name][1]

    def rs_free(name):
        cm, _ = RS.pop(name)
        cm.__exit__(None, None, None)

    BQf32 = BQ.bitcast(F32)
    BQrow = BQf32.rearrange("p a b -> p (a b)")
    rowsA = BQrow[0:1, 0:T]
    rowsB = BQrow[0:1, T:2 * T]
    rowsE = BQrow[0:1, 2 * T:3 * T]
    BKflat = BK.rearrange("p a b -> p (a b)")
    rowsC = BKflat[0:1, 0:T]
    rowsD = BKflat[0:1, T:2 * T]
    BKf32 = BK.bitcast(F32).rearrange("p a b -> p (a b)")
    invden = BKf32[0:1, 0:T]
    a_invdenC = BKflat[0:1, 2 * T:3 * T]
    BQflat32 = BQf32.rearrange("p a b -> p (a b)")
    outStage = [BQflat32[:, 0:1024], BQflat32[:, 2048:3072],
                BQflat32[:, 4096:5120], BQflat32[:, 6144:7168]]
    bcMoE = BKflat[:, 0:T]
    esel = BKflat[0:8, 3 * T:3 * T + 1024]
    GB = BQflat32[:, 1024:1216].rearrange("p (a b) -> p a b", b=12)
    MB = GB
    LA = BQflat32[:, 1216:1344].rearrange("p (a b) -> p a b", b=8)
    EA = BQflat32[:, 1344:1472].rearrange("p (a b) -> p a b", b=8)
    MA = BQflat32[:, 1472:1600].rearrange("p (a b) -> p a b", b=8)
    GL = BQflat32[:, 1600:1792].rearrange("p (a b) -> p a b", b=12)
    ML = BQflat32[:, 1792:1984].rearrange("p (a b) -> p a b", b=12)
    psA = stack.enter_context(nc.psum_tensor("psA", [128, T], F32))
    psB = stack.enter_context(nc.psum_tensor("psB", [128, T], F32))

    sStage = rs_alloc("sStage", [128, T], BF16)

    nc.gpsimd.dma_reset(range(155, 256))
    nc.gpsimd.sem_clear(range(155, 256))
    nc.all_engine_barrier()

    sy, gp, te, ve, sc = nc.sync, nc.gpsimd, nc.tensor, nc.vector, nc.scalar

    P.inc(sy.dma_start(out=onesA,
                       in_=vones_h.rearrange("p a b -> p (a b)")[:, 0:128]), "sIN", 16)
    for dst, src in [(ident, ident_h), (ones1, ones1_h), (ones8, ones8_h),
                     (onesc, onesc_h), (fconsts, fconsts_h), (bqkv, bqkv_h),
                     (bv, bv_h), (bproj, bproj_h), (mur, mur_h), (muk, muk_h),
                     (upb, upb_h), (shupb, shupb_h), (shdwb, shdwb_h)]:
        P.inc(sy.dma_start(out=dst[:], in_=src), "sIN", 16)
    for dst, src in [(Aqkv, Aqkv_h), (Aproj, Aproj_h), (gateT, gateT_h)]:
        P.inc(gp.dma_start(out=dst[:], in_=src), "sW", 16)

    wstate = {"i": 0, "free": [0, 0]}

    def load_w(src_ap, b_src=None, width=256):
        i = wstate["i"] % 2
        wstate["i"] += 1
        gp.wait_ge(P.sems["cT"], wstate["free"][i])
        P.inc(gp.dma_start(out=ST[i][:, :, 0:width], in_=src_ap), "sW", 16)
        if b_src is not None:
            P.inc(gp.dma_start(out=BqS[i][:, 0:width], in_=b_src), "sW", 16)
        te.wait_ge(P.sems["sW"], P.cnt["sW"])
        return (ST[i], BqS[i])

    def slot_done(instr):
        i = (wstate["i"] - 1) % 2
        wstate["free"][i] = P.cnt["cT"]

    def wcols(h, f0, w):
        return h.rearrange("(j p) f -> p j f", p=128)[:, :, f0:f0 + w]

    def xn(cc, t0=0, t1=T):
        return xn_state["XN"][:, cc, GCOL + t0:GCOL + t1]

    def finish():
        P.wait(te, "cV")
        P.wait(te, "cS")
        P.wait(te, "sMV")
        if debug_stage is None:
            dma_done = {}
            pendf = {}
            fj = 0
            for tt in range(NT):
                ob = outStage[tt % 4]
                dq, dsem = sy, "sOUT"
                if tt >= 4:
                    ve.wait_ge(P.sems[dsem], dma_done[tt - 4])
                for cc in range(NCT):
                    pslot = psB[:, (fj % 4) * 512:(fj % 4) * 512 + 128]
                    if fj >= 4:
                        te.wait_ge(P.sems["cV"], pendf[fj - 4])
                    P.inc(te.transpose(pslot, XT[:, cc, tt * 128:(tt + 1) * 128],
                                       ident[:]), "cT")
                    P.wait(ve, "cT")
                    pendf[fj] = P.inc(ve.tensor_copy(out=ob[:, cc * 128:(cc + 1) * 128],
                                                     in_=pslot), "cV")
                    fj += 1
                P.wait(dq, "cV")
                dma_done[tt] = P.inc(dq.dma_start(out=out_h[tt * 128:(tt + 1) * 128, :],
                                                  in_=ob), dsem, 16)
            P.wait(sy, "sOUT")
        else:
            P.wait(sy, "cV")
            P.wait(sy, "cS")
            P.wait(sy, "cT")
            pairs = [("dbg_xt", XT), ("dbg_bq", BQ),
                     ("dbg_bk", BK),
                     ("dbg_la", LA), ("dbg_ea", EA)]
            if DBG_MIN:
                pairs = [("dbg_xt", XT)]
            if "combT" in RS and not DBG_MIN:
                pairs.append(("dbg_comb", RS["combT"][1]))
            if "normC" in RS and not DBG_MIN:
                pairs.append(("dbg_normc", RS["normC"][1]))
            if "VR2" in RS and not DBG_MIN:
                pairs.append(("dbg_bv", RS["VR2"][1]))
            elif "VR" in RS:
                pairs.append(("dbg_bv", RS["VR"][1]))
            if not DBG_MIN:
                pairs.append(("dbg_xn", xn_state["XN"]))
            for nm, sb in pairs:
                P.inc(sy.dma_start(out=dbg_h[nm], in_=sb[:]), "sOUT", 16)
            P.wait(sy, "sOUT")
        stack.close()
        return nc

    xs0 = BQ.bitcast(F32)
    xs1 = BK.bitcast(F32)
    P.inc(sy.dma_start(out=xs0[:], in_=x_h[0:1024].rearrange("(a p) c -> p a c", p=128)), "sIN", 16)
    P.inc(sy.dma_start(out=xs1[:], in_=x_h[1024:2048].rearrange("(a p) c -> p a c", p=128)), "sIN", 16)
    P.wait(te, "sIN")
    pendx = {}
    ti = 0
    for tt in range(NT):
        src = xs0 if tt < 8 else xs1
        a = tt % 8
        for cc in range(NCT):
            pslot = psA[:, (ti % 4) * 512:(ti % 4) * 512 + 128]
            if ti >= 4:
                te.wait_ge(P.sems["cV"], pendx[ti - 4])
            P.inc(te.transpose(pslot, src[:, a, cc * 128:(cc + 1) * 128], ident[:]), "cT")
            P.wait(ve, "cT")
            pendx[ti] = P.inc(ve.tensor_copy(out=XT[:, cc, tt * 128:(tt + 1) * 128],
                                             in_=pslot), "cV")
            ti += 1
    te.wait_ge(P.sems["cV"], pendx[ti - 1])

    def layernorm():
        P.wait(sc, "cV")
        for cc in range(NCT):
            P.inc(sc.activation(out=BK[:, cc, :], in_=XT[:, cc, :], func=AF.Copy), "cS")
            P.inc(sc.activation(out=BQ[:, cc, :], in_=XT[:, cc, :], func=AF.Square), "cS")
        P.wait(te, "cS")
        for ch in range(4):
            for cc in range(NCT):
                P.inc(te.matmul(psA[0:1, ch * 512:(ch + 1) * 512], onesc[:, 0:1],
                                BK[:, cc, ch * 512:(ch + 1) * 512],
                                start=(cc == 0), stop=(cc == NCT - 1)), "cT")
            for cc in range(NCT):
                P.inc(te.matmul(psB[0:1, ch * 512:(ch + 1) * 512], onesc[:, 0:1],
                                BQ[:, cc, ch * 512:(ch + 1) * 512],
                                start=(cc == 0), stop=(cc == NCT - 1)), "cT")
        P.wait(ve, "cT")
        P.inc(ve.tensor_scalar(out=rowsA[:], in0=psA[0:1, 0:T], scalar1=1.0 / C,
                               scalar2=None, op0=ALU.mult), "cV")
        P.inc(ve.tensor_scalar(out=rowsB[:], in0=psB[0:1, 0:T], scalar1=1.0 / C,
                               scalar2=None, op0=ALU.mult), "cV")
        P.inc(ve.tensor_tensor(out=rowsE[:], in0=rowsA[:], in1=rowsA[:], op=ALU.mult), "cV")
        P.inc(ve.tensor_tensor(out=rowsB[:], in0=rowsB[:], in1=rowsE[:], op=ALU.subtract), "cV")
        P.inc(ve.tensor_scalar(out=rowsB[:], in0=rowsB[:], scalar1=1e-5, scalar2=None,
                               op0=ALU.add), "cV")
        P.inc(ve.reciprocal(out=rowsB[:], in_=rowsB[:]), "cV")
        P.wait(sc, "cV")
        P.inc(sc.activation(out=rowsD[:], in_=rowsB[:], func=AF.Sqrt), "cS")
        P.inc(sc.activation(out=rowsC[:], in_=rowsA[:], func=AF.Copy, scale=-1.0), "cS")
        P.wait(te, "cS")
        for ch in range(4):
            P.inc(te.matmul(psA[:, ch * 512:(ch + 1) * 512], ones1[:],
                            rowsC[:, ch * 512:(ch + 1) * 512], start=True, stop=True), "cT")
            P.inc(te.matmul(psB[:, ch * 512:(ch + 1) * 512], ones1[:],
                            rowsD[:, ch * 512:(ch + 1) * 512], start=True, stop=True), "cT")
        P.wait(ve, "cT")
        for cc in range(NCT):
            P.inc(ve.tensor_tensor(out=BQ[:, cc, :], in0=XT[:, cc, :], in1=psA[:, 0:T],
                                   op=ALU.add), "cV")
        for cc in range(NCT):
            P.inc(ve.tensor_tensor(out=xn(cc), in0=BQ[:, cc, :], in1=psB[:, 0:T],
                                   op=ALU.mult), "cV")
        P.wait(te, "cV")
        P.wait(sc, "cV")

    xn_guard_zero()
    layernorm()
    if debug_stage == "ln0":
        return finish()

    lora1T = sStage[0:8, 0:T]

    def compute_lora1T(A_ap):
        P.wait(te, "sW")
        P.wait(te, "cV")
        P.wait(te, "cS")
        for ch in range(4):
            for cc in range(NCT):
                P.inc(te.matmul(psB[0:8, ch * 512:(ch + 1) * 512], A_ap[:, cc, :],
                                xn(cc, ch * 512, (ch + 1) * 512),
                                start=(cc == 0), stop=(cc == NCT - 1)), "cT")
        P.wait(ve, "cT")
        P.inc(ve.tensor_scalar(out=lora1T, in0=psB[0:8, 0:T], scalar1=LORA_SCALE,
                               scalar2=None, op0=ALU.mult), "cV")
        P.wait(te, "cV")

    ffn_state = {}

    def build_shift(mu_ap):
        P.wait(ve, "cS")
        for cc in range(NCT):
            P.inc(ve.tensor_tensor(out=sStage[:, 0:T], in0=xn(cc),
                                   in1=xn_state["XN"][:, cc, GCOL - 1:GCOL - 1 + T],
                                   op=ALU.subtract), "cV")
            P.inc(ve.tensor_scalar(out=sStage[:, 0:T], in0=sStage[:, 0:T],
                                   scalar1=mu_ap[:, cc:cc + 1], scalar2=None,
                                   op0=ALU.mult), "cV")
            P.inc(ve.tensor_tensor(out=BQ[:, cc, :], in0=sStage[:, 0:T],
                                   in1=xn_state["XN"][:, cc, GCOL - 1:GCOL - 1 + T],
                                   op=ALU.add), "cV")

    def mm_1024(dst_cb, w_h, rhs_cb):
        pend = {}
        for fo in range(0, NCT, 2):
            slot, _b = load_w(wcols(w_h, fo * 128, 256))
            last = None
            for sub in range(2):
                ft = fo + sub
                psum = (psA, psB)[ft % 2]
                if ft >= 2:
                    sem, cnt = pend[ft - 2]
                    te.wait_ge(P.sems[sem], cnt)
                for ch in range(4):
                    ps = psum[:, ch * 512:(ch + 1) * 512]
                    for cc in range(NCT):
                        last = te.matmul(ps, slot[:, cc, sub * 128:(sub + 1) * 128],
                                         rhs_cb(cc, ch * 512, (ch + 1) * 512),
                                         start=(cc == 0), stop=(cc == NCT - 1))
                        P.inc(last, "cT")
                pend[ft] = dst_cb(ft, psum)
            slot_done(last)
        for ft in (NCT - 2, NCT - 1):
            sem, cnt = pend[ft]
            te.wait_ge(P.sems[sem], cnt)

    def ffn0():
        kact = rs_alloc("kactR", [128, NCT, T], BF16)
        ffn_state["kact"] = kact
        build_shift(mur)
        P.wait(te, "cV")

        def evict_r(ft, psum):
            P.wait(sc, "cT")
            c = P.inc(sc.activation(out=BK[:, ft, :], in_=psum[:, 0:T], func=AF.Sigmoid), "cS")
            return ("cS", c)
        mm_1024(evict_r, WrT_h, lambda cc, a, b: BQ[:, cc, a:b])

        build_shift(muk)
        P.wait(te, "cV")

        def evict_k(ft, psum):
            P.wait(sc, "cT")
            kact = ffn_state["kact"]
            P.inc(sc.activation(out=kact[:, ft, 0:T], in_=psum[:, 0:T], func=AF.Relu), "cS")
            P.wait(ve, "cS")
            c = P.inc(ve.tensor_tensor(out=kact[:, ft, 0:T], in0=kact[:, ft, 0:T],
                                       in1=kact[:, ft, 0:T], op=ALU.mult), "cV")
            return ("cV", c)
        mm_1024(evict_k, WkT_h, lambda cc, a, b: BQ[:, cc, a:b])

        def evict_kv(ft, psum):
            P.wait(ve, "cT")
            tmp = sStage[:, 0:T]
            P.inc(ve.tensor_tensor(out=tmp, in0=psum[:, 0:T], in1=BK[:, ft, :],
                                   op=ALU.mult), "cV")
            c = P.inc(ve.tensor_tensor(out=XT[:, ft, :], in0=XT[:, ft, :], in1=tmp,
                                       op=ALU.add), "cV")
            return ("cV", c)
        mm_1024(evict_kv, WvT_h, lambda cc, a, b: ffn_state["kact"][:, cc, a:b])
        P.wait(gp, "cT")
        P.wait(gp, "cV")
        rs_free("kactR")

    ffn0()
    if debug_stage == "ffn0":
        return finish()

    vst = {}

    def qkv_half(half):
        pend = {}
        fi = 0
        for sec, tb in [(0, 0), (1, 4)]:
            fbase = sec * C + half * 512
            for fo in range(0, 4, 2):
                slot, bsl = load_w(wcols(Wqkv_h, fbase + fo * 128, 256),
                                   Bqkv_h[:, fbase + fo * 128:fbase + (fo + 2) * 128])
                last = None
                for sub in range(2):
                    ft = fo + sub
                    psum = (psA, psB)[fi % 2]
                    if fi >= 2:
                        te.wait_ge(P.sems["cS"], pend[fi - 2])
                    for ch in range(4):
                        ps = psum[:, ch * 512:(ch + 1) * 512]
                        for cc in range(NCT):
                            last = te.matmul(ps, slot[:, cc, sub * 128:(sub + 1) * 128],
                                             xn(cc, ch * 512, (ch + 1) * 512),
                                             start=(cc == 0), stop=False)
                            P.inc(last, "cT")
                        last = te.matmul(ps, bsl[:, sub * 128:(sub + 1) * 128],
                                         lora1T[:, ch * 512:(ch + 1) * 512],
                                         start=False, stop=True)
                        P.inc(last, "cT")
                    P.wait(sc, "cT")
                    bcol = half * 4 + ft + sec * 8
                    pend[fi] = P.inc(sc.activation(out=BQ[:, tb + ft, :],
                                                   in_=psum[:, 0:T], func=AF.Identity,
                                                   bias=bqkv[:, bcol:bcol + 1]), "cS")
                    fi += 1
                slot_done(last)
        sc.wait_ge(P.sems["cS"], pend[fi - 1])
        te.wait_ge(P.sems["cS"], pend[fi - 1])

    def build_v():
        V1 = rs_alloc("VR", [128, 16, 1040], BF16)
        vst["V1"] = V1
        P.wait(ve, "cT")
        P.inc(ve.memset(V1[:, :, 1024:1040], 1.0), "cV")
        P.wait(te, "cV")
        for ci in range(4):
            slot, bsl = load_w(wcols(Wqkv_h, 2048 + ci * 256, 256),
                               Bqkv_h[:, 2048 + ci * 256:2048 + (ci + 1) * 256])
            last = None
            pend = {}
            for tt in range(NT):
                ps = psA[:, (tt % 4) * 512:(tt % 4) * 512 + 256]
                if tt >= 4:
                    te.wait_ge(P.sems["cV"], pend[tt - 4])
                for cc in range(NCT):
                    last = te.matmul(ps, xn(cc, tt * 128, (tt + 1) * 128),
                                     slot[:, cc, 0:256],
                                     start=(cc == 0), stop=False)
                    P.inc(last, "cT")
                last = te.matmul(ps, lora1T[:, tt * 128:(tt + 1) * 128],
                                 bsl[:, 0:256], start=False, stop=True)
                P.inc(last, "cT")
                P.wait(ve, "cT")
                pend[tt] = P.inc(ve.tensor_copy(out=V1[:, tt, ci * 256:(ci + 1) * 256],
                                                in_=ps), "cV")
            te.wait_ge(P.sems["cV"], pend[NT - 1])
            slot_done(last)
        P.wait(te, "sMV")

    def attn_half(half):
        V1 = vst["V1"]
        ST1f32 = ST[1].rearrange("p a b -> p (a b)").bitcast(F32)
        ebuf = [[sStage[:, 0:512], sStage[:, 512:1024]],
                [sStage[:, 1024:1536], sStage[:, 1536:2048]]]
        gate = {"pvprev": (P.cnt["cT"], P.cnt["cT"]),
                "expprev": (P.cnt["cS"], P.cnt["cS"]),
                "tt": [P.cnt["cV"], P.cnt["cV"]]}
        pp = 0
        for p in range(4):
            tile = p
            gpair = half * 4 + p
            he = half * 8 + 2 * p
            for qq in range(4):
                q0 = qq * 512
                SB = (pp % 2) * 1024
                exp_done = {}
                pv_done = {}
                sv = {}

                def emit_S(kt):
                    par = kt % 2
                    if kt >= 2:
                        te.wait_ge(P.sems["cS"], exp_done[(kt - 2, 1)])
                    else:
                        te.wait_ge(P.sems["cS"], gate["expprev"][kt])
                    for eo in range(2):
                        P.inc(te.matmul(
                            psA[:, eo * 1024 + par * 512:eo * 1024 + (par + 1) * 512],
                            BQ[64 * eo:64 * eo + 64, 4 + tile, kt * 128:(kt + 1) * 128],
                            BQ[64 * eo:64 * eo + 64, tile, q0:q0 + 512],
                            start=True, stop=True,
                            tile_position=(64 * eo, 0)), "cT")
                    sv[kt] = P.cnt["cT"]

                emit_S(0)
                emit_S(1)
                for kt in range(NT):
                    par = kt % 2
                    for eo in range(2):
                        if eo == 0:
                            sc.wait_ge(P.sems["cT"], sv[kt])
                        if kt >= 2:
                            sc.wait_ge(P.sems["cT"], pv_done[(kt - 2, eo)])
                        elif eo == 0:
                            sc.wait_ge(P.sems["cT"], gate["pvprev"][kt])
                        exp_done[(kt, eo)] = P.inc(
                            sc.activation(out=ebuf[par][eo],
                                          in_=psA[:, eo * 1024 + par * 512:eo * 1024 + (par + 1) * 512],
                                          func=AF.Exp, bias=fconsts[:, 0:1],
                                          scale=0.125), "cS")
                    vt = V1[:, kt, 0:1040].rearrange("p (d h) -> p d h", h=16)
                    for eo in range(2):
                        te.wait_ge(P.sems["cS"], exp_done[(kt, eo)])
                        if kt == 0 and eo == 0:
                            te.wait_ge(P.sems["cV"], gate["tt"][pp % 2])
                        r0 = 64 * eo
                        P.inc(te.matmul(psB[r0:r0 + 64, SB:SB + 512],
                                        vt[:, 0:64, he + eo],
                                        ebuf[par][eo],
                                        start=(kt == 0), stop=(kt == NT - 1),
                                        tile_position=(0, r0),
                                        skip_group_check=True), "cT")
                        P.inc(te.matmul(psB[r0:r0 + 64, SB + 512:SB + 1024],
                                        onesA[:, 0:64], ebuf[par][eo],
                                        start=(kt == 0), stop=(kt == NT - 1),
                                        tile_position=(0, r0),
                                        skip_group_check=True), "cT")
                        pv_done[(kt, eo)] = P.cnt["cT"]
                        if eo == 0 and kt + 2 < NT:
                            emit_S(kt + 2)
                gate["pvprev"] = (pv_done[(NT - 2, 1)], pv_done[(NT - 1, 1)])
                gate["expprev"] = (exp_done[(NT - 2, 1)], exp_done[(NT - 1, 1)])
                P.wait(ve, "cT")
                P.inc(ve.reciprocal(out=ST1f32[:, 0:512],
                                    in_=psB[:, SB + 512:SB + 1024]), "cV")
                P.inc(ve.tensor_tensor(out=BK[:, gpair, q0:q0 + 512],
                                       in0=psB[:, SB:SB + 512],
                                       in1=ST1f32[:, 0:512], op=ALU.mult), "cV")
                gate["tt"][pp % 2] = P.cnt["cV"]
                P.inc(ve.tensor_scalar(out=BK[:, gpair, q0:q0 + 512],
                                       in0=BK[:, gpair, q0:q0 + 512],
                                       scalar1=bv[:, gpair:gpair + 1],
                                       scalar2=None, op0=ALU.add), "cV")
                pp += 1
        P.wait(gp, "cV")
        wstate["free"] = [P.cnt["cT"], P.cnt["cT"]]

    compute_lora1T(Aqkv)
    qkv_half(0)
    if debug_stage == "qkv0":
        return finish()
    build_v()
    if debug_stage == "v0":
        return finish()
    attn_half(0)
    compute_lora1T(Aqkv)
    qkv_half(1)
    attn_half(1)
    if debug_stage == "attn0":
        return finish()

    P.wait(te, "cV")
    P.wait(te, "cS")
    for ch in range(4):
        for j in range(8):
            P.inc(te.matmul(psA[0:8, ch * 512:(ch + 1) * 512], Aproj[:, j, :],
                            BK[:, j, ch * 512:(ch + 1) * 512],
                            start=(j == 0), stop=(j == 7)), "cT")
    P.wait(ve, "cT")
    P.inc(ve.tensor_scalar(out=lora1T, in0=psA[0:8, 0:T], scalar1=LORA_SCALE,
                           scalar2=None, op0=ALU.mult), "cV")
    P.wait(te, "cV")
    tmpP = BV.rearrange("p a b -> p (a b)")[:, 0:T]

    def evict_proj(ft, psum):
        P.wait(sc, "cT")
        P.inc(sc.activation(out=tmpP, in_=psum[:, 0:T], func=AF.Identity,
                            bias=bproj[:, ft:ft + 1]), "cS")
        P.wait(ve, "cS")
        return P.inc(ve.tensor_tensor(out=XT[:, ft, :], in0=XT[:, ft, :], in1=tmpP,
                                      op=ALU.add), "cV")

    pendp = {}
    for fo in range(0, NCT, 2):
        slot, bsl = load_w(wcols(Wproj_h, fo * 128, 256), Bproj_h[:, fo * 128:(fo + 2) * 128])
        last = None
        for sub in range(2):
            ft = fo + sub
            psum = (psA, psB)[ft % 2]
            if ft >= 2:
                te.wait_ge(P.sems["cV"], pendp[ft - 2])
            for ch in range(4):
                ps = psum[:, ch * 512:(ch + 1) * 512]
                for j in range(8):
                    last = te.matmul(ps, slot[:, j, sub * 128:(sub + 1) * 128],
                                     BK[:, j, ch * 512:(ch + 1) * 512],
                                     start=(j == 0), stop=False)
                    P.inc(last, "cT")
                last = te.matmul(ps, bsl[:, sub * 128:(sub + 1) * 128],
                                 lora1T[:, ch * 512:(ch + 1) * 512],
                                 start=False, stop=True)
                P.inc(last, "cT")
            pendp[ft] = evict_proj(ft, psum)
        slot_done(last)
    te.wait_ge(P.sems["cV"], pendp[NCT - 1])
    ve.wait_ge(P.sems["cV"], pendp[NCT - 1])
    if debug_stage == "proj0":
        return finish()

    P.wait(gp, "cT")
    P.wait(gp, "cV")
    P.wait(gp, "cS")
    P.wait(sy, "cT")
    rs_free("VR")
    normC = rs_alloc("normC", [8, T], BF16)
    combT = rs_alloc("combT", [8, T], BF16)
    xn_guard_zero()
    layernorm()
    if debug_stage == "ln1":
        return finish()

    P.wait(te, "cV")
    for tt in range(NT):
        ps = psA[:, (tt % 4) * 512:(tt % 4) * 512 + 8]
        for cc in range(NCT):
            P.inc(te.matmul(ps, xn(cc, tt * 128, (tt + 1) * 128), gateT[:, cc, :],
                            start=(cc == 0), stop=(cc == NCT - 1)), "cT")
        P.wait(ve, "cT")
        P.inc(ve.tensor_copy(out=LA[:, tt, :], in_=ps), "cV")
        P.wait(te, "cV")
    P.inc(ve.tensor_tensor(out=GB[:, :, 0:4], in0=LA[:, :, 0:4], in1=LA[:, :, 4:8],
                           op=ALU.max), "cV")
    P.inc(ve.tensor_tensor(out=GB[:, :, 4:8], in0=GB[:, :, 0:4], in1=GB[:, :, 2:6],
                           op=ALU.max), "cV")
    P.inc(ve.tensor_tensor(out=GB[:, :, 8:12], in0=GB[:, :, 4:8], in1=GB[:, :, 5:9],
                           op=ALU.max), "cV")
    P.inc(ve.tensor_scalar(out=GB[:, :, 4:8], in0=GB[:, :, 8:12], scalar1=-1.0,
                           scalar2=None, op0=ALU.mult), "cV")
    P.wait(sy, "cV")
    P.inc(sy.dma_start(out=GL, in_=GB), "sMV", 16)
    P.wait(ve, "sMV")
    for tt in range(NT):
        P.inc(ve.tensor_scalar(out=MA[:, tt, :], in0=LA[:, tt, :],
                               scalar1=GL[:, tt, 8:9], scalar2=None, op0=ALU.is_ge), "cV")
    P.inc(ve.tensor_scalar(out=EA[:], in0=MA[:], scalar1=1e5, scalar2=None,
                           op0=ALU.mult), "cV")
    P.inc(ve.tensor_tensor(out=EA[:], in0=LA[:], in1=EA[:], op=ALU.subtract), "cV")
    P.inc(ve.tensor_tensor(out=MB[:, :, 0:4], in0=EA[:, :, 0:4], in1=EA[:, :, 4:8],
                           op=ALU.max), "cV")
    P.inc(ve.tensor_tensor(out=MB[:, :, 4:8], in0=MB[:, :, 0:4], in1=MB[:, :, 2:6],
                           op=ALU.max), "cV")
    P.inc(ve.tensor_tensor(out=MB[:, :, 8:12], in0=MB[:, :, 4:8], in1=MB[:, :, 5:9],
                           op=ALU.max), "cV")
    P.wait(sy, "cV")
    P.inc(sy.dma_start(out=ML[:], in_=MB[:]), "sMV", 16)
    P.wait(ve, "sMV")
    P.wait(sc, "sMV")
    for tt in range(NT):
        P.inc(ve.tensor_scalar(out=MA[:, tt, :], in0=LA[:, tt, :],
                               scalar1=ML[:, tt, 8:9], scalar2=None, op0=ALU.is_ge), "cV")
        P.inc(sc.activation(out=EA[:, tt, :], in_=LA[:, tt, :], func=AF.Exp,
                            bias=GL[:, tt, 4:5]), "cS")
    P.wait(ve, "cS")
    P.inc(ve.tensor_tensor(out=EA[:], in0=EA[:], in1=MA[:], op=ALU.mult), "cV")
    P.wait(te, "cV")
    for tt in range(NT):
        ps = psB[0:8, (tt % 4) * 512:(tt % 4) * 512 + 128]
        P.inc(te.transpose(ps, EA[:, tt, :], ident[:]), "cT")
        P.wait(ve, "cT")
        P.inc(ve.tensor_copy(out=combT[:, tt * 128:(tt + 1) * 128], in_=ps), "cV")
        P.wait(te, "cV")
    for ch in range(4):
        P.inc(te.matmul(psA[0:64, ch * 512:(ch + 1) * 512], onesA[0:8, 0:64],
                        combT[:, ch * 512:(ch + 1) * 512], start=True, stop=True), "cT")
    P.wait(ve, "cT")
    P.inc(ve.reciprocal(out=psB[0:64, 0:T], in_=psA[0:64, 0:T]), "cV")
    P.inc(ve.tensor_tensor(out=normC[:], in0=combT[:], in1=psB[0:8, 0:T],
                           op=ALU.mult), "cV")
    P.wait(te, "cV")
    if debug_stage == "gate":
        return finish()
    rs_free("combT")

    def evict_shared_h(ft, psum):
        P.wait(sc, "cT")
        c = P.inc(sc.activation(out=BK[:, ft, :], in_=psum[:, 0:T], func=AF.Silu,
                                bias=shupb[:, ft:ft + 1]), "cS")
        return ("cS", c)
    mm_1024(evict_shared_h, shup_h, lambda cc, a, b: xn(cc, a, b))

    def evict_shared_d(ft, psum):
        P.wait(sc, "cT")
        P.inc(sc.activation(out=sStage[:, 0:T], in_=psum[:, 0:T], func=AF.Identity,
                            bias=shdwb[:, ft:ft + 1]), "cS")
        P.wait(ve, "cS")
        c = P.inc(ve.tensor_tensor(out=XT[:, ft, :], in0=XT[:, ft, :], in1=sStage[:, 0:T],
                                   op=ALU.add), "cV")
        return ("cV", c)
    mm_1024(evict_shared_d, shdw_h, lambda cc, a, b: BK[:, cc, a:b])
    P.wait(sy, "cT")
    P.inc(sy.dma_start(out=esel, in_=esel_h), "sIN", 16)
    P.wait(te, "sIN")
    if debug_stage == "shared":
        return finish()

    for e in range(8):
        P.wait(te, "cV")
        for ch in range(4):
            P.inc(te.matmul(psB[:, ch * 512:(ch + 1) * 512], esel[:, e * 128:(e + 1) * 128],
                            normC[:, ch * 512:(ch + 1) * 512],
                            start=True, stop=True), "cT")
        P.wait(sc, "cT")
        P.inc(sc.copy(out=bcMoE, in_=psB[:, 0:T]), "cS")
        P.wait(ve, "cS")
        P.wait(te, "cS")

        def evict_up(ft, psum, e=e):
            P.wait(sc, "cT")
            P.inc(sc.activation(out=sStage[:, 0:T], in_=psum[:, 0:T], func=AF.Silu,
                                bias=upb[:, ft, e:e + 1]), "cS")
            P.wait(ve, "cS")
            c = P.inc(ve.tensor_tensor(out=BQ[:, ft, :], in0=sStage[:, 0:T], in1=bcMoE,
                                       op=ALU.mult), "cV")
            return ("cV", c)
        mm_1024(evict_up, upw_h[e], lambda cc, a, b: xn(cc, a, b))

        def evict_down(ft, psum, e=e):
            P.wait(ve, "cT")
            return P.inc(ve.tensor_tensor(out=XT[:, ft, :], in0=XT[:, ft, :],
                                          in1=psum[:, 0:T], op=ALU.add), "cV")

        pendd = {}
        for fo in range(0, NCT, 2):
            slot, bsl = load_w(wcols(dww_h[e], fo * 128, 256),
                               dwb_h[:, fo * 128:(fo + 2) * 128] if e == 0 else None)
            last = None
            for sub in range(2):
                ft = fo + sub
                psum = (psA, psB)[ft % 2]
                if ft >= 2:
                    te.wait_ge(P.sems["cV"], pendd[ft - 2])
                for ch in range(4):
                    ps = psum[:, ch * 512:(ch + 1) * 512]
                    for cc in range(NCT):
                        last = te.matmul(ps, slot[:, cc, sub * 128:(sub + 1) * 128],
                                         BQ[:, cc, ch * 512:(ch + 1) * 512],
                                         start=(cc == 0), stop=(e != 0 and cc == NCT - 1))
                        P.inc(last, "cT")
                    if e == 0:
                        last = te.matmul(ps, bsl[:, sub * 128:(sub + 1) * 128],
                                         normC[:, ch * 512:(ch + 1) * 512],
                                         start=False, stop=True)
                        P.inc(last, "cT")
                pendd[ft] = evict_down(ft, psum)
            slot_done(last)
        te.wait_ge(P.sems["cV"], pendd[NCT - 1])
    if debug_stage == "moe":
        return finish()
    rs_free("normC")

    def evict_qT(dst):
        def f(ft, psum):
            P.wait(ve, "cT")
            c = P.inc(ve.tensor_copy(out=dst[:, ft, :], in_=psum[:, 0:T]), "cV")
            return ("cV", c)
        return f
    mm_1024(evict_qT(BQ), wqT_h, lambda cc, a, b: xn(cc, a, b))
    mm_1024(evict_qT(BK), wkT_h, lambda cc, a, b: xn(cc, a, b))

    VR2 = rs_alloc("VR2", [128, 16, 1040], BF16)
    vst["V1"] = VR2
    vrf = VR2.rearrange("p a b -> p (a b)")
    cosv = vrf[:, 0:4 * T].rearrange("p (a t) -> p a t", t=T)
    sinv = vrf[:, 4 * T:8 * T].rearrange("p (a t) -> p a t", t=T)
    P.wait(gp, "cV")
    P.wait(gp, "cT")
    P.inc(gp.dma_start(out=cosv[:], in_=cosT_h), "sW", 16)
    P.inc(gp.dma_start(out=sinv[:], in_=sinT_h), "sW", 16)
    P.wait(ve, "sW")
    for buf in (BQ, BK):
        for i in range(4):
            qe, qo = buf[:, i, :], buf[:, i + 4, :]
            cs, sn = cosv[:, i, :], sinv[:, i, :]
            P.inc(ve.tensor_tensor(out=sStage[:, 0:T], in0=qo, in1=sn, op=ALU.mult), "cV")
            P.inc(ve.tensor_tensor(out=psA[:, 0:T], in0=qe, in1=cs, op=ALU.mult), "cV")
            P.inc(ve.tensor_tensor(out=psB[:, 0:T], in0=psA[:, 0:T], in1=sStage[:, 0:T],
                                   op=ALU.add), "cV")
            P.inc(ve.tensor_tensor(out=sStage[:, 0:T], in0=qe, in1=sn, op=ALU.mult), "cV")
            P.inc(ve.tensor_tensor(out=psA[:, 0:T], in0=qo, in1=cs, op=ALU.mult), "cV")
            P.inc(ve.tensor_tensor(out=qo, in0=psA[:, 0:T], in1=sStage[:, 0:T],
                                   op=ALU.subtract), "cV")
            P.inc(ve.tensor_copy(out=qe, in_=psB[:, 0:T]), "cV")
    P.wait(te, "cV")
    P.wait(gp, "cV")

    V1 = VR2
    for ci in range(4):
        slot, _b = load_w(wcols(wvT_h, ci * 256, 256))
        last = None
        pend = {}
        for tt in range(NT):
            ps = psA[:, (tt % 4) * 512:(tt % 4) * 512 + 256]
            if tt >= 4:
                te.wait_ge(P.sems["cV"], pend[tt - 4])
            for cc in range(NCT):
                last = te.matmul(ps, xn(cc, tt * 128, (tt + 1) * 128),
                                 slot[:, cc, 0:256], start=(cc == 0), stop=(cc == NCT - 1))
                P.inc(last, "cT")
            P.wait(ve, "cT")
            pend[tt] = P.inc(ve.tensor_copy(out=V1[:, tt, ci * 256:(ci + 1) * 256],
                                            in_=ps), "cV")
        te.wait_ge(P.sems["cV"], pend[NT - 1])
        slot_done(last)
    if debug_stage == "qkv1":
        return finish()

    P.wait(gp, "cT")
    P.wait(gp, "cV")
    P.wait(sy, "cT")
    BVflat = BV.rearrange("p a b -> p (a b)")
    m0 = BVflat[:, 7680:8576]
    P.inc(sy.dma_start(out=m0[:], in_=m0_h), "sIN", 16)
    P.wait(ve, "sIN")

    psO = psB.rearrange("p (a b) -> p a b", b=512)

    BVf32 = BVflat.bitcast(F32)
    a1inv32 = BVf32[:, 3328:3840]
    a1tmp = BVflat[:, 8576:9088]

    def expslot(kt):
        if kt < 4:
            return sStage[:, kt * 512:(kt + 1) * 512]
        return BVflat[:, (kt - 4) * 512:(kt - 3) * 512]

    for qc in range(4):
        q0 = qc * 512
        nk = 4 * qc + 4
        exp_gd = {}
        for kg in range(0, nk, 2):
            g = kg // 2
            gb = (g % 2) * 1024
            kn = min(2, nk - kg)
            if g >= 2:
                te.wait_ge(P.sems["cS"], exp_gd[g - 2])
            for ki in range(kn):
                kt = kg + ki
                for cc in range(NCT):
                    P.inc(te.matmul(psA[:, gb + ki * 512:gb + (ki + 1) * 512],
                                    BK[:, cc, kt * 128:(kt + 1) * 128],
                                    BQ[:, cc, q0:q0 + 512],
                                    start=(cc == 0), stop=(cc == NCT - 1)), "cT")
            P.wait(sc, "cT")
            for ki in range(kn):
                exp_gd[g] = P.inc(
                    sc.activation(out=expslot(kg + ki), in_=psA[:, gb + ki * 512:gb + (ki + 1) * 512],
                                  func=AF.Exp, bias=fconsts[:, 1:2], scale=0.03125), "cS")
            for ki in range(kn):
                kt = kg + ki
                r = kt - 4 * qc
                if 0 <= r <= 3:
                    ve.wait_ge(P.sems["cS"], exp_gd[g])
                    P.inc(ve.tensor_tensor(out=expslot(kt), in0=expslot(kt),
                                           in1=m0[:, 384 - 128 * r:384 - 128 * r + 512],
                                           op=ALU.mult), "cV")
        P.wait(te, "cV")
        P.wait(te, "cS")
        for kt in range(nk):
            for dt in range(4):
                P.inc(te.matmul(psO[:, dt, :], V1[:, kt, dt * 128:(dt + 1) * 128],
                                expslot(kt), start=(kt == 0), stop=(kt == nk - 1)), "cT")
            P.inc(te.matmul(psA[:, 1024:1536], onesA, expslot(kt),
                            start=(kt == 0), stop=(kt == nk - 1)), "cT")
        P.wait(ve, "cT")
        P.inc(ve.reciprocal(out=a1inv32, in_=psA[:, 1024:1536]), "cV")
        for dt in range(4):
            P.inc(ve.tensor_tensor(out=a1tmp, in0=psO[:, dt, :],
                                   in1=a1inv32, op=ALU.mult), "cV")
            P.inc(ve.tensor_tensor(out=XT[:, dt, q0:q0 + 512], in0=XT[:, dt, q0:q0 + 512],
                                   in1=a1tmp, op=ALU.add), "cV")
        P.wait(te, "cV")
        if debug_stage == "attn1q0" and qc == 0:
            return finish()
        for kt in range(nk):
            for dt in range(4):
                P.inc(te.matmul(psO[:, dt, :], V1[:, kt, (dt + 4) * 128:(dt + 5) * 128],
                                expslot(kt), start=(kt == 0), stop=(kt == nk - 1)), "cT")
        P.wait(ve, "cT")
        for dt in range(4):
            P.inc(ve.tensor_tensor(out=a1tmp, in0=psO[:, dt, :],
                                   in1=a1inv32, op=ALU.mult), "cV")
            P.inc(ve.tensor_tensor(out=XT[:, dt + 4, q0:q0 + 512],
                                   in0=XT[:, dt + 4, q0:q0 + 512],
                                   in1=a1tmp, op=ALU.add), "cV")
        P.wait(te, "cV")
        P.wait(sc, "cV")

    return finish()




_BUILD = {}


def _get_program(debug_stage=None):
    key = debug_stage
    if key not in _BUILD:
        _BUILD[key] = build_program(debug_stage)
    return _BUILD[key]


def kernel(**inputs):
    cst = _host_constants()
    H = _prep_inputs(inputs)
    nc = _get_program(None)
    x = np.asarray(inputs["L0_attn_x"], dtype=np.float32)
    base = {}
    base.update(H)
    base.update(cst)
    in_maps = []
    for c in range(8):
        m = dict(base)
        m["x"] = np.ascontiguousarray(x[c])
        in_maps.append(m)
    res = run_bass_kernel_spmd(nc, in_maps, core_ids=list(range(8)))
    out = np.stack([np.asarray(res.results[c]["out"]) for c in range(8)], axis=0)
    return out.astype(np.float32)


# version 18
# speedup vs baseline: 1.2036x; 1.0026x over previous
import numpy as np
import ml_dtypes
import contextlib
import concourse.bass as bass
import concourse.mybir as mybir
from concourse.bass_utils import run_bass_kernel_spmd

F32 = mybir.dt.float32
BF16 = mybir.dt.bfloat16
AF = mybir.ActivationFunctionType
ALU = mybir.AluOpType

T = 2048
C = 1024
NT = 16
NCT = 8
HD = 64
LORA_SCALE = 2.0
EXP_SHIFT0 = 6.0
EXP_SHIFT1 = 6.0
GCOL = 8
_bf = ml_dtypes.bfloat16
DBG_MIN = False



def _host_constants():
    cst = {}
    cst["ident_f32"] = np.eye(128, dtype=np.float32)
    cst["ones_1x128"] = np.ones((1, 128), dtype=_bf)
    cst["ones_8x1"] = np.ones((8, 16), dtype=_bf)
    cst["ones_c128"] = np.ones((128, 8), dtype=_bf)
    cst["zeros_guard"] = np.zeros((128, NCT, GCOL), dtype=_bf)
    cst["vones"] = np.ones((128, 16, 16), dtype=_bf)
    ps = np.zeros((16, 8 * 128), dtype=_bf)
    for j in range(8):
        ps[2 * j, j * 128:j * 128 + 64] = 1.0
        ps[2 * j + 1, j * 128 + 64:j * 128 + 128] = 1.0
    cst["pairsel"] = ps
    es = np.zeros((8, 1024), dtype=_bf)
    for e in range(8):
        es[e, e * 128:(e + 1) * 128] = 1.0
    cst["esel"] = es
    k = np.arange(128)[:, None]
    c0 = np.arange(1024)[None, :]
    cst["m0"] = (c0 >= k + 384).astype(_bf)[:, :896]
    i = np.arange(C // 2, dtype=np.float64)
    phi = np.power(10000.0, -2.0 * (i - 1.0) / C)
    ang = np.arange(T, dtype=np.float64)[None, :] * phi[:, None]
    cst["cosT"] = np.ascontiguousarray(
        np.cos(ang).astype(_bf).reshape(4, 128, T).transpose(1, 0, 2))
    cst["sinT"] = np.ascontiguousarray(
        np.sin(ang).astype(_bf).reshape(4, 128, T).transpose(1, 0, 2))
    sc = np.zeros((128, 8), dtype=np.float32)
    sc[:, 0] = -EXP_SHIFT0
    sc[:, 1] = -EXP_SHIFT1
    cst["fconsts"] = sc
    return cst


def _prep_inputs(I):
    H = {}
    A = np.ascontiguousarray
    dd, hh = np.meshgrid(np.arange(64), np.arange(16), indexing="ij")
    vperm = (64 * hh + dd).reshape(-1)
    W = I["L0_attn_W_qkv"]
    H["Wqkv"] = A(np.concatenate([W[:, :2048], W[:, 2048:][:, vperm]], axis=1))
    B = I["L0_attn_B_qkv"]
    H["B_qkv"] = A(np.concatenate([B[:, :2048], B[:, 2048:][:, vperm]], axis=1))
    H["b_qkv"] = A(I["L0_attn_b_qkv"][:2048].reshape(16, 128).T)
    H["b_v"] = A(I["L0_attn_b_qkv"][2048:].reshape(8, 128).T)
    H["A_qkv"] = A(I["L0_attn_A_qkv"].reshape(8, 128, 8).transpose(1, 0, 2))
    H["W_proj"] = A(I["L0_attn_W_proj"])
    H["b_proj"] = A(I["L0_attn_b_proj"].reshape(8, 128).T)
    H["A_proj"] = A(I["L0_attn_A_proj"].reshape(8, 128, 8).transpose(1, 0, 2))
    H["B_proj"] = A(I["L0_attn_B_proj"])
    H["WrT"] = A(I["L0_ffn_W_r"].T)
    H["WkT"] = A(I["L0_ffn_W_k"].T)
    H["WvT"] = A(I["L0_ffn_W_v"].T)
    H["mu_r"] = A(I["L0_ffn_mu_r"].reshape(8, 128).T)
    H["mu_k"] = A(I["L0_ffn_mu_k"].reshape(8, 128).T)
    perm = np.concatenate([np.arange(0, C, 2), np.arange(1, C, 2)])
    H["wqT"] = A(I["L1_attn_wq"][perm].T)
    H["wkT"] = A(I["L1_attn_wk"][perm].T)
    H["wvT"] = A(I["L1_attn_wv"].T)
    H["gate_wT"] = A(I["L1_ffn_gate_w"].T.reshape(8, 128, 8).transpose(1, 0, 2))
    H["up_w"] = A(I["L1_ffn_up_w"])
    H["up_b"] = A(I["L1_ffn_up_b"].reshape(8, 8, 128).transpose(2, 1, 0))
    H["down_w"] = A(I["L1_ffn_down_w"])
    H["down_b"] = A(I["L1_ffn_down_b"])
    H["sh_up_w"] = A(I["L1_ffn_sh_up_w"])
    H["sh_up_b"] = A(I["L1_ffn_sh_up_b"].reshape(8, 128).T)
    H["sh_down_w"] = A(I["L1_ffn_sh_down_w"])
    H["sh_down_b"] = A(I["L1_ffn_sh_down_b"].reshape(8, 128).T)
    for k2 in H:
        if H[k2].dtype != np.float32:
            H[k2] = H[k2].astype(np.float32)
    return H



class Prog:
    def __init__(self, debug_stage=None):
        self.debug_stage = debug_stage
        self.dbg = {}
        self.stopped = False

    def sem_init(self, nc):
        self.nc = nc
        self.cnt = {}
        self.sems = {}
        for s in ["sIN", "sW", "sMV", "sOUT", "cT", "cV", "cS"]:
            cm = nc.semaphore(s)
            self.sems[s] = self._stack.enter_context(cm)
            self.cnt[s] = 0

    def inc(self, instr, name, by=1):
        instr.then_inc(self.sems[name], by)
        self.cnt[name] += by
        return self.cnt[name]

    def wait(self, eng, name, target=None):
        eng.wait_ge(self.sems[name], self.cnt[name] if target is None else target)


def build_program(debug_stage=None):
    nc = bass.Bass("TRN2", target_bir_lowering=False,
                   detect_race_conditions=False)
    P = Prog(debug_stage)
    P._stack = contextlib.ExitStack()
    P.sem_init(nc)
    stack = P._stack

    din = {}
    def dram_in(name, shape, dtype=F32):
        din[name] = nc.dram_tensor(name, list(shape), dtype, kind="ExternalInput").ap()
        return din[name]

    x_h = dram_in("x", (T, C))
    Wqkv_h = dram_in("Wqkv", (C, 3 * C))
    bqkv_h = dram_in("b_qkv", (128, 16))
    bv_h = dram_in("b_v", (128, 8))
    Aqkv_h = dram_in("A_qkv", (128, 8, 8))
    Bqkv_h = dram_in("B_qkv", (8, 3 * C))
    Wproj_h = dram_in("W_proj", (C, C))
    bproj_h = dram_in("b_proj", (128, 8))
    Aproj_h = dram_in("A_proj", (128, 8, 8))
    Bproj_h = dram_in("B_proj", (8, C))
    WrT_h = dram_in("WrT", (C, C))
    WkT_h = dram_in("WkT", (C, C))
    WvT_h = dram_in("WvT", (C, C))
    mur_h = dram_in("mu_r", (128, 8))
    muk_h = dram_in("mu_k", (128, 8))
    wqT_h = dram_in("wqT", (C, C))
    wkT_h = dram_in("wkT", (C, C))
    wvT_h = dram_in("wvT", (C, C))
    gateT_h = dram_in("gate_wT", (128, 8, 8))
    upw_h = dram_in("up_w", (8, C, C))
    upb_h = dram_in("up_b", (128, 8, 8))
    dww_h = dram_in("down_w", (8, C, C))
    dwb_h = dram_in("down_b", (8, C))
    shup_h = dram_in("sh_up_w", (C, C))
    shupb_h = dram_in("sh_up_b", (128, 8))
    shdw_h = dram_in("sh_down_w", (C, C))
    shdwb_h = dram_in("sh_down_b", (128, 8))
    ident_h = dram_in("ident_f32", (128, 128))
    ones1_h = dram_in("ones_1x128", (1, 128), BF16)
    ones8_h = dram_in("ones_8x1", (8, 16), BF16)
    onesc_h = dram_in("ones_c128", (128, 8), BF16)
    zguard_h = dram_in("zeros_guard", (128, NCT, GCOL), BF16)
    vones_h = dram_in("vones", (128, 16, 16), BF16)
    pairsel_h = dram_in("pairsel", (16, 1024), BF16)
    esel_h = dram_in("esel", (8, 1024), BF16)
    m0_h = dram_in("m0", (128, 896), BF16)
    cosT_h = dram_in("cosT", (128, 4, T), BF16)
    sinT_h = dram_in("sinT", (128, 4, T), BF16)
    fconsts_h = dram_in("fconsts", (128, 8))
    out_h = nc.dram_tensor("out", [T, C], F32, kind="ExternalOutput").ap()
    dbg_h = {}
    if debug_stage is not None:
        for nm, shp, dt in [("dbg_xt", [128, NCT, T], F32),
                            ("dbg_xn", [128, NCT, GCOL + T], BF16),
                            ("dbg_bq", [128, NCT, T], BF16),
                            ("dbg_bk", [128, NCT, T], BF16),
                            ("dbg_bv", [128, 16, 1040], BF16),
                            ("dbg_sall", [16, T], BF16),
                            ("dbg_comb", [8, T], BF16),
                            ("dbg_normc", [8, T], BF16),
                            ("dbg_la", [128, 16, 8], F32),
                            ("dbg_ea", [128, 16, 8], F32)]:
            dbg_h[nm] = nc.dram_tensor(nm, shp, dt, kind="ExternalOutput").ap()

    S = lambda *a: stack.enter_context(nc.sbuf_tensor(*a))

    XT = S("XT", [128, NCT, T], F32)
    BQ = S("BQ", [128, NCT, T], BF16)
    BK = S("BK", [128, NCT, T], BF16)
    BV = S("BV", [128, 16, 1040], BF16)
    ST = [S("ST0", [128, NCT, 256], BF16), S("ST1", [128, NCT, 256], BF16)]
    BqS = [S("BqS0", [8, 256], BF16), S("BqS1", [8, 256], BF16)]
    XNv = BV.rearrange("p a b -> p (a b)")[:, 0:NCT * (GCOL + T)] \
            .rearrange("p (a b) -> p a b", b=GCOL + T)
    xn_state = {"XN": XNv, "cm": None}

    def xn_guard_zero():
        P.inc(sy.dma_start(out=XNv[:, :, 0:GCOL], in_=zguard_h), "sIN", 16)
    BVsp = BV.rearrange("p a b -> p (a b)")
    onesA = BVsp[:, 16448:16576]
    ident = S("identb", [128, 128], F32)
    ones1 = S("ones1b", [1, 128], BF16)
    ones8 = S("ones8b", [8, 16], BF16)
    onesc = S("onescb", [128, 8], BF16)
    fconsts = S("fconstsb", [128, 8], F32)
    bqkv = S("bqkvb", [128, 16], F32)
    bv = S("bvb", [128, 8], F32)
    bproj = S("bprojb", [128, 8], F32)
    mur = S("murb", [128, 8], F32)
    muk = S("mukb", [128, 8], F32)
    upb = S("upbb", [128, 8, 8], F32)
    shupb = S("shupbb", [128, 8], F32)
    shdwb = S("shdwbb", [128, 8], F32)
    Aqkv = S("Aqkvb", [128, 8, 8], BF16)
    Aproj = S("Aprojb", [128, 8, 8], BF16)
    gateT = S("gateTb", [128, 8, 8], BF16)

    RS = {}

    def rs_alloc(name, shape, dtype):
        cm = nc.sbuf_tensor(name, shape, dtype, side="right")
        RS[name] = (cm, cm.__enter__())
        return RS[name][1]

    def rs_free(name):
        cm, _ = RS.pop(name)
        cm.__exit__(None, None, None)

    BQf32 = BQ.bitcast(F32)
    BQrow = BQf32.rearrange("p a b -> p (a b)")
    rowsA = BQrow[0:1, 0:T]
    rowsB = BQrow[0:1, T:2 * T]
    rowsE = BQrow[0:1, 2 * T:3 * T]
    BKflat = BK.rearrange("p a b -> p (a b)")
    rowsC = BKflat[0:1, 0:T]
    rowsD = BKflat[0:1, T:2 * T]
    BKf32 = BK.bitcast(F32).rearrange("p a b -> p (a b)")
    invden = BKf32[0:1, 0:T]
    a_invdenC = BKflat[0:1, 2 * T:3 * T]
    BQflat32 = BQf32.rearrange("p a b -> p (a b)")
    outStage = [BQflat32[:, 0:1024], BQflat32[:, 2048:3072],
                BQflat32[:, 4096:5120], BQflat32[:, 6144:7168]]
    bcMoE = BKflat[:, 0:T]
    esel = BKflat[0:8, 3 * T:3 * T + 1024]
    GB = BQflat32[:, 1024:1216].rearrange("p (a b) -> p a b", b=12)
    MB = GB
    LA = BQflat32[:, 1216:1344].rearrange("p (a b) -> p a b", b=8)
    EA = BQflat32[:, 1344:1472].rearrange("p (a b) -> p a b", b=8)
    MA = BQflat32[:, 1472:1600].rearrange("p (a b) -> p a b", b=8)
    GL = BQflat32[:, 1600:1792].rearrange("p (a b) -> p a b", b=12)
    ML = BQflat32[:, 1792:1984].rearrange("p (a b) -> p a b", b=12)
    psA = stack.enter_context(nc.psum_tensor("psA", [128, T], F32))
    psB = stack.enter_context(nc.psum_tensor("psB", [128, T], F32))

    sStage = rs_alloc("sStage", [128, T], BF16)

    nc.gpsimd.dma_reset(range(155, 256))
    nc.gpsimd.sem_clear(range(155, 256))
    nc.all_engine_barrier()

    sy, gp, te, ve, sc = nc.sync, nc.gpsimd, nc.tensor, nc.vector, nc.scalar

    P.inc(sy.dma_start(out=onesA,
                       in_=vones_h.rearrange("p a b -> p (a b)")[:, 0:128]), "sIN", 16)
    for dst, src in [(ident, ident_h), (ones1, ones1_h), (ones8, ones8_h),
                     (onesc, onesc_h), (fconsts, fconsts_h), (bqkv, bqkv_h),
                     (bv, bv_h), (bproj, bproj_h), (mur, mur_h), (muk, muk_h),
                     (upb, upb_h), (shupb, shupb_h), (shdwb, shdwb_h)]:
        P.inc(sy.dma_start(out=dst[:], in_=src), "sIN", 16)
    for dst, src in [(Aqkv, Aqkv_h), (Aproj, Aproj_h), (gateT, gateT_h)]:
        P.inc(gp.dma_start(out=dst[:], in_=src), "sW", 16)

    wstate = {"i": 0, "free": [0, 0]}

    def load_w(src_ap, b_src=None, width=256):
        i = wstate["i"] % 2
        wstate["i"] += 1
        gp.wait_ge(P.sems["cT"], wstate["free"][i])
        P.inc(gp.dma_start(out=ST[i][:, :, 0:width], in_=src_ap), "sW", 16)
        if b_src is not None:
            P.inc(gp.dma_start(out=BqS[i][:, 0:width], in_=b_src), "sW", 16)
        te.wait_ge(P.sems["sW"], P.cnt["sW"])
        return (ST[i], BqS[i])

    def slot_done(instr):
        i = (wstate["i"] - 1) % 2
        wstate["free"][i] = P.cnt["cT"]

    def wcols(h, f0, w):
        return h.rearrange("(j p) f -> p j f", p=128)[:, :, f0:f0 + w]

    def xn(cc, t0=0, t1=T):
        return xn_state["XN"][:, cc, GCOL + t0:GCOL + t1]

    def finish():
        P.wait(te, "cV")
        P.wait(te, "cS")
        P.wait(te, "sMV")
        if debug_stage is None:
            dma_done = {}
            pendf = {}
            fj = 0
            for tt in range(NT):
                ob = outStage[tt % 4]
                dq, dsem = sy, "sOUT"
                if tt >= 4:
                    ve.wait_ge(P.sems[dsem], dma_done[tt - 4])
                for cc in range(NCT):
                    pslot = psB[:, (fj % 4) * 512:(fj % 4) * 512 + 128]
                    if fj >= 4:
                        te.wait_ge(P.sems["cV"], pendf[fj - 4])
                    P.inc(te.transpose(pslot, XT[:, cc, tt * 128:(tt + 1) * 128],
                                       ident[:]), "cT")
                    P.wait(ve, "cT")
                    pendf[fj] = P.inc(ve.tensor_copy(out=ob[:, cc * 128:(cc + 1) * 128],
                                                     in_=pslot), "cV")
                    fj += 1
                P.wait(dq, "cV")
                dma_done[tt] = P.inc(dq.dma_start(out=out_h[tt * 128:(tt + 1) * 128, :],
                                                  in_=ob), dsem, 16)
            P.wait(sy, "sOUT")
        else:
            P.wait(sy, "cV")
            P.wait(sy, "cS")
            P.wait(sy, "cT")
            pairs = [("dbg_xt", XT), ("dbg_bq", BQ),
                     ("dbg_bk", BK),
                     ("dbg_la", LA), ("dbg_ea", EA)]
            if DBG_MIN:
                pairs = [("dbg_xt", XT)]
            if "combT" in RS and not DBG_MIN:
                pairs.append(("dbg_comb", RS["combT"][1]))
            if "normC" in RS and not DBG_MIN:
                pairs.append(("dbg_normc", RS["normC"][1]))
            if "VR2" in RS and not DBG_MIN:
                pairs.append(("dbg_bv", RS["VR2"][1]))
            elif "VR" in RS:
                pairs.append(("dbg_bv", RS["VR"][1]))
            if not DBG_MIN:
                pairs.append(("dbg_xn", xn_state["XN"]))
            for nm, sb in pairs:
                P.inc(sy.dma_start(out=dbg_h[nm], in_=sb[:]), "sOUT", 16)
            P.wait(sy, "sOUT")
        stack.close()
        return nc

    xs0 = BQ.bitcast(F32)
    xs1 = BK.bitcast(F32)
    P.inc(sy.dma_start(out=xs0[:], in_=x_h[0:1024].rearrange("(a p) c -> p a c", p=128)), "sIN", 16)
    P.inc(sy.dma_start(out=xs1[:], in_=x_h[1024:2048].rearrange("(a p) c -> p a c", p=128)), "sIN", 16)
    P.wait(te, "sIN")
    pendx = {}
    ti = 0
    for tt in range(NT):
        src = xs0 if tt < 8 else xs1
        a = tt % 8
        for cc in range(NCT):
            pslot = psA[:, (ti % 4) * 512:(ti % 4) * 512 + 128]
            if ti >= 4:
                te.wait_ge(P.sems["cV"], pendx[ti - 4])
            P.inc(te.transpose(pslot, src[:, a, cc * 128:(cc + 1) * 128], ident[:]), "cT")
            P.wait(ve, "cT")
            pendx[ti] = P.inc(ve.tensor_copy(out=XT[:, cc, tt * 128:(tt + 1) * 128],
                                             in_=pslot), "cV")
            ti += 1
    te.wait_ge(P.sems["cV"], pendx[ti - 1])

    def layernorm():
        P.wait(sc, "cV")
        for cc in range(NCT):
            P.inc(sc.activation(out=BK[:, cc, :], in_=XT[:, cc, :], func=AF.Copy), "cS")
            P.inc(sc.activation(out=BQ[:, cc, :], in_=XT[:, cc, :], func=AF.Square), "cS")
        P.wait(te, "cS")
        for ch in range(4):
            for cc in range(NCT):
                P.inc(te.matmul(psA[:, ch * 512:(ch + 1) * 512], onesA,
                                BK[:, cc, ch * 512:(ch + 1) * 512],
                                start=(cc == 0), stop=(cc == NCT - 1)), "cT")
            for cc in range(NCT):
                P.inc(te.matmul(psB[:, ch * 512:(ch + 1) * 512], onesA,
                                BQ[:, cc, ch * 512:(ch + 1) * 512],
                                start=(cc == 0), stop=(cc == NCT - 1)), "cT")
        tmpm2 = BQf32.rearrange("p a b -> p (a b)")[:, 0:T]
        P.wait(ve, "cT")
        P.inc(ve.tensor_scalar(out=psA[:, 0:T], in0=psA[:, 0:T], scalar1=1.0 / C,
                               scalar2=None, op0=ALU.mult), "cV")
        P.inc(ve.tensor_scalar(out=psB[:, 0:T], in0=psB[:, 0:T], scalar1=1.0 / C,
                               scalar2=None, op0=ALU.mult), "cV")
        P.wait(sc, "cV")
        P.inc(sc.activation(out=tmpm2, in_=psA[:, 0:T], func=AF.Square), "cS")
        P.wait(ve, "cS")
        P.inc(ve.tensor_tensor(out=psB[:, 0:T], in0=psB[:, 0:T], in1=tmpm2,
                               op=ALU.subtract), "cV")
        P.inc(ve.tensor_scalar(out=psB[:, 0:T], in0=psB[:, 0:T], scalar1=1e-5,
                               scalar2=None, op0=ALU.add), "cV")
        P.inc(ve.reciprocal(out=psB[:, 0:T], in_=psB[:, 0:T]), "cV")
        P.wait(sc, "cV")
        P.inc(sc.activation(out=psB[:, 0:T], in_=psB[:, 0:T], func=AF.Sqrt), "cS")
        P.wait(ve, "cS")
        for cc in range(NCT):
            P.inc(ve.tensor_tensor(out=BK[:, cc, :], in0=XT[:, cc, :], in1=psA[:, 0:T],
                                   op=ALU.subtract), "cV")
        for cc in range(NCT):
            P.inc(ve.tensor_tensor(out=xn(cc), in0=BK[:, cc, :], in1=psB[:, 0:T],
                                   op=ALU.mult), "cV")
        P.wait(te, "cV")
        P.wait(sc, "cV")

    xn_guard_zero()
    layernorm()
    if debug_stage == "ln0":
        return finish()

    lora1T = sStage[0:8, 0:T]

    def compute_lora1T(A_ap):
        P.wait(te, "sW")
        P.wait(te, "cV")
        P.wait(te, "cS")
        for ch in range(4):
            for cc in range(NCT):
                P.inc(te.matmul(psB[0:8, ch * 512:(ch + 1) * 512], A_ap[:, cc, :],
                                xn(cc, ch * 512, (ch + 1) * 512),
                                start=(cc == 0), stop=(cc == NCT - 1)), "cT")
        P.wait(ve, "cT")
        P.inc(ve.tensor_scalar(out=lora1T, in0=psB[0:8, 0:T], scalar1=LORA_SCALE,
                               scalar2=None, op0=ALU.mult), "cV")
        P.wait(te, "cV")

    ffn_state = {}

    def build_shift(mu_ap):
        P.wait(ve, "cS")
        for cc in range(NCT):
            P.inc(ve.tensor_tensor(out=sStage[:, 0:T], in0=xn(cc),
                                   in1=xn_state["XN"][:, cc, GCOL - 1:GCOL - 1 + T],
                                   op=ALU.subtract), "cV")
            P.inc(ve.tensor_scalar(out=sStage[:, 0:T], in0=sStage[:, 0:T],
                                   scalar1=mu_ap[:, cc:cc + 1], scalar2=None,
                                   op0=ALU.mult), "cV")
            P.inc(ve.tensor_tensor(out=BQ[:, cc, :], in0=sStage[:, 0:T],
                                   in1=xn_state["XN"][:, cc, GCOL - 1:GCOL - 1 + T],
                                   op=ALU.add), "cV")

    def mm_1024(dst_cb, w_h, rhs_cb):
        pend = {}
        for fo in range(0, NCT, 2):
            slot, _b = load_w(wcols(w_h, fo * 128, 256))
            last = None
            for sub in range(2):
                ft = fo + sub
                psum = (psA, psB)[ft % 2]
                if ft >= 2:
                    sem, cnt = pend[ft - 2]
                    te.wait_ge(P.sems[sem], cnt)
                for ch in range(4):
                    ps = psum[:, ch * 512:(ch + 1) * 512]
                    for cc in range(NCT):
                        last = te.matmul(ps, slot[:, cc, sub * 128:(sub + 1) * 128],
                                         rhs_cb(cc, ch * 512, (ch + 1) * 512),
                                         start=(cc == 0), stop=(cc == NCT - 1))
                        P.inc(last, "cT")
                pend[ft] = dst_cb(ft, psum)
            slot_done(last)
        for ft in (NCT - 2, NCT - 1):
            sem, cnt = pend[ft]
            te.wait_ge(P.sems[sem], cnt)

    def ffn0():
        kact = rs_alloc("kactR", [128, NCT, T], BF16)
        ffn_state["kact"] = kact
        build_shift(mur)
        P.wait(te, "cV")

        def evict_r(ft, psum):
            P.wait(sc, "cT")
            c = P.inc(sc.activation(out=BK[:, ft, :], in_=psum[:, 0:T], func=AF.Sigmoid), "cS")
            return ("cS", c)
        mm_1024(evict_r, WrT_h, lambda cc, a, b: BQ[:, cc, a:b])

        build_shift(muk)
        P.wait(te, "cV")

        def evict_k(ft, psum):
            P.wait(sc, "cT")
            kact = ffn_state["kact"]
            P.inc(sc.activation(out=kact[:, ft, 0:T], in_=psum[:, 0:T], func=AF.Relu), "cS")
            P.wait(ve, "cS")
            c = P.inc(ve.tensor_tensor(out=kact[:, ft, 0:T], in0=kact[:, ft, 0:T],
                                       in1=kact[:, ft, 0:T], op=ALU.mult), "cV")
            return ("cV", c)
        mm_1024(evict_k, WkT_h, lambda cc, a, b: BQ[:, cc, a:b])

        def evict_kv(ft, psum):
            P.wait(ve, "cT")
            tmp = sStage[:, 0:T]
            P.inc(ve.tensor_tensor(out=tmp, in0=psum[:, 0:T], in1=BK[:, ft, :],
                                   op=ALU.mult), "cV")
            c = P.inc(ve.tensor_tensor(out=XT[:, ft, :], in0=XT[:, ft, :], in1=tmp,
                                       op=ALU.add), "cV")
            return ("cV", c)
        mm_1024(evict_kv, WvT_h, lambda cc, a, b: ffn_state["kact"][:, cc, a:b])
        P.wait(gp, "cT")
        P.wait(gp, "cV")
        rs_free("kactR")

    ffn0()
    if debug_stage == "ffn0":
        return finish()

    vst = {}

    def qkv_half(half):
        pend = {}
        fi = 0
        for sec, tb in [(0, 0), (1, 4)]:
            fbase = sec * C + half * 512
            for fo in range(0, 4, 2):
                slot, bsl = load_w(wcols(Wqkv_h, fbase + fo * 128, 256),
                                   Bqkv_h[:, fbase + fo * 128:fbase + (fo + 2) * 128])
                last = None
                for sub in range(2):
                    ft = fo + sub
                    psum = (psA, psB)[fi % 2]
                    if fi >= 2:
                        te.wait_ge(P.sems["cS"], pend[fi - 2])
                    for ch in range(4):
                        ps = psum[:, ch * 512:(ch + 1) * 512]
                        for cc in range(NCT):
                            last = te.matmul(ps, slot[:, cc, sub * 128:(sub + 1) * 128],
                                             xn(cc, ch * 512, (ch + 1) * 512),
                                             start=(cc == 0), stop=False)
                            P.inc(last, "cT")
                        last = te.matmul(ps, bsl[:, sub * 128:(sub + 1) * 128],
                                         lora1T[:, ch * 512:(ch + 1) * 512],
                                         start=False, stop=True)
                        P.inc(last, "cT")
                    P.wait(sc, "cT")
                    bcol = half * 4 + ft + sec * 8
                    pend[fi] = P.inc(sc.activation(out=BQ[:, tb + ft, :],
                                                   in_=psum[:, 0:T], func=AF.Identity,
                                                   bias=bqkv[:, bcol:bcol + 1]), "cS")
                    fi += 1
                slot_done(last)
        sc.wait_ge(P.sems["cS"], pend[fi - 1])
        te.wait_ge(P.sems["cS"], pend[fi - 1])

    def build_v():
        V1 = rs_alloc("VR", [128, 16, 1040], BF16)
        vst["V1"] = V1
        P.wait(ve, "cT")
        P.inc(ve.memset(V1[:, :, 1024:1040], 1.0), "cV")
        P.wait(te, "cV")
        for ci in range(4):
            slot, bsl = load_w(wcols(Wqkv_h, 2048 + ci * 256, 256),
                               Bqkv_h[:, 2048 + ci * 256:2048 + (ci + 1) * 256])
            last = None
            pend = {}
            for tt in range(NT):
                ps = psA[:, (tt % 4) * 512:(tt % 4) * 512 + 256]
                if tt >= 4:
                    te.wait_ge(P.sems["cV"], pend[tt - 4])
                for cc in range(NCT):
                    last = te.matmul(ps, xn(cc, tt * 128, (tt + 1) * 128),
                                     slot[:, cc, 0:256],
                                     start=(cc == 0), stop=False)
                    P.inc(last, "cT")
                last = te.matmul(ps, lora1T[:, tt * 128:(tt + 1) * 128],
                                 bsl[:, 0:256], start=False, stop=True)
                P.inc(last, "cT")
                P.wait(ve, "cT")
                pend[tt] = P.inc(ve.tensor_copy(out=V1[:, tt, ci * 256:(ci + 1) * 256],
                                                in_=ps), "cV")
            te.wait_ge(P.sems["cV"], pend[NT - 1])
            slot_done(last)
        P.wait(te, "sMV")

    def attn_half(half):
        V1 = vst["V1"]
        ST1f32 = ST[1].rearrange("p a b -> p (a b)").bitcast(F32)
        ebuf = [[sStage[:, 0:512], sStage[:, 512:1024]],
                [sStage[:, 1024:1536], sStage[:, 1536:2048]]]
        gate = {"pvprev": (P.cnt["cT"], P.cnt["cT"]),
                "expprev": (P.cnt["cS"], P.cnt["cS"]),
                "tt": [P.cnt["cV"], P.cnt["cV"]]}
        pp = 0
        for p in range(4):
            tile = p
            gpair = half * 4 + p
            he = half * 8 + 2 * p
            for qq in range(4):
                q0 = qq * 512
                SB = (pp % 2) * 1024
                exp_done = {}
                pv_done = {}
                sv = {}

                def emit_S(kt):
                    par = kt % 2
                    if kt >= 2:
                        te.wait_ge(P.sems["cS"], exp_done[(kt - 2, 1)])
                    else:
                        te.wait_ge(P.sems["cS"], gate["expprev"][kt])
                    for eo in range(2):
                        P.inc(te.matmul(
                            psA[:, eo * 1024 + par * 512:eo * 1024 + (par + 1) * 512],
                            BQ[64 * eo:64 * eo + 64, 4 + tile, kt * 128:(kt + 1) * 128],
                            BQ[64 * eo:64 * eo + 64, tile, q0:q0 + 512],
                            start=True, stop=True,
                            tile_position=(64 * eo, 0)), "cT")
                    sv[kt] = P.cnt["cT"]

                emit_S(0)
                emit_S(1)
                for kt in range(NT):
                    par = kt % 2
                    for eo in range(2):
                        if eo == 0:
                            sc.wait_ge(P.sems["cT"], sv[kt])
                        if kt >= 2:
                            sc.wait_ge(P.sems["cT"], pv_done[(kt - 2, eo)])
                        elif eo == 0:
                            sc.wait_ge(P.sems["cT"], gate["pvprev"][kt])
                        exp_done[(kt, eo)] = P.inc(
                            sc.activation(out=ebuf[par][eo],
                                          in_=psA[:, eo * 1024 + par * 512:eo * 1024 + (par + 1) * 512],
                                          func=AF.Exp, bias=fconsts[:, 0:1],
                                          scale=0.125), "cS")
                    vt = V1[:, kt, 0:1040].rearrange("p (d h) -> p d h", h=16)
                    for eo in range(2):
                        te.wait_ge(P.sems["cS"], exp_done[(kt, eo)])
                        if kt == 0 and eo == 0:
                            te.wait_ge(P.sems["cV"], gate["tt"][pp % 2])
                        r0 = 64 * eo
                        P.inc(te.matmul(psB[r0:r0 + 64, SB:SB + 512],
                                        vt[:, 0:64, he + eo],
                                        ebuf[par][eo],
                                        start=(kt == 0), stop=(kt == NT - 1),
                                        tile_position=(0, r0),
                                        skip_group_check=True), "cT")
                        P.inc(te.matmul(psB[r0:r0 + 64, SB + 512:SB + 1024],
                                        onesA[:, 0:64], ebuf[par][eo],
                                        start=(kt == 0), stop=(kt == NT - 1),
                                        tile_position=(0, r0),
                                        skip_group_check=True), "cT")
                        pv_done[(kt, eo)] = P.cnt["cT"]
                        if eo == 0 and kt + 2 < NT:
                            emit_S(kt + 2)
                gate["pvprev"] = (pv_done[(NT - 2, 1)], pv_done[(NT - 1, 1)])
                gate["expprev"] = (exp_done[(NT - 2, 1)], exp_done[(NT - 1, 1)])
                P.wait(ve, "cT")
                P.inc(ve.reciprocal(out=ST1f32[:, 0:512],
                                    in_=psB[:, SB + 512:SB + 1024]), "cV")
                P.inc(ve.tensor_tensor(out=BK[:, gpair, q0:q0 + 512],
                                       in0=psB[:, SB:SB + 512],
                                       in1=ST1f32[:, 0:512], op=ALU.mult), "cV")
                gate["tt"][pp % 2] = P.cnt["cV"]
                P.inc(ve.tensor_scalar(out=BK[:, gpair, q0:q0 + 512],
                                       in0=BK[:, gpair, q0:q0 + 512],
                                       scalar1=bv[:, gpair:gpair + 1],
                                       scalar2=None, op0=ALU.add), "cV")
                pp += 1
        P.wait(gp, "cV")
        wstate["free"] = [P.cnt["cT"], P.cnt["cT"]]

    compute_lora1T(Aqkv)
    qkv_half(0)
    if debug_stage == "qkv0":
        return finish()
    build_v()
    if debug_stage == "v0":
        return finish()
    attn_half(0)
    compute_lora1T(Aqkv)
    qkv_half(1)
    attn_half(1)
    if debug_stage == "attn0":
        return finish()

    P.wait(te, "cV")
    P.wait(te, "cS")
    for ch in range(4):
        for j in range(8):
            P.inc(te.matmul(psA[0:8, ch * 512:(ch + 1) * 512], Aproj[:, j, :],
                            BK[:, j, ch * 512:(ch + 1) * 512],
                            start=(j == 0), stop=(j == 7)), "cT")
    P.wait(ve, "cT")
    P.inc(ve.tensor_scalar(out=lora1T, in0=psA[0:8, 0:T], scalar1=LORA_SCALE,
                           scalar2=None, op0=ALU.mult), "cV")
    P.wait(te, "cV")
    tmpP = BV.rearrange("p a b -> p (a b)")[:, 0:T]

    def evict_proj(ft, psum):
        P.wait(sc, "cT")
        P.inc(sc.activation(out=tmpP, in_=psum[:, 0:T], func=AF.Identity,
                            bias=bproj[:, ft:ft + 1]), "cS")
        P.wait(ve, "cS")
        return P.inc(ve.tensor_tensor(out=XT[:, ft, :], in0=XT[:, ft, :], in1=tmpP,
                                      op=ALU.add), "cV")

    pendp = {}
    for fo in range(0, NCT, 2):
        slot, bsl = load_w(wcols(Wproj_h, fo * 128, 256), Bproj_h[:, fo * 128:(fo + 2) * 128])
        last = None
        for sub in range(2):
            ft = fo + sub
            psum = (psA, psB)[ft % 2]
            if ft >= 2:
                te.wait_ge(P.sems["cV"], pendp[ft - 2])
            for ch in range(4):
                ps = psum[:, ch * 512:(ch + 1) * 512]
                for j in range(8):
                    last = te.matmul(ps, slot[:, j, sub * 128:(sub + 1) * 128],
                                     BK[:, j, ch * 512:(ch + 1) * 512],
                                     start=(j == 0), stop=False)
                    P.inc(last, "cT")
                last = te.matmul(ps, bsl[:, sub * 128:(sub + 1) * 128],
                                 lora1T[:, ch * 512:(ch + 1) * 512],
                                 start=False, stop=True)
                P.inc(last, "cT")
            pendp[ft] = evict_proj(ft, psum)
        slot_done(last)
    te.wait_ge(P.sems["cV"], pendp[NCT - 1])
    ve.wait_ge(P.sems["cV"], pendp[NCT - 1])
    if debug_stage == "proj0":
        return finish()

    P.wait(gp, "cT")
    P.wait(gp, "cV")
    P.wait(gp, "cS")
    P.wait(sy, "cT")
    rs_free("VR")
    normC = rs_alloc("normC", [8, T], BF16)
    combT = rs_alloc("combT", [8, T], BF16)
    xn_guard_zero()
    layernorm()
    if debug_stage == "ln1":
        return finish()

    P.wait(te, "cV")
    for tt in range(NT):
        ps = psA[:, (tt % 4) * 512:(tt % 4) * 512 + 8]
        for cc in range(NCT):
            P.inc(te.matmul(ps, xn(cc, tt * 128, (tt + 1) * 128), gateT[:, cc, :],
                            start=(cc == 0), stop=(cc == NCT - 1)), "cT")
        P.wait(ve, "cT")
        P.inc(ve.tensor_copy(out=LA[:, tt, :], in_=ps), "cV")
        P.wait(te, "cV")
    P.inc(ve.tensor_tensor(out=GB[:, :, 0:4], in0=LA[:, :, 0:4], in1=LA[:, :, 4:8],
                           op=ALU.max), "cV")
    P.inc(ve.tensor_tensor(out=GB[:, :, 4:8], in0=GB[:, :, 0:4], in1=GB[:, :, 2:6],
                           op=ALU.max), "cV")
    P.inc(ve.tensor_tensor(out=GB[:, :, 8:12], in0=GB[:, :, 4:8], in1=GB[:, :, 5:9],
                           op=ALU.max), "cV")
    P.inc(ve.tensor_scalar(out=GB[:, :, 4:8], in0=GB[:, :, 8:12], scalar1=-1.0,
                           scalar2=None, op0=ALU.mult), "cV")
    P.wait(sy, "cV")
    P.inc(sy.dma_start(out=GL, in_=GB), "sMV", 16)
    P.wait(ve, "sMV")
    for tt in range(NT):
        P.inc(ve.tensor_scalar(out=MA[:, tt, :], in0=LA[:, tt, :],
                               scalar1=GL[:, tt, 8:9], scalar2=None, op0=ALU.is_ge), "cV")
    P.inc(ve.tensor_scalar(out=EA[:], in0=MA[:], scalar1=1e5, scalar2=None,
                           op0=ALU.mult), "cV")
    P.inc(ve.tensor_tensor(out=EA[:], in0=LA[:], in1=EA[:], op=ALU.subtract), "cV")
    P.inc(ve.tensor_tensor(out=MB[:, :, 0:4], in0=EA[:, :, 0:4], in1=EA[:, :, 4:8],
                           op=ALU.max), "cV")
    P.inc(ve.tensor_tensor(out=MB[:, :, 4:8], in0=MB[:, :, 0:4], in1=MB[:, :, 2:6],
                           op=ALU.max), "cV")
    P.inc(ve.tensor_tensor(out=MB[:, :, 8:12], in0=MB[:, :, 4:8], in1=MB[:, :, 5:9],
                           op=ALU.max), "cV")
    P.wait(sy, "cV")
    P.inc(sy.dma_start(out=ML[:], in_=MB[:]), "sMV", 16)
    P.wait(ve, "sMV")
    P.wait(sc, "sMV")
    for tt in range(NT):
        P.inc(ve.tensor_scalar(out=MA[:, tt, :], in0=LA[:, tt, :],
                               scalar1=ML[:, tt, 8:9], scalar2=None, op0=ALU.is_ge), "cV")
        P.inc(sc.activation(out=EA[:, tt, :], in_=LA[:, tt, :], func=AF.Exp,
                            bias=GL[:, tt, 4:5]), "cS")
    P.wait(ve, "cS")
    P.inc(ve.tensor_tensor(out=EA[:], in0=EA[:], in1=MA[:], op=ALU.mult), "cV")
    P.wait(te, "cV")
    for tt in range(NT):
        ps = psB[0:8, (tt % 4) * 512:(tt % 4) * 512 + 128]
        P.inc(te.transpose(ps, EA[:, tt, :], ident[:]), "cT")
        P.wait(ve, "cT")
        P.inc(ve.tensor_copy(out=combT[:, tt * 128:(tt + 1) * 128], in_=ps), "cV")
        P.wait(te, "cV")
    for ch in range(4):
        P.inc(te.matmul(psA[0:64, ch * 512:(ch + 1) * 512], onesA[0:8, 0:64],
                        combT[:, ch * 512:(ch + 1) * 512], start=True, stop=True), "cT")
    P.wait(ve, "cT")
    P.inc(ve.reciprocal(out=psB[0:64, 0:T], in_=psA[0:64, 0:T]), "cV")
    P.inc(ve.tensor_tensor(out=normC[:], in0=combT[:], in1=psB[0:8, 0:T],
                           op=ALU.mult), "cV")
    P.wait(te, "cV")
    if debug_stage == "gate":
        return finish()
    rs_free("combT")

    def evict_shared_h(ft, psum):
        P.wait(sc, "cT")
        c = P.inc(sc.activation(out=BK[:, ft, :], in_=psum[:, 0:T], func=AF.Silu,
                                bias=shupb[:, ft:ft + 1]), "cS")
        return ("cS", c)
    mm_1024(evict_shared_h, shup_h, lambda cc, a, b: xn(cc, a, b))

    def evict_shared_d(ft, psum):
        P.wait(sc, "cT")
        P.inc(sc.activation(out=sStage[:, 0:T], in_=psum[:, 0:T], func=AF.Identity,
                            bias=shdwb[:, ft:ft + 1]), "cS")
        P.wait(ve, "cS")
        c = P.inc(ve.tensor_tensor(out=XT[:, ft, :], in0=XT[:, ft, :], in1=sStage[:, 0:T],
                                   op=ALU.add), "cV")
        return ("cV", c)
    mm_1024(evict_shared_d, shdw_h, lambda cc, a, b: BK[:, cc, a:b])
    P.wait(sy, "cT")
    P.inc(sy.dma_start(out=esel, in_=esel_h), "sIN", 16)
    P.wait(te, "sIN")
    if debug_stage == "shared":
        return finish()

    for e in range(8):
        P.wait(te, "cV")
        for ch in range(4):
            P.inc(te.matmul(psB[:, ch * 512:(ch + 1) * 512], esel[:, e * 128:(e + 1) * 128],
                            normC[:, ch * 512:(ch + 1) * 512],
                            start=True, stop=True), "cT")
        P.wait(sc, "cT")
        P.inc(sc.copy(out=bcMoE, in_=psB[:, 0:T]), "cS")
        P.wait(ve, "cS")
        P.wait(te, "cS")

        def evict_up(ft, psum, e=e):
            P.wait(sc, "cT")
            P.inc(sc.activation(out=sStage[:, 0:T], in_=psum[:, 0:T], func=AF.Silu,
                                bias=upb[:, ft, e:e + 1]), "cS")
            P.wait(ve, "cS")
            c = P.inc(ve.tensor_tensor(out=BQ[:, ft, :], in0=sStage[:, 0:T], in1=bcMoE,
                                       op=ALU.mult), "cV")
            return ("cV", c)
        mm_1024(evict_up, upw_h[e], lambda cc, a, b: xn(cc, a, b))

        def evict_down(ft, psum, e=e):
            P.wait(ve, "cT")
            return P.inc(ve.tensor_tensor(out=XT[:, ft, :], in0=XT[:, ft, :],
                                          in1=psum[:, 0:T], op=ALU.add), "cV")

        pendd = {}
        for fo in range(0, NCT, 2):
            slot, bsl = load_w(wcols(dww_h[e], fo * 128, 256),
                               dwb_h[:, fo * 128:(fo + 2) * 128] if e == 0 else None)
            last = None
            for sub in range(2):
                ft = fo + sub
                psum = (psA, psB)[ft % 2]
                if ft >= 2:
                    te.wait_ge(P.sems["cV"], pendd[ft - 2])
                for ch in range(4):
                    ps = psum[:, ch * 512:(ch + 1) * 512]
                    for cc in range(NCT):
                        last = te.matmul(ps, slot[:, cc, sub * 128:(sub + 1) * 128],
                                         BQ[:, cc, ch * 512:(ch + 1) * 512],
                                         start=(cc == 0), stop=(e != 0 and cc == NCT - 1))
                        P.inc(last, "cT")
                    if e == 0:
                        last = te.matmul(ps, bsl[:, sub * 128:(sub + 1) * 128],
                                         normC[:, ch * 512:(ch + 1) * 512],
                                         start=False, stop=True)
                        P.inc(last, "cT")
                pendd[ft] = evict_down(ft, psum)
            slot_done(last)
        te.wait_ge(P.sems["cV"], pendd[NCT - 1])
    if debug_stage == "moe":
        return finish()
    rs_free("normC")

    def evict_qT(dst):
        def f(ft, psum):
            P.wait(ve, "cT")
            c = P.inc(ve.tensor_copy(out=dst[:, ft, :], in_=psum[:, 0:T]), "cV")
            return ("cV", c)
        return f
    mm_1024(evict_qT(BQ), wqT_h, lambda cc, a, b: xn(cc, a, b))
    mm_1024(evict_qT(BK), wkT_h, lambda cc, a, b: xn(cc, a, b))

    VR2 = rs_alloc("VR2", [128, 16, 1040], BF16)
    vst["V1"] = VR2
    vrf = VR2.rearrange("p a b -> p (a b)")
    cosv = vrf[:, 0:4 * T].rearrange("p (a t) -> p a t", t=T)
    sinv = vrf[:, 4 * T:8 * T].rearrange("p (a t) -> p a t", t=T)
    P.wait(gp, "cV")
    P.wait(gp, "cT")
    P.inc(gp.dma_start(out=cosv[:], in_=cosT_h), "sW", 16)
    P.inc(gp.dma_start(out=sinv[:], in_=sinT_h), "sW", 16)
    P.wait(ve, "sW")
    for buf in (BQ, BK):
        for i in range(4):
            qe, qo = buf[:, i, :], buf[:, i + 4, :]
            cs, sn = cosv[:, i, :], sinv[:, i, :]
            P.inc(ve.tensor_tensor(out=sStage[:, 0:T], in0=qo, in1=sn, op=ALU.mult), "cV")
            P.inc(ve.tensor_tensor(out=psA[:, 0:T], in0=qe, in1=cs, op=ALU.mult), "cV")
            P.inc(ve.tensor_tensor(out=psB[:, 0:T], in0=psA[:, 0:T], in1=sStage[:, 0:T],
                                   op=ALU.add), "cV")
            P.inc(ve.tensor_tensor(out=sStage[:, 0:T], in0=qe, in1=sn, op=ALU.mult), "cV")
            P.inc(ve.tensor_tensor(out=psA[:, 0:T], in0=qo, in1=cs, op=ALU.mult), "cV")
            P.inc(ve.tensor_tensor(out=qo, in0=psA[:, 0:T], in1=sStage[:, 0:T],
                                   op=ALU.subtract), "cV")
            P.inc(ve.tensor_copy(out=qe, in_=psB[:, 0:T]), "cV")
    P.wait(te, "cV")
    P.wait(gp, "cV")

    V1 = VR2
    for ci in range(4):
        slot, _b = load_w(wcols(wvT_h, ci * 256, 256))
        last = None
        pend = {}
        for tt in range(NT):
            ps = psA[:, (tt % 4) * 512:(tt % 4) * 512 + 256]
            if tt >= 4:
                te.wait_ge(P.sems["cV"], pend[tt - 4])
            for cc in range(NCT):
                last = te.matmul(ps, xn(cc, tt * 128, (tt + 1) * 128),
                                 slot[:, cc, 0:256], start=(cc == 0), stop=(cc == NCT - 1))
                P.inc(last, "cT")
            P.wait(ve, "cT")
            pend[tt] = P.inc(ve.tensor_copy(out=V1[:, tt, ci * 256:(ci + 1) * 256],
                                            in_=ps), "cV")
        te.wait_ge(P.sems["cV"], pend[NT - 1])
        slot_done(last)
    if debug_stage == "qkv1":
        return finish()

    P.wait(gp, "cT")
    P.wait(gp, "cV")
    P.wait(sy, "cT")
    BVflat = BV.rearrange("p a b -> p (a b)")
    m0 = BVflat[:, 7680:8576]
    P.inc(sy.dma_start(out=m0[:], in_=m0_h), "sIN", 16)
    P.wait(ve, "sIN")

    psO = psB.rearrange("p (a b) -> p a b", b=512)

    BVf32 = BVflat.bitcast(F32)
    a1inv32 = BVf32[:, 3328:3840]
    a1tmp = BVflat[:, 8576:9088]

    def expslot(kt):
        if kt < 4:
            return sStage[:, kt * 512:(kt + 1) * 512]
        return BVflat[:, (kt - 4) * 512:(kt - 3) * 512]

    for qc in range(4):
        q0 = qc * 512
        nk = 4 * qc + 4
        exp_gd = {}
        for kg in range(0, nk, 2):
            g = kg // 2
            gb = (g % 2) * 1024
            kn = min(2, nk - kg)
            if g >= 2:
                te.wait_ge(P.sems["cS"], exp_gd[g - 2])
            for ki in range(kn):
                kt = kg + ki
                for cc in range(NCT):
                    P.inc(te.matmul(psA[:, gb + ki * 512:gb + (ki + 1) * 512],
                                    BK[:, cc, kt * 128:(kt + 1) * 128],
                                    BQ[:, cc, q0:q0 + 512],
                                    start=(cc == 0), stop=(cc == NCT - 1)), "cT")
            P.wait(sc, "cT")
            for ki in range(kn):
                exp_gd[g] = P.inc(
                    sc.activation(out=expslot(kg + ki), in_=psA[:, gb + ki * 512:gb + (ki + 1) * 512],
                                  func=AF.Exp, bias=fconsts[:, 1:2], scale=0.03125), "cS")
            for ki in range(kn):
                kt = kg + ki
                r = kt - 4 * qc
                if 0 <= r <= 3:
                    ve.wait_ge(P.sems["cS"], exp_gd[g])
                    P.inc(ve.tensor_tensor(out=expslot(kt), in0=expslot(kt),
                                           in1=m0[:, 384 - 128 * r:384 - 128 * r + 512],
                                           op=ALU.mult), "cV")
        P.wait(te, "cV")
        P.wait(te, "cS")
        for kt in range(nk):
            for dt in range(4):
                P.inc(te.matmul(psO[:, dt, :], V1[:, kt, dt * 128:(dt + 1) * 128],
                                expslot(kt), start=(kt == 0), stop=(kt == nk - 1)), "cT")
            P.inc(te.matmul(psA[:, 1024:1536], onesA, expslot(kt),
                            start=(kt == 0), stop=(kt == nk - 1)), "cT")
        P.wait(ve, "cT")
        P.inc(ve.reciprocal(out=a1inv32, in_=psA[:, 1024:1536]), "cV")
        for dt in range(4):
            P.inc(ve.tensor_tensor(out=a1tmp, in0=psO[:, dt, :],
                                   in1=a1inv32, op=ALU.mult), "cV")
            P.inc(ve.tensor_tensor(out=XT[:, dt, q0:q0 + 512], in0=XT[:, dt, q0:q0 + 512],
                                   in1=a1tmp, op=ALU.add), "cV")
        P.wait(te, "cV")
        if debug_stage == "attn1q0" and qc == 0:
            return finish()
        for kt in range(nk):
            for dt in range(4):
                P.inc(te.matmul(psO[:, dt, :], V1[:, kt, (dt + 4) * 128:(dt + 5) * 128],
                                expslot(kt), start=(kt == 0), stop=(kt == nk - 1)), "cT")
        P.wait(ve, "cT")
        for dt in range(4):
            P.inc(ve.tensor_tensor(out=a1tmp, in0=psO[:, dt, :],
                                   in1=a1inv32, op=ALU.mult), "cV")
            P.inc(ve.tensor_tensor(out=XT[:, dt + 4, q0:q0 + 512],
                                   in0=XT[:, dt + 4, q0:q0 + 512],
                                   in1=a1tmp, op=ALU.add), "cV")
        P.wait(te, "cV")
        P.wait(sc, "cV")

    return finish()




_BUILD = {}


def _get_program(debug_stage=None):
    key = debug_stage
    if key not in _BUILD:
        _BUILD[key] = build_program(debug_stage)
    return _BUILD[key]


def kernel(**inputs):
    cst = _host_constants()
    H = _prep_inputs(inputs)
    nc = _get_program(None)
    x = np.asarray(inputs["L0_attn_x"], dtype=np.float32)
    base = {}
    base.update(H)
    base.update(cst)
    in_maps = []
    for c in range(8):
        m = dict(base)
        m["x"] = np.ascontiguousarray(x[c])
        in_maps.append(m)
    res = run_bass_kernel_spmd(nc, in_maps, core_ids=list(range(8)))
    out = np.stack([np.asarray(res.results[c]["out"]) for c in range(8)], axis=0)
    return out.astype(np.float32)


# version 19 (final)
# speedup vs baseline: 1.2075x; 1.0032x over previous
import numpy as np
import ml_dtypes
import contextlib
import concourse.bass as bass
import concourse.mybir as mybir
from concourse.bass_utils import run_bass_kernel_spmd

F32 = mybir.dt.float32
BF16 = mybir.dt.bfloat16
AF = mybir.ActivationFunctionType
ALU = mybir.AluOpType

T = 2048
C = 1024
NT = 16
NCT = 8
HD = 64
LORA_SCALE = 2.0
EXP_SHIFT0 = 6.0
EXP_SHIFT1 = 6.0
GCOL = 8
_bf = ml_dtypes.bfloat16
DBG_MIN = False



def _host_constants():
    cst = {}
    cst["ident_f32"] = np.eye(128, dtype=np.float32)
    cst["ones_1x128"] = np.ones((1, 128), dtype=_bf)
    cst["ones_8x1"] = np.ones((8, 16), dtype=_bf)
    cst["ones_c128"] = np.ones((128, 8), dtype=_bf)
    cst["zeros_guard"] = np.zeros((128, NCT, GCOL), dtype=_bf)
    cst["vones"] = np.ones((128, 16, 16), dtype=_bf)
    ps = np.zeros((16, 8 * 128), dtype=_bf)
    for j in range(8):
        ps[2 * j, j * 128:j * 128 + 64] = 1.0
        ps[2 * j + 1, j * 128 + 64:j * 128 + 128] = 1.0
    cst["pairsel"] = ps
    es = np.zeros((8, 1024), dtype=_bf)
    for e in range(8):
        es[e, e * 128:(e + 1) * 128] = 1.0
    cst["esel"] = es
    k = np.arange(128)[:, None]
    c0 = np.arange(1024)[None, :]
    cst["m0"] = (c0 >= k + 384).astype(_bf)[:, :896]
    i = np.arange(C // 2, dtype=np.float64)
    phi = np.power(10000.0, -2.0 * (i - 1.0) / C)
    ang = np.arange(T, dtype=np.float64)[None, :] * phi[:, None]
    cst["cosT"] = np.ascontiguousarray(
        np.cos(ang).astype(_bf).reshape(4, 128, T).transpose(1, 0, 2))
    cst["sinT"] = np.ascontiguousarray(
        np.sin(ang).astype(_bf).reshape(4, 128, T).transpose(1, 0, 2))
    sc = np.zeros((128, 8), dtype=np.float32)
    sc[:, 0] = -EXP_SHIFT0
    sc[:, 1] = -EXP_SHIFT1
    cst["fconsts"] = sc
    return cst


def _prep_inputs(I):
    H = {}
    A = np.ascontiguousarray
    dd, hh = np.meshgrid(np.arange(64), np.arange(16), indexing="ij")
    vperm = (64 * hh + dd).reshape(-1)
    W = I["L0_attn_W_qkv"]
    H["Wqkv"] = A(np.concatenate([W[:, :2048], W[:, 2048:][:, vperm]], axis=1))
    B = I["L0_attn_B_qkv"]
    H["B_qkv"] = A(np.concatenate([B[:, :2048], B[:, 2048:][:, vperm]], axis=1))
    H["b_qkv"] = A(I["L0_attn_b_qkv"][:2048].reshape(16, 128).T)
    H["b_v"] = A(I["L0_attn_b_qkv"][2048:].reshape(8, 128).T)
    H["A_qkv"] = A(I["L0_attn_A_qkv"].reshape(8, 128, 8).transpose(1, 0, 2))
    H["W_proj"] = A(I["L0_attn_W_proj"])
    H["b_proj"] = A(I["L0_attn_b_proj"].reshape(8, 128).T)
    H["A_proj"] = A(I["L0_attn_A_proj"].reshape(8, 128, 8).transpose(1, 0, 2))
    H["B_proj"] = A(I["L0_attn_B_proj"])
    H["WrT"] = A(I["L0_ffn_W_r"].T)
    H["WkT"] = A(I["L0_ffn_W_k"].T)
    H["WvT"] = A(I["L0_ffn_W_v"].T)
    H["mu_r"] = A(I["L0_ffn_mu_r"].reshape(8, 128).T)
    H["mu_k"] = A(I["L0_ffn_mu_k"].reshape(8, 128).T)
    perm = np.concatenate([np.arange(0, C, 2), np.arange(1, C, 2)])
    H["wqT"] = A(I["L1_attn_wq"][perm].T)
    H["wkT"] = A(I["L1_attn_wk"][perm].T)
    H["wvT"] = A(I["L1_attn_wv"].T)
    H["gate_wT"] = A(I["L1_ffn_gate_w"].T.reshape(8, 128, 8).transpose(1, 0, 2))
    H["up_w"] = A(I["L1_ffn_up_w"])
    H["up_b"] = A(I["L1_ffn_up_b"].reshape(8, 8, 128).transpose(2, 1, 0))
    H["down_w"] = A(I["L1_ffn_down_w"])
    H["down_b"] = A(I["L1_ffn_down_b"])
    H["sh_up_w"] = A(I["L1_ffn_sh_up_w"])
    H["sh_up_b"] = A(I["L1_ffn_sh_up_b"].reshape(8, 128).T)
    H["sh_down_w"] = A(I["L1_ffn_sh_down_w"])
    H["sh_down_b"] = A(I["L1_ffn_sh_down_b"].reshape(8, 128).T)
    for k2 in H:
        if H[k2].dtype != np.float32:
            H[k2] = H[k2].astype(np.float32)
    return H



class Prog:
    def __init__(self, debug_stage=None):
        self.debug_stage = debug_stage
        self.dbg = {}
        self.stopped = False

    def sem_init(self, nc):
        self.nc = nc
        self.cnt = {}
        self.sems = {}
        for s in ["sIN", "sW", "sMV", "sOUT", "cT", "cV", "cS"]:
            cm = nc.semaphore(s)
            self.sems[s] = self._stack.enter_context(cm)
            self.cnt[s] = 0

    def inc(self, instr, name, by=1):
        instr.then_inc(self.sems[name], by)
        self.cnt[name] += by
        return self.cnt[name]

    def wait(self, eng, name, target=None):
        eng.wait_ge(self.sems[name], self.cnt[name] if target is None else target)


def build_program(debug_stage=None):
    nc = bass.Bass("TRN2", target_bir_lowering=False,
                   detect_race_conditions=False)
    P = Prog(debug_stage)
    P._stack = contextlib.ExitStack()
    P.sem_init(nc)
    stack = P._stack

    din = {}
    def dram_in(name, shape, dtype=F32):
        din[name] = nc.dram_tensor(name, list(shape), dtype, kind="ExternalInput").ap()
        return din[name]

    x_h = dram_in("x", (T, C))
    Wqkv_h = dram_in("Wqkv", (C, 3 * C))
    bqkv_h = dram_in("b_qkv", (128, 16))
    bv_h = dram_in("b_v", (128, 8))
    Aqkv_h = dram_in("A_qkv", (128, 8, 8))
    Bqkv_h = dram_in("B_qkv", (8, 3 * C))
    Wproj_h = dram_in("W_proj", (C, C))
    bproj_h = dram_in("b_proj", (128, 8))
    Aproj_h = dram_in("A_proj", (128, 8, 8))
    Bproj_h = dram_in("B_proj", (8, C))
    WrT_h = dram_in("WrT", (C, C))
    WkT_h = dram_in("WkT", (C, C))
    WvT_h = dram_in("WvT", (C, C))
    mur_h = dram_in("mu_r", (128, 8))
    muk_h = dram_in("mu_k", (128, 8))
    wqT_h = dram_in("wqT", (C, C))
    wkT_h = dram_in("wkT", (C, C))
    wvT_h = dram_in("wvT", (C, C))
    gateT_h = dram_in("gate_wT", (128, 8, 8))
    upw_h = dram_in("up_w", (8, C, C))
    upb_h = dram_in("up_b", (128, 8, 8))
    dww_h = dram_in("down_w", (8, C, C))
    dwb_h = dram_in("down_b", (8, C))
    shup_h = dram_in("sh_up_w", (C, C))
    shupb_h = dram_in("sh_up_b", (128, 8))
    shdw_h = dram_in("sh_down_w", (C, C))
    shdwb_h = dram_in("sh_down_b", (128, 8))
    ident_h = dram_in("ident_f32", (128, 128))
    ones1_h = dram_in("ones_1x128", (1, 128), BF16)
    ones8_h = dram_in("ones_8x1", (8, 16), BF16)
    onesc_h = dram_in("ones_c128", (128, 8), BF16)
    zguard_h = dram_in("zeros_guard", (128, NCT, GCOL), BF16)
    vones_h = dram_in("vones", (128, 16, 16), BF16)
    pairsel_h = dram_in("pairsel", (16, 1024), BF16)
    esel_h = dram_in("esel", (8, 1024), BF16)
    m0_h = dram_in("m0", (128, 896), BF16)
    cosT_h = dram_in("cosT", (128, 4, T), BF16)
    sinT_h = dram_in("sinT", (128, 4, T), BF16)
    fconsts_h = dram_in("fconsts", (128, 8))
    out_h = nc.dram_tensor("out", [T, C], F32, kind="ExternalOutput").ap()
    dbg_h = {}
    if debug_stage is not None:
        for nm, shp, dt in [("dbg_xt", [128, NCT, T], F32),
                            ("dbg_xn", [128, NCT, GCOL + T], BF16),
                            ("dbg_bq", [128, NCT, T], BF16),
                            ("dbg_bk", [128, NCT, T], BF16),
                            ("dbg_bv", [128, 16, 1040], BF16),
                            ("dbg_sall", [16, T], BF16),
                            ("dbg_comb", [8, T], BF16),
                            ("dbg_normc", [8, T], BF16),
                            ("dbg_la", [128, 16, 8], F32),
                            ("dbg_ea", [128, 16, 8], F32)]:
            dbg_h[nm] = nc.dram_tensor(nm, shp, dt, kind="ExternalOutput").ap()

    S = lambda *a: stack.enter_context(nc.sbuf_tensor(*a))

    XT = S("XT", [128, NCT, T], F32)
    BQ = S("BQ", [128, NCT, T], BF16)
    BK = S("BK", [128, NCT, T], BF16)
    BV = S("BV", [128, 16, 1040], BF16)
    ST = [S("ST0", [128, NCT, 256], BF16), S("ST1", [128, NCT, 256], BF16)]
    BqS = [S("BqS0", [8, 256], BF16), S("BqS1", [8, 256], BF16)]
    XNv = BV.rearrange("p a b -> p (a b)")[:, 0:NCT * (GCOL + T)] \
            .rearrange("p (a b) -> p a b", b=GCOL + T)
    xn_state = {"XN": XNv, "cm": None}

    def xn_guard_zero():
        P.inc(sy.dma_start(out=XNv[:, :, 0:GCOL], in_=zguard_h), "sIN", 16)
    BVsp = BV.rearrange("p a b -> p (a b)")
    onesA = BVsp[:, 16448:16576]
    ident = S("identb", [128, 128], F32)
    ones1 = S("ones1b", [1, 128], BF16)
    ones8 = S("ones8b", [8, 16], BF16)
    onesc = S("onescb", [128, 8], BF16)
    fconsts = S("fconstsb", [128, 8], F32)
    bqkv = S("bqkvb", [128, 16], F32)
    bv = S("bvb", [128, 8], F32)
    bproj = S("bprojb", [128, 8], F32)
    mur = S("murb", [128, 8], F32)
    muk = S("mukb", [128, 8], F32)
    upb = S("upbb", [128, 8, 8], F32)
    shupb = S("shupbb", [128, 8], F32)
    shdwb = S("shdwbb", [128, 8], F32)
    Aqkv = S("Aqkvb", [128, 8, 8], BF16)
    Aproj = S("Aprojb", [128, 8, 8], BF16)
    gateT = S("gateTb", [128, 8, 8], BF16)

    RS = {}

    def rs_alloc(name, shape, dtype):
        cm = nc.sbuf_tensor(name, shape, dtype, side="right")
        RS[name] = (cm, cm.__enter__())
        return RS[name][1]

    def rs_free(name):
        cm, _ = RS.pop(name)
        cm.__exit__(None, None, None)

    BQf32 = BQ.bitcast(F32)
    BQrow = BQf32.rearrange("p a b -> p (a b)")
    rowsA = BQrow[0:1, 0:T]
    rowsB = BQrow[0:1, T:2 * T]
    rowsE = BQrow[0:1, 2 * T:3 * T]
    BKflat = BK.rearrange("p a b -> p (a b)")
    rowsC = BKflat[0:1, 0:T]
    rowsD = BKflat[0:1, T:2 * T]
    BKf32 = BK.bitcast(F32).rearrange("p a b -> p (a b)")
    invden = BKf32[0:1, 0:T]
    a_invdenC = BKflat[0:1, 2 * T:3 * T]
    BQflat32 = BQf32.rearrange("p a b -> p (a b)")
    outStage = [BQflat32[:, 0:1024], BQflat32[:, 2048:3072],
                BQflat32[:, 4096:5120], BQflat32[:, 6144:7168]]
    bcMoE = BKflat[:, 0:T]
    esel = BKflat[0:8, 3 * T:3 * T + 1024]
    GB = BQflat32[:, 1024:1216].rearrange("p (a b) -> p a b", b=12)
    MB = GB
    LA = BQflat32[:, 1216:1344].rearrange("p (a b) -> p a b", b=8)
    EA = BQflat32[:, 1344:1472].rearrange("p (a b) -> p a b", b=8)
    MA = BQflat32[:, 1472:1600].rearrange("p (a b) -> p a b", b=8)
    GL = BQflat32[:, 1600:1792].rearrange("p (a b) -> p a b", b=12)
    ML = BQflat32[:, 1792:1984].rearrange("p (a b) -> p a b", b=12)
    psA = stack.enter_context(nc.psum_tensor("psA", [128, T], F32))
    psB = stack.enter_context(nc.psum_tensor("psB", [128, T], F32))

    sStage = rs_alloc("sStage", [128, T], BF16)

    nc.gpsimd.dma_reset(range(155, 256))
    nc.gpsimd.sem_clear(range(155, 256))
    nc.all_engine_barrier()

    sy, gp, te, ve, sc = nc.sync, nc.gpsimd, nc.tensor, nc.vector, nc.scalar

    xs0 = BQ.bitcast(F32)
    xs1 = BK.bitcast(F32)
    P.inc(sy.dma_start(out=ident[:], in_=ident_h), "sIN", 16)
    P.inc(sy.dma_start(out=xs0[:],
                       in_=x_h[0:1024].rearrange("(a p) c -> p a c", p=128)), "sIN", 16)
    x0done = P.cnt["sIN"]
    P.inc(gp.dma_start(out=xs1[:],
                       in_=x_h[1024:2048].rearrange("(a p) c -> p a c", p=128)), "sW", 16)
    x1done = P.cnt["sW"]
    P.inc(sy.dma_start(out=onesA,
                       in_=vones_h.rearrange("p a b -> p (a b)")[:, 0:128]), "sIN", 16)
    for dst, src in [(ones1, ones1_h), (ones8, ones8_h),
                     (onesc, onesc_h), (fconsts, fconsts_h), (bqkv, bqkv_h),
                     (bv, bv_h), (bproj, bproj_h), (mur, mur_h), (muk, muk_h),
                     (upb, upb_h), (shupb, shupb_h), (shdwb, shdwb_h)]:
        P.inc(sy.dma_start(out=dst[:], in_=src), "sIN", 16)
    for dst, src in [(Aqkv, Aqkv_h), (Aproj, Aproj_h), (gateT, gateT_h)]:
        P.inc(gp.dma_start(out=dst[:], in_=src), "sW", 16)

    wstate = {"i": 0, "free": [0, 0]}

    def load_w(src_ap, b_src=None, width=256):
        i = wstate["i"] % 2
        wstate["i"] += 1
        gp.wait_ge(P.sems["cT"], wstate["free"][i])
        P.inc(gp.dma_start(out=ST[i][:, :, 0:width], in_=src_ap), "sW", 16)
        if b_src is not None:
            P.inc(gp.dma_start(out=BqS[i][:, 0:width], in_=b_src), "sW", 16)
        te.wait_ge(P.sems["sW"], P.cnt["sW"])
        return (ST[i], BqS[i])

    def slot_done(instr):
        i = (wstate["i"] - 1) % 2
        wstate["free"][i] = P.cnt["cT"]

    def wcols(h, f0, w):
        return h.rearrange("(j p) f -> p j f", p=128)[:, :, f0:f0 + w]

    def xn(cc, t0=0, t1=T):
        return xn_state["XN"][:, cc, GCOL + t0:GCOL + t1]

    def finish():
        P.wait(te, "cV")
        P.wait(te, "cS")
        P.wait(te, "sMV")
        if debug_stage is None:
            dma_done = {}
            pendf = {}
            fj = 0
            for tt in range(NT):
                ob = outStage[tt % 4]
                dq, dsem = sy, "sOUT"
                if tt >= 4:
                    ve.wait_ge(P.sems[dsem], dma_done[tt - 4])
                for cc in range(NCT):
                    pslot = psB[:, (fj % 4) * 512:(fj % 4) * 512 + 128]
                    if fj >= 4:
                        te.wait_ge(P.sems["cV"], pendf[fj - 4])
                    P.inc(te.transpose(pslot, XT[:, cc, tt * 128:(tt + 1) * 128],
                                       ident[:]), "cT")
                    P.wait(ve, "cT")
                    pendf[fj] = P.inc(ve.tensor_copy(out=ob[:, cc * 128:(cc + 1) * 128],
                                                     in_=pslot), "cV")
                    fj += 1
                P.wait(dq, "cV")
                dma_done[tt] = P.inc(dq.dma_start(out=out_h[tt * 128:(tt + 1) * 128, :],
                                                  in_=ob), dsem, 16)
            P.wait(sy, "sOUT")
        else:
            P.wait(sy, "cV")
            P.wait(sy, "cS")
            P.wait(sy, "cT")
            pairs = [("dbg_xt", XT), ("dbg_bq", BQ),
                     ("dbg_bk", BK),
                     ("dbg_la", LA), ("dbg_ea", EA)]
            if DBG_MIN:
                pairs = [("dbg_xt", XT)]
            if "combT" in RS and not DBG_MIN:
                pairs.append(("dbg_comb", RS["combT"][1]))
            if "normC" in RS and not DBG_MIN:
                pairs.append(("dbg_normc", RS["normC"][1]))
            if "VR2" in RS and not DBG_MIN:
                pairs.append(("dbg_bv", RS["VR2"][1]))
            elif "VR" in RS:
                pairs.append(("dbg_bv", RS["VR"][1]))
            if not DBG_MIN:
                pairs.append(("dbg_xn", xn_state["XN"]))
            for nm, sb in pairs:
                P.inc(sy.dma_start(out=dbg_h[nm], in_=sb[:]), "sOUT", 16)
            P.wait(sy, "sOUT")
        stack.close()
        return nc

    te.wait_ge(P.sems["sIN"], x0done)
    pendx = {}
    ti = 0
    for tt in range(NT):
        if tt == 8:
            te.wait_ge(P.sems["sW"], x1done)
        src = xs0 if tt < 8 else xs1
        a = tt % 8
        for cc in range(NCT):
            pslot = psA[:, (ti % 4) * 512:(ti % 4) * 512 + 128]
            if ti >= 4:
                te.wait_ge(P.sems["cV"], pendx[ti - 4])
            P.inc(te.transpose(pslot, src[:, a, cc * 128:(cc + 1) * 128], ident[:]), "cT")
            P.wait(ve, "cT")
            pendx[ti] = P.inc(ve.tensor_copy(out=XT[:, cc, tt * 128:(tt + 1) * 128],
                                             in_=pslot), "cV")
            ti += 1
    te.wait_ge(P.sems["cV"], pendx[ti - 1])

    def layernorm():
        P.wait(te, "sIN")
        P.wait(te, "sW")
        P.wait(sc, "cV")
        for cc in range(NCT):
            P.inc(sc.activation(out=BK[:, cc, :], in_=XT[:, cc, :], func=AF.Copy), "cS")
            P.inc(sc.activation(out=BQ[:, cc, :], in_=XT[:, cc, :], func=AF.Square), "cS")
        P.wait(te, "cS")
        for ch in range(4):
            for cc in range(NCT):
                P.inc(te.matmul(psA[:, ch * 512:(ch + 1) * 512], onesA,
                                BK[:, cc, ch * 512:(ch + 1) * 512],
                                start=(cc == 0), stop=(cc == NCT - 1)), "cT")
            for cc in range(NCT):
                P.inc(te.matmul(psB[:, ch * 512:(ch + 1) * 512], onesA,
                                BQ[:, cc, ch * 512:(ch + 1) * 512],
                                start=(cc == 0), stop=(cc == NCT - 1)), "cT")
        tmpm2 = BQf32.rearrange("p a b -> p (a b)")[:, 0:T]
        P.wait(ve, "cT")
        P.inc(ve.tensor_scalar(out=psA[:, 0:T], in0=psA[:, 0:T], scalar1=1.0 / C,
                               scalar2=None, op0=ALU.mult), "cV")
        P.inc(ve.tensor_scalar(out=psB[:, 0:T], in0=psB[:, 0:T], scalar1=1.0 / C,
                               scalar2=None, op0=ALU.mult), "cV")
        P.wait(sc, "cV")
        P.inc(sc.activation(out=tmpm2, in_=psA[:, 0:T], func=AF.Square), "cS")
        P.wait(ve, "cS")
        P.inc(ve.tensor_tensor(out=psB[:, 0:T], in0=psB[:, 0:T], in1=tmpm2,
                               op=ALU.subtract), "cV")
        P.inc(ve.tensor_scalar(out=psB[:, 0:T], in0=psB[:, 0:T], scalar1=1e-5,
                               scalar2=None, op0=ALU.add), "cV")
        P.inc(ve.reciprocal(out=psB[:, 0:T], in_=psB[:, 0:T]), "cV")
        P.wait(sc, "cV")
        P.inc(sc.activation(out=psB[:, 0:T], in_=psB[:, 0:T], func=AF.Sqrt), "cS")
        P.wait(ve, "cS")
        for cc in range(NCT):
            P.inc(ve.tensor_tensor(out=BK[:, cc, :], in0=XT[:, cc, :], in1=psA[:, 0:T],
                                   op=ALU.subtract), "cV")
        for cc in range(NCT):
            P.inc(ve.tensor_tensor(out=xn(cc), in0=BK[:, cc, :], in1=psB[:, 0:T],
                                   op=ALU.mult), "cV")
        P.wait(te, "cV")
        P.wait(sc, "cV")

    xn_guard_zero()
    layernorm()
    if debug_stage == "ln0":
        return finish()

    lora1T = sStage[0:8, 0:T]

    def compute_lora1T(A_ap):
        P.wait(te, "sW")
        P.wait(te, "cV")
        P.wait(te, "cS")
        for ch in range(4):
            for cc in range(NCT):
                P.inc(te.matmul(psB[0:8, ch * 512:(ch + 1) * 512], A_ap[:, cc, :],
                                xn(cc, ch * 512, (ch + 1) * 512),
                                start=(cc == 0), stop=(cc == NCT - 1)), "cT")
        P.wait(ve, "cT")
        P.inc(ve.tensor_scalar(out=lora1T, in0=psB[0:8, 0:T], scalar1=LORA_SCALE,
                               scalar2=None, op0=ALU.mult), "cV")
        P.wait(te, "cV")

    ffn_state = {}

    def build_shift(mu_ap):
        P.wait(ve, "cS")
        for cc in range(NCT):
            P.inc(ve.tensor_tensor(out=sStage[:, 0:T], in0=xn(cc),
                                   in1=xn_state["XN"][:, cc, GCOL - 1:GCOL - 1 + T],
                                   op=ALU.subtract), "cV")
            P.inc(ve.tensor_scalar(out=sStage[:, 0:T], in0=sStage[:, 0:T],
                                   scalar1=mu_ap[:, cc:cc + 1], scalar2=None,
                                   op0=ALU.mult), "cV")
            P.inc(ve.tensor_tensor(out=BQ[:, cc, :], in0=sStage[:, 0:T],
                                   in1=xn_state["XN"][:, cc, GCOL - 1:GCOL - 1 + T],
                                   op=ALU.add), "cV")

    def mm_1024(dst_cb, w_h, rhs_cb):
        pend = {}
        for fo in range(0, NCT, 2):
            slot, _b = load_w(wcols(w_h, fo * 128, 256))
            last = None
            for sub in range(2):
                ft = fo + sub
                psum = (psA, psB)[ft % 2]
                if ft >= 2:
                    sem, cnt = pend[ft - 2]
                    te.wait_ge(P.sems[sem], cnt)
                for ch in range(4):
                    ps = psum[:, ch * 512:(ch + 1) * 512]
                    for cc in range(NCT):
                        last = te.matmul(ps, slot[:, cc, sub * 128:(sub + 1) * 128],
                                         rhs_cb(cc, ch * 512, (ch + 1) * 512),
                                         start=(cc == 0), stop=(cc == NCT - 1))
                        P.inc(last, "cT")
                pend[ft] = dst_cb(ft, psum)
            slot_done(last)
        for ft in (NCT - 2, NCT - 1):
            sem, cnt = pend[ft]
            te.wait_ge(P.sems[sem], cnt)

    def ffn0():
        kact = rs_alloc("kactR", [128, NCT, T], BF16)
        ffn_state["kact"] = kact
        build_shift(mur)
        P.wait(te, "cV")

        def evict_r(ft, psum):
            P.wait(sc, "cT")
            c = P.inc(sc.activation(out=BK[:, ft, :], in_=psum[:, 0:T], func=AF.Sigmoid), "cS")
            return ("cS", c)
        mm_1024(evict_r, WrT_h, lambda cc, a, b: BQ[:, cc, a:b])

        build_shift(muk)
        P.wait(te, "cV")

        def evict_k(ft, psum):
            P.wait(sc, "cT")
            kact = ffn_state["kact"]
            P.inc(sc.activation(out=kact[:, ft, 0:T], in_=psum[:, 0:T], func=AF.Relu), "cS")
            P.wait(ve, "cS")
            c = P.inc(ve.tensor_tensor(out=kact[:, ft, 0:T], in0=kact[:, ft, 0:T],
                                       in1=kact[:, ft, 0:T], op=ALU.mult), "cV")
            return ("cV", c)
        mm_1024(evict_k, WkT_h, lambda cc, a, b: BQ[:, cc, a:b])

        def evict_kv(ft, psum):
            P.wait(ve, "cT")
            tmp = sStage[:, 0:T]
            P.inc(ve.tensor_tensor(out=tmp, in0=psum[:, 0:T], in1=BK[:, ft, :],
                                   op=ALU.mult), "cV")
            c = P.inc(ve.tensor_tensor(out=XT[:, ft, :], in0=XT[:, ft, :], in1=tmp,
                                       op=ALU.add), "cV")
            return ("cV", c)
        mm_1024(evict_kv, WvT_h, lambda cc, a, b: ffn_state["kact"][:, cc, a:b])
        P.wait(gp, "cT")
        P.wait(gp, "cV")
        rs_free("kactR")

    ffn0()
    if debug_stage == "ffn0":
        return finish()

    vst = {}

    def qkv_half(half):
        pend = {}
        fi = 0
        for sec, tb in [(0, 0), (1, 4)]:
            fbase = sec * C + half * 512
            for fo in range(0, 4, 2):
                slot, bsl = load_w(wcols(Wqkv_h, fbase + fo * 128, 256),
                                   Bqkv_h[:, fbase + fo * 128:fbase + (fo + 2) * 128])
                last = None
                for sub in range(2):
                    ft = fo + sub
                    psum = (psA, psB)[fi % 2]
                    if fi >= 2:
                        te.wait_ge(P.sems["cS"], pend[fi - 2])
                    for ch in range(4):
                        ps = psum[:, ch * 512:(ch + 1) * 512]
                        for cc in range(NCT):
                            last = te.matmul(ps, slot[:, cc, sub * 128:(sub + 1) * 128],
                                             xn(cc, ch * 512, (ch + 1) * 512),
                                             start=(cc == 0), stop=False)
                            P.inc(last, "cT")
                        last = te.matmul(ps, bsl[:, sub * 128:(sub + 1) * 128],
                                         lora1T[:, ch * 512:(ch + 1) * 512],
                                         start=False, stop=True)
                        P.inc(last, "cT")
                    P.wait(sc, "cT")
                    bcol = half * 4 + ft + sec * 8
                    pend[fi] = P.inc(sc.activation(out=BQ[:, tb + ft, :],
                                                   in_=psum[:, 0:T], func=AF.Identity,
                                                   bias=bqkv[:, bcol:bcol + 1]), "cS")
                    fi += 1
                slot_done(last)
        sc.wait_ge(P.sems["cS"], pend[fi - 1])
        te.wait_ge(P.sems["cS"], pend[fi - 1])

    def build_v():
        V1 = rs_alloc("VR", [128, 16, 1040], BF16)
        vst["V1"] = V1
        P.wait(ve, "cT")
        P.inc(ve.memset(V1[:, :, 1024:1040], 1.0), "cV")
        P.wait(te, "cV")
        for ci in range(4):
            slot, bsl = load_w(wcols(Wqkv_h, 2048 + ci * 256, 256),
                               Bqkv_h[:, 2048 + ci * 256:2048 + (ci + 1) * 256])
            last = None
            pend = {}
            for tt in range(NT):
                ps = psA[:, (tt % 4) * 512:(tt % 4) * 512 + 256]
                if tt >= 4:
                    te.wait_ge(P.sems["cV"], pend[tt - 4])
                for cc in range(NCT):
                    last = te.matmul(ps, xn(cc, tt * 128, (tt + 1) * 128),
                                     slot[:, cc, 0:256],
                                     start=(cc == 0), stop=False)
                    P.inc(last, "cT")
                last = te.matmul(ps, lora1T[:, tt * 128:(tt + 1) * 128],
                                 bsl[:, 0:256], start=False, stop=True)
                P.inc(last, "cT")
                P.wait(ve, "cT")
                pend[tt] = P.inc(ve.tensor_copy(out=V1[:, tt, ci * 256:(ci + 1) * 256],
                                                in_=ps), "cV")
            te.wait_ge(P.sems["cV"], pend[NT - 1])
            slot_done(last)
        P.wait(te, "sMV")

    def attn_half(half):
        V1 = vst["V1"]
        ST1f32 = ST[1].rearrange("p a b -> p (a b)").bitcast(F32)
        ebuf = [[sStage[:, 0:512], sStage[:, 512:1024]],
                [sStage[:, 1024:1536], sStage[:, 1536:2048]]]
        gate = {"pvprev": (P.cnt["cT"], P.cnt["cT"]),
                "expprev": (P.cnt["cS"], P.cnt["cS"]),
                "tt": [P.cnt["cV"], P.cnt["cV"]]}
        pp = 0
        for p in range(4):
            tile = p
            gpair = half * 4 + p
            he = half * 8 + 2 * p
            for qq in range(4):
                q0 = qq * 512
                SB = (pp % 2) * 1024
                exp_done = {}
                pv_done = {}
                sv = {}

                def emit_S(kt):
                    par = kt % 2
                    if kt >= 2:
                        te.wait_ge(P.sems["cS"], exp_done[(kt - 2, 1)])
                    else:
                        te.wait_ge(P.sems["cS"], gate["expprev"][kt])
                    for eo in range(2):
                        P.inc(te.matmul(
                            psA[:, eo * 1024 + par * 512:eo * 1024 + (par + 1) * 512],
                            BQ[64 * eo:64 * eo + 64, 4 + tile, kt * 128:(kt + 1) * 128],
                            BQ[64 * eo:64 * eo + 64, tile, q0:q0 + 512],
                            start=True, stop=True,
                            tile_position=(64 * eo, 0)), "cT")
                    sv[kt] = P.cnt["cT"]

                emit_S(0)
                emit_S(1)
                for kt in range(NT):
                    par = kt % 2
                    for eo in range(2):
                        if eo == 0:
                            sc.wait_ge(P.sems["cT"], sv[kt])
                        if kt >= 2:
                            sc.wait_ge(P.sems["cT"], pv_done[(kt - 2, eo)])
                        elif eo == 0:
                            sc.wait_ge(P.sems["cT"], gate["pvprev"][kt])
                        exp_done[(kt, eo)] = P.inc(
                            sc.activation(out=ebuf[par][eo],
                                          in_=psA[:, eo * 1024 + par * 512:eo * 1024 + (par + 1) * 512],
                                          func=AF.Exp, bias=fconsts[:, 0:1],
                                          scale=0.125), "cS")
                    vt = V1[:, kt, 0:1040].rearrange("p (d h) -> p d h", h=16)
                    for eo in range(2):
                        te.wait_ge(P.sems["cS"], exp_done[(kt, eo)])
                        if kt == 0 and eo == 0:
                            te.wait_ge(P.sems["cV"], gate["tt"][pp % 2])
                        r0 = 64 * eo
                        P.inc(te.matmul(psB[r0:r0 + 64, SB:SB + 512],
                                        vt[:, 0:64, he + eo],
                                        ebuf[par][eo],
                                        start=(kt == 0), stop=(kt == NT - 1),
                                        tile_position=(0, r0),
                                        skip_group_check=True), "cT")
                        P.inc(te.matmul(psB[r0:r0 + 64, SB + 512:SB + 1024],
                                        onesA[:, 0:64], ebuf[par][eo],
                                        start=(kt == 0), stop=(kt == NT - 1),
                                        tile_position=(0, r0),
                                        skip_group_check=True), "cT")
                        pv_done[(kt, eo)] = P.cnt["cT"]
                        if eo == 0 and kt + 2 < NT:
                            emit_S(kt + 2)
                gate["pvprev"] = (pv_done[(NT - 2, 1)], pv_done[(NT - 1, 1)])
                gate["expprev"] = (exp_done[(NT - 2, 1)], exp_done[(NT - 1, 1)])
                P.wait(ve, "cT")
                P.inc(ve.reciprocal(out=ST1f32[:, 0:512],
                                    in_=psB[:, SB + 512:SB + 1024]), "cV")
                P.inc(ve.tensor_tensor(out=BK[:, gpair, q0:q0 + 512],
                                       in0=psB[:, SB:SB + 512],
                                       in1=ST1f32[:, 0:512], op=ALU.mult), "cV")
                gate["tt"][pp % 2] = P.cnt["cV"]
                P.inc(ve.tensor_scalar(out=BK[:, gpair, q0:q0 + 512],
                                       in0=BK[:, gpair, q0:q0 + 512],
                                       scalar1=bv[:, gpair:gpair + 1],
                                       scalar2=None, op0=ALU.add), "cV")
                pp += 1
        P.wait(gp, "cV")
        wstate["free"] = [P.cnt["cT"], P.cnt["cT"]]

    compute_lora1T(Aqkv)
    qkv_half(0)
    if debug_stage == "qkv0":
        return finish()
    build_v()
    if debug_stage == "v0":
        return finish()
    attn_half(0)
    compute_lora1T(Aqkv)
    qkv_half(1)
    attn_half(1)
    if debug_stage == "attn0":
        return finish()

    P.wait(te, "cV")
    P.wait(te, "cS")
    for ch in range(4):
        for j in range(8):
            P.inc(te.matmul(psA[0:8, ch * 512:(ch + 1) * 512], Aproj[:, j, :],
                            BK[:, j, ch * 512:(ch + 1) * 512],
                            start=(j == 0), stop=(j == 7)), "cT")
    P.wait(ve, "cT")
    P.inc(ve.tensor_scalar(out=lora1T, in0=psA[0:8, 0:T], scalar1=LORA_SCALE,
                           scalar2=None, op0=ALU.mult), "cV")
    P.wait(te, "cV")
    tmpP = BV.rearrange("p a b -> p (a b)")[:, 0:T]

    def evict_proj(ft, psum):
        P.wait(sc, "cT")
        P.inc(sc.activation(out=tmpP, in_=psum[:, 0:T], func=AF.Identity,
                            bias=bproj[:, ft:ft + 1]), "cS")
        P.wait(ve, "cS")
        return P.inc(ve.tensor_tensor(out=XT[:, ft, :], in0=XT[:, ft, :], in1=tmpP,
                                      op=ALU.add), "cV")

    pendp = {}
    for fo in range(0, NCT, 2):
        slot, bsl = load_w(wcols(Wproj_h, fo * 128, 256), Bproj_h[:, fo * 128:(fo + 2) * 128])
        last = None
        for sub in range(2):
            ft = fo + sub
            psum = (psA, psB)[ft % 2]
            if ft >= 2:
                te.wait_ge(P.sems["cV"], pendp[ft - 2])
            for ch in range(4):
                ps = psum[:, ch * 512:(ch + 1) * 512]
                for j in range(8):
                    last = te.matmul(ps, slot[:, j, sub * 128:(sub + 1) * 128],
                                     BK[:, j, ch * 512:(ch + 1) * 512],
                                     start=(j == 0), stop=False)
                    P.inc(last, "cT")
                last = te.matmul(ps, bsl[:, sub * 128:(sub + 1) * 128],
                                 lora1T[:, ch * 512:(ch + 1) * 512],
                                 start=False, stop=True)
                P.inc(last, "cT")
            pendp[ft] = evict_proj(ft, psum)
        slot_done(last)
    te.wait_ge(P.sems["cV"], pendp[NCT - 1])
    ve.wait_ge(P.sems["cV"], pendp[NCT - 1])
    if debug_stage == "proj0":
        return finish()

    P.wait(gp, "cT")
    P.wait(gp, "cV")
    P.wait(gp, "cS")
    P.wait(sy, "cT")
    rs_free("VR")
    normC = rs_alloc("normC", [8, T], BF16)
    combT = rs_alloc("combT", [8, T], BF16)
    xn_guard_zero()
    layernorm()
    if debug_stage == "ln1":
        return finish()

    P.wait(te, "cV")
    for tt in range(NT):
        ps = psA[:, (tt % 4) * 512:(tt % 4) * 512 + 8]
        for cc in range(NCT):
            P.inc(te.matmul(ps, xn(cc, tt * 128, (tt + 1) * 128), gateT[:, cc, :],
                            start=(cc == 0), stop=(cc == NCT - 1)), "cT")
        P.wait(ve, "cT")
        P.inc(ve.tensor_copy(out=LA[:, tt, :], in_=ps), "cV")
        P.wait(te, "cV")
    P.inc(ve.tensor_tensor(out=GB[:, :, 0:4], in0=LA[:, :, 0:4], in1=LA[:, :, 4:8],
                           op=ALU.max), "cV")
    P.inc(ve.tensor_tensor(out=GB[:, :, 4:8], in0=GB[:, :, 0:4], in1=GB[:, :, 2:6],
                           op=ALU.max), "cV")
    P.inc(ve.tensor_tensor(out=GB[:, :, 8:12], in0=GB[:, :, 4:8], in1=GB[:, :, 5:9],
                           op=ALU.max), "cV")
    P.inc(ve.tensor_scalar(out=GB[:, :, 4:8], in0=GB[:, :, 8:12], scalar1=-1.0,
                           scalar2=None, op0=ALU.mult), "cV")
    P.wait(sy, "cV")
    P.inc(sy.dma_start(out=GL, in_=GB), "sMV", 16)
    P.wait(ve, "sMV")
    for tt in range(NT):
        P.inc(ve.tensor_scalar(out=MA[:, tt, :], in0=LA[:, tt, :],
                               scalar1=GL[:, tt, 8:9], scalar2=None, op0=ALU.is_ge), "cV")
    P.inc(ve.tensor_scalar(out=EA[:], in0=MA[:], scalar1=1e5, scalar2=None,
                           op0=ALU.mult), "cV")
    P.inc(ve.tensor_tensor(out=EA[:], in0=LA[:], in1=EA[:], op=ALU.subtract), "cV")
    P.inc(ve.tensor_tensor(out=MB[:, :, 0:4], in0=EA[:, :, 0:4], in1=EA[:, :, 4:8],
                           op=ALU.max), "cV")
    P.inc(ve.tensor_tensor(out=MB[:, :, 4:8], in0=MB[:, :, 0:4], in1=MB[:, :, 2:6],
                           op=ALU.max), "cV")
    P.inc(ve.tensor_tensor(out=MB[:, :, 8:12], in0=MB[:, :, 4:8], in1=MB[:, :, 5:9],
                           op=ALU.max), "cV")
    P.wait(sy, "cV")
    P.inc(sy.dma_start(out=ML[:], in_=MB[:]), "sMV", 16)
    P.wait(ve, "sMV")
    P.wait(sc, "sMV")
    for tt in range(NT):
        P.inc(ve.tensor_scalar(out=MA[:, tt, :], in0=LA[:, tt, :],
                               scalar1=ML[:, tt, 8:9], scalar2=None, op0=ALU.is_ge), "cV")
        P.inc(sc.activation(out=EA[:, tt, :], in_=LA[:, tt, :], func=AF.Exp,
                            bias=GL[:, tt, 4:5]), "cS")
    P.wait(ve, "cS")
    P.inc(ve.tensor_tensor(out=EA[:], in0=EA[:], in1=MA[:], op=ALU.mult), "cV")
    P.wait(te, "cV")
    for tt in range(NT):
        ps = psB[0:8, (tt % 4) * 512:(tt % 4) * 512 + 128]
        P.inc(te.transpose(ps, EA[:, tt, :], ident[:]), "cT")
        P.wait(ve, "cT")
        P.inc(ve.tensor_copy(out=combT[:, tt * 128:(tt + 1) * 128], in_=ps), "cV")
        P.wait(te, "cV")
    for ch in range(4):
        P.inc(te.matmul(psA[0:64, ch * 512:(ch + 1) * 512], onesA[0:8, 0:64],
                        combT[:, ch * 512:(ch + 1) * 512], start=True, stop=True), "cT")
    P.wait(ve, "cT")
    P.inc(ve.reciprocal(out=psB[0:64, 0:T], in_=psA[0:64, 0:T]), "cV")
    P.inc(ve.tensor_tensor(out=normC[:], in0=combT[:], in1=psB[0:8, 0:T],
                           op=ALU.mult), "cV")
    P.wait(te, "cV")
    if debug_stage == "gate":
        return finish()
    rs_free("combT")

    def evict_shared_h(ft, psum):
        P.wait(sc, "cT")
        c = P.inc(sc.activation(out=BK[:, ft, :], in_=psum[:, 0:T], func=AF.Silu,
                                bias=shupb[:, ft:ft + 1]), "cS")
        return ("cS", c)
    mm_1024(evict_shared_h, shup_h, lambda cc, a, b: xn(cc, a, b))

    def evict_shared_d(ft, psum):
        P.wait(sc, "cT")
        P.inc(sc.activation(out=sStage[:, 0:T], in_=psum[:, 0:T], func=AF.Identity,
                            bias=shdwb[:, ft:ft + 1]), "cS")
        P.wait(ve, "cS")
        c = P.inc(ve.tensor_tensor(out=XT[:, ft, :], in0=XT[:, ft, :], in1=sStage[:, 0:T],
                                   op=ALU.add), "cV")
        return ("cV", c)
    mm_1024(evict_shared_d, shdw_h, lambda cc, a, b: BK[:, cc, a:b])
    P.wait(sy, "cT")
    P.inc(sy.dma_start(out=esel, in_=esel_h), "sIN", 16)
    P.wait(te, "sIN")
    if debug_stage == "shared":
        return finish()

    for e in range(8):
        P.wait(te, "cV")
        for ch in range(4):
            P.inc(te.matmul(psB[:, ch * 512:(ch + 1) * 512], esel[:, e * 128:(e + 1) * 128],
                            normC[:, ch * 512:(ch + 1) * 512],
                            start=True, stop=True), "cT")
        P.wait(sc, "cT")
        P.inc(sc.copy(out=bcMoE, in_=psB[:, 0:T]), "cS")
        P.wait(ve, "cS")
        P.wait(te, "cS")

        def evict_up(ft, psum, e=e):
            P.wait(sc, "cT")
            P.inc(sc.activation(out=sStage[:, 0:T], in_=psum[:, 0:T], func=AF.Silu,
                                bias=upb[:, ft, e:e + 1]), "cS")
            P.wait(ve, "cS")
            c = P.inc(ve.tensor_tensor(out=BQ[:, ft, :], in0=sStage[:, 0:T], in1=bcMoE,
                                       op=ALU.mult), "cV")
            return ("cV", c)
        mm_1024(evict_up, upw_h[e], lambda cc, a, b: xn(cc, a, b))

        def evict_down(ft, psum, e=e):
            P.wait(ve, "cT")
            return P.inc(ve.tensor_tensor(out=XT[:, ft, :], in0=XT[:, ft, :],
                                          in1=psum[:, 0:T], op=ALU.add), "cV")

        pendd = {}
        for fo in range(0, NCT, 2):
            slot, bsl = load_w(wcols(dww_h[e], fo * 128, 256),
                               dwb_h[:, fo * 128:(fo + 2) * 128] if e == 0 else None)
            last = None
            for sub in range(2):
                ft = fo + sub
                psum = (psA, psB)[ft % 2]
                if ft >= 2:
                    te.wait_ge(P.sems["cV"], pendd[ft - 2])
                for ch in range(4):
                    ps = psum[:, ch * 512:(ch + 1) * 512]
                    for cc in range(NCT):
                        last = te.matmul(ps, slot[:, cc, sub * 128:(sub + 1) * 128],
                                         BQ[:, cc, ch * 512:(ch + 1) * 512],
                                         start=(cc == 0), stop=(e != 0 and cc == NCT - 1))
                        P.inc(last, "cT")
                    if e == 0:
                        last = te.matmul(ps, bsl[:, sub * 128:(sub + 1) * 128],
                                         normC[:, ch * 512:(ch + 1) * 512],
                                         start=False, stop=True)
                        P.inc(last, "cT")
                pendd[ft] = evict_down(ft, psum)
            slot_done(last)
        te.wait_ge(P.sems["cV"], pendd[NCT - 1])
    if debug_stage == "moe":
        return finish()
    rs_free("normC")

    def evict_qT(dst):
        def f(ft, psum):
            P.wait(ve, "cT")
            c = P.inc(ve.tensor_copy(out=dst[:, ft, :], in_=psum[:, 0:T]), "cV")
            return ("cV", c)
        return f
    mm_1024(evict_qT(BQ), wqT_h, lambda cc, a, b: xn(cc, a, b))
    mm_1024(evict_qT(BK), wkT_h, lambda cc, a, b: xn(cc, a, b))

    VR2 = rs_alloc("VR2", [128, 16, 1040], BF16)
    vst["V1"] = VR2
    vrf = VR2.rearrange("p a b -> p (a b)")
    cosv = vrf[:, 0:4 * T].rearrange("p (a t) -> p a t", t=T)
    sinv = vrf[:, 4 * T:8 * T].rearrange("p (a t) -> p a t", t=T)
    P.wait(gp, "cV")
    P.wait(gp, "cT")
    P.inc(gp.dma_start(out=cosv[:], in_=cosT_h), "sW", 16)
    P.inc(gp.dma_start(out=sinv[:], in_=sinT_h), "sW", 16)
    P.wait(ve, "sW")
    for buf in (BQ, BK):
        for i in range(4):
            qe, qo = buf[:, i, :], buf[:, i + 4, :]
            cs, sn = cosv[:, i, :], sinv[:, i, :]
            P.inc(ve.tensor_tensor(out=sStage[:, 0:T], in0=qo, in1=sn, op=ALU.mult), "cV")
            P.inc(ve.tensor_tensor(out=psA[:, 0:T], in0=qe, in1=cs, op=ALU.mult), "cV")
            P.inc(ve.tensor_tensor(out=psB[:, 0:T], in0=psA[:, 0:T], in1=sStage[:, 0:T],
                                   op=ALU.add), "cV")
            P.inc(ve.tensor_tensor(out=sStage[:, 0:T], in0=qe, in1=sn, op=ALU.mult), "cV")
            P.inc(ve.tensor_tensor(out=psA[:, 0:T], in0=qo, in1=cs, op=ALU.mult), "cV")
            P.inc(ve.tensor_tensor(out=qo, in0=psA[:, 0:T], in1=sStage[:, 0:T],
                                   op=ALU.subtract), "cV")
            P.inc(ve.tensor_copy(out=qe, in_=psB[:, 0:T]), "cV")
    P.wait(te, "cV")
    P.wait(gp, "cV")

    V1 = VR2
    for ci in range(4):
        slot, _b = load_w(wcols(wvT_h, ci * 256, 256))
        last = None
        pend = {}
        for tt in range(NT):
            ps = psA[:, (tt % 4) * 512:(tt % 4) * 512 + 256]
            if tt >= 4:
                te.wait_ge(P.sems["cV"], pend[tt - 4])
            for cc in range(NCT):
                last = te.matmul(ps, xn(cc, tt * 128, (tt + 1) * 128),
                                 slot[:, cc, 0:256], start=(cc == 0), stop=(cc == NCT - 1))
                P.inc(last, "cT")
            P.wait(ve, "cT")
            pend[tt] = P.inc(ve.tensor_copy(out=V1[:, tt, ci * 256:(ci + 1) * 256],
                                            in_=ps), "cV")
        te.wait_ge(P.sems["cV"], pend[NT - 1])
        slot_done(last)
    if debug_stage == "qkv1":
        return finish()

    P.wait(gp, "cT")
    P.wait(gp, "cV")
    P.wait(sy, "cT")
    BVflat = BV.rearrange("p a b -> p (a b)")
    m0 = BVflat[:, 7680:8576]
    P.inc(sy.dma_start(out=m0[:], in_=m0_h), "sIN", 16)
    P.wait(ve, "sIN")

    psO = psB.rearrange("p (a b) -> p a b", b=512)

    BVf32 = BVflat.bitcast(F32)
    a1inv32 = BVf32[:, 3328:3840]
    a1tmp = BVflat[:, 8576:9088]

    def expslot(kt):
        if kt < 4:
            return sStage[:, kt * 512:(kt + 1) * 512]
        return BVflat[:, (kt - 4) * 512:(kt - 3) * 512]

    for qc in range(4):
        q0 = qc * 512
        nk = 4 * qc + 4
        exp_gd = {}
        for kg in range(0, nk, 2):
            g = kg // 2
            gb = (g % 2) * 1024
            kn = min(2, nk - kg)
            if g >= 2:
                te.wait_ge(P.sems["cS"], exp_gd[g - 2])
            for ki in range(kn):
                kt = kg + ki
                for cc in range(NCT):
                    P.inc(te.matmul(psA[:, gb + ki * 512:gb + (ki + 1) * 512],
                                    BK[:, cc, kt * 128:(kt + 1) * 128],
                                    BQ[:, cc, q0:q0 + 512],
                                    start=(cc == 0), stop=(cc == NCT - 1)), "cT")
            P.wait(sc, "cT")
            for ki in range(kn):
                exp_gd[g] = P.inc(
                    sc.activation(out=expslot(kg + ki), in_=psA[:, gb + ki * 512:gb + (ki + 1) * 512],
                                  func=AF.Exp, bias=fconsts[:, 1:2], scale=0.03125), "cS")
            for ki in range(kn):
                kt = kg + ki
                r = kt - 4 * qc
                if 0 <= r <= 3:
                    ve.wait_ge(P.sems["cS"], exp_gd[g])
                    P.inc(ve.tensor_tensor(out=expslot(kt), in0=expslot(kt),
                                           in1=m0[:, 384 - 128 * r:384 - 128 * r + 512],
                                           op=ALU.mult), "cV")
        P.wait(te, "cV")
        P.wait(te, "cS")
        for kt in range(nk):
            for dt in range(4):
                P.inc(te.matmul(psO[:, dt, :], V1[:, kt, dt * 128:(dt + 1) * 128],
                                expslot(kt), start=(kt == 0), stop=(kt == nk - 1)), "cT")
            P.inc(te.matmul(psA[:, 1024:1536], onesA, expslot(kt),
                            start=(kt == 0), stop=(kt == nk - 1)), "cT")
        P.wait(ve, "cT")
        P.inc(ve.reciprocal(out=a1inv32, in_=psA[:, 1024:1536]), "cV")
        for dt in range(4):
            P.inc(ve.tensor_tensor(out=a1tmp, in0=psO[:, dt, :],
                                   in1=a1inv32, op=ALU.mult), "cV")
            P.inc(ve.tensor_tensor(out=XT[:, dt, q0:q0 + 512], in0=XT[:, dt, q0:q0 + 512],
                                   in1=a1tmp, op=ALU.add), "cV")
        P.wait(te, "cV")
        if debug_stage == "attn1q0" and qc == 0:
            return finish()
        for kt in range(nk):
            for dt in range(4):
                P.inc(te.matmul(psO[:, dt, :], V1[:, kt, (dt + 4) * 128:(dt + 5) * 128],
                                expslot(kt), start=(kt == 0), stop=(kt == nk - 1)), "cT")
        P.wait(ve, "cT")
        for dt in range(4):
            P.inc(ve.tensor_tensor(out=a1tmp, in0=psO[:, dt, :],
                                   in1=a1inv32, op=ALU.mult), "cV")
            P.inc(ve.tensor_tensor(out=XT[:, dt + 4, q0:q0 + 512],
                                   in0=XT[:, dt + 4, q0:q0 + 512],
                                   in1=a1tmp, op=ALU.add), "cV")
        P.wait(te, "cV")
        P.wait(sc, "cV")

    return finish()




_BUILD = {}


def _get_program(debug_stage=None):
    key = debug_stage
    if key not in _BUILD:
        _BUILD[key] = build_program(debug_stage)
    return _BUILD[key]


def kernel(**inputs):
    cst = _host_constants()
    H = _prep_inputs(inputs)
    nc = _get_program(None)
    x = np.asarray(inputs["L0_attn_x"], dtype=np.float32)
    base = {}
    base.update(H)
    base.update(cst)
    in_maps = []
    for c in range(8):
        m = dict(base)
        m["x"] = np.ascontiguousarray(x[c])
        in_maps.append(m)
    res = run_bass_kernel_spmd(nc, in_maps, core_ids=list(range(8)))
    out = np.stack([np.asarray(res.results[c]["out"]) for c in range(8)], axis=0)
    return out.astype(np.float32)
